# version 05
# baseline (speedup 1.0000x reference)
import time
from contextlib import ExitStack

import numpy as np
import ml_dtypes

import concourse.bass as bass
import concourse.mybir as mybir
from concourse.bass_utils import run_bass_kernel_spmd

F32 = mybir.dt.float32
BF16 = mybir.dt.bfloat16
U32 = mybir.dt.uint32
AF = mybir.ActivationFunctionType
ALU = mybir.AluOpType
AX = mybir.AxisListType.X

B, L, DM = 4, 4096, 1024
DI, DS, DTR = 2048, 16, 64
TOK = 2048
TC = 512
NTC = 4
NSTC = 8
NE = 16
HALF = 1024
EPS = 1e-5

DEBUG_TAPS = False


class Plan:
    def __init__(self):
        self.p = {k: [] for k in ("sync", "scalar", "vector", "tensor", "gpsimd")}
        self.cnt = {}

    def em(self, eng, fn):
        self.p[eng].append(fn)

    def bump(self, sem_name, v):
        self.cnt[sem_name] = self.cnt.get(sem_name, 0) + v
        return self.cnt[sem_name]

    def get(self, sem_name):
        return self.cnt.get(sem_name, 0)


def build_program():
    nc = bass.Bass("TRN2", target_bir_lowering=False)
    P = Plan()

    def inp(name, shape, dt=F32):
        return nc.dram_tensor(name, shape, dt, kind="ExternalInput").ap()

    xrows = inp("xrows", [TOK, DM])
    xbnd = inp("xbnd", [4, DM])
    mask1 = inp("mask1", [128, 1])
    parity = inp("parity", [1, 1], U32)
    nonces = inp("nonces", [1, 16], U32)
    w_in = inp("w_in", [DM, 2 * DI])
    convw_cols = inp("convw_cols", [128, 64])
    convb_cols = inp("convb_cols", [128, 16])
    w_xproj = inp("w_xproj", [DI, 96])
    w_dt_my = inp("w_dt_my", [DTR, 1024])
    bdt_cols = inp("bdt_cols", [128, 8])
    alog_cols = inp("alog_cols", [128, 128])
    d_cols = inp("d_cols", [128, 8])
    w_out = inp("w_out", [DI, DM])
    lng_cols = inp("lng_cols", [128, 8])
    lnb_cols = inp("lnb_cols", [128, 8])
    router_w = inp("router_w", [DM, NE])
    rb_col = inp("rb_col", [NE, 1])
    w1 = inp("w1", [NE, DM, DI])
    b1_cols = inp("b1_cols", [128, NE * 16])
    w2 = inp("w2", [NE, DI, DM])
    b2_st = inp("b2_st", [NE, DM])
    identb = inp("identb", [128, 128], BF16)
    fconst = inp("fconst", [128, 4])
    identf = inp("identf", [128, 128])

    yout = nc.dram_tensor("yout", [TOK, DM], F32, kind="ExternalOutput").ap()
    taps = {}
    if DEBUG_TAPS:
        for nm, shp in [("tap_u", [DI, TC]), ("tap_xdbl", [96, TC]),
                        ("tap_dt", [128, TC]),
                        ("tap_c", [NE, TOK]), ("tap_g", [DM, HALF]),
                        ("tap_xn", [DM, TOK]), ("tap_st2", [2, TOK]),
                        ("tap_hmid", [DI, HALF]), ("tap_crep", [128, HALF]),
                        ("tap_z", [128, TC]), ("tap_yg", [128, TC]),
                        ("tap_yfull", [128, TC]),
                        ("tap_h", [DM, TC]), ("tap_lg", [NE, TC]),
                        ("tap_stats", [128, 128]),
                        ("tap_mo", [DM, TC]), ("tap_y", [128, TC])]:
            taps[nm] = nc.dram_tensor(nm, shp, F32, kind="ExternalOutput").ap()

    xn_dram = nc.dram_tensor("xn_dram", [DM, TOK], BF16).ap()
    c_dram = nc.dram_tensor("c_dram", [NE, TOK], F32).ap()
    st2_dram = nc.dram_tensor("st2_dram", [2, TOK], F32).ap()
    sh_u = nc.dram_tensor("sh_u", [2, DI, TOK], F32, addr_space="Shared").ap()
    sh_xdbl = nc.dram_tensor("sh_xdbl", [2, 96, TOK], F32, addr_space="Shared").ap()
    sh_y = nc.dram_tensor("sh_y", [2, 1024, L], F32, addr_space="Shared").ap()
    sh_fu = nc.dram_tensor("sh_fu", [2, 4], U32, addr_space="Shared").ap()
    sh_fy = nc.dram_tensor("sh_fy", [2, 8], U32, addr_space="Shared").ap()

    with ExitStack() as ctx:
        e = ctx.enter_context

        def sb(name, shape, dt=F32):
            return e(nc.sbuf_tensor(name, shape, dt))

        pid_sb = sb("pid_sb", [1, 1], U32)
        non_sb = sb("non_sb", [1, 16], U32)
        mask_sb = sb("mask_sb", [128, 1])
        fc_sb = sb("fc_sb", [128, 4])
        idb_sb = sb("idb_sb", [128, 128], BF16)
        idf_sb = sb("idf_sb", [128, 128])
        wxp_sb = sb("wxp_sb", [128, 2 * 96])
        wdt_sb = sb("wdt_sb", [DTR, 2 * 128])
        cw_sb = sb("cw_sb", [128, 64])
        cb_sb = sb("cb_sb", [128, 16])
        bdt_sb = sb("bdt_sb", [128, 8])
        asb = sb("asb", [128, 128])
        dco_sb = sb("dco_sb", [128, 8])
        lng_sb = sb("lng_sb", [128, 8])
        lnb_sb = sb("lnb_sb", [128, 8])
        rw_sb = sb("rw_sb", [128, 8 * 16])
        rwhi = sb("rwhi", [128, 8 * 16], BF16)
        rwlo = sb("rwlo", [128, 8 * 16], BF16)
        rb_sb = sb("rb_sb", [NE, 1])
        b1_sb = sb("b1_sb", [128, NE * 16])
        b2_sb = sb("b2_sb", [NE, DM])
        stats = sb("stats", [128, 16 * 8])
        carry = sb("carry", [128, 128])
        uprev3 = sb("uprev3", [128, 48])
        cfm_sb = sb("cfm_sb", [NE, TOK])
        flagb = sb("flagb", [1, 2], U32)

        arenaA = sb("arenaA", [128, 11392])
        arA = arenaA[:, :]

        def fview(a, b):
            return arA[:, a:b]

        def bview(a, b):
            return arA[:, a:b].bitcast(BF16)

        x_tm1 = fview(0, 1024)
        xn_tm = fview(1024, 2048)
        xnb_tm = bview(2048, 2560)
        xnl_tm = bview(2560, 3072)
        xnl_fm_tc = bview(3072, 5120)
        uraw = fview(5120, 6150)
        uconv = fview(6150, 7174)
        u_dt = fview(7174, 8710)
        xdbl_tc = fview(8710, 9222)[0:96, :]
        logit_tc = fview(9222, 9734)[0:16, :]
        xb_tm = fview(9734, 10758)[0:4, :]
        hb_stage = bview(10758, 11270)[0:4, :]
        xnh = bview(0, 4096)
        ygate = bview(4096, 8192)

        scr = sb("scr", [128, TC])
        bigA = sb("bigA", [128, 8192])
        xn_fm_tc = bigA[:, 0:2048].bitcast(BF16)
        h_fm_tc = bigA[:, 2048:4096].bitcast(BF16)
        wcA_b = bigA[:, 4096:6144].bitcast(BF16)
        wcB_b = bigA[:, 6144:8192].bitcast(BF16)
        scr2 = bigA[:, 4096:5120]
        mu2rep = bigA[:, 6144:6656]
        rstd2rep = bigA[:, 6656:7168]
        pA_b = bigA[:, 0:4096].bitcast(BF16)
        pB_b = bigA[:, 4096:8192].bitcast(BF16)
        hb_fm = sb("hb_fm", [128, 8 * 4], BF16)
        ctr_st = sb("ctr_st", [NE, 128])
        ltm = sb("ltm", [128, NE])
        mx8 = sb("mx8", [128, 8])
        expw = sb("expw", [128, NE])
        maskw = sb("maskw", [128, NE])
        ctm = sb("ctm", [128, NE])
        tcol = sb("tcol", [128, 4])

        u_scan = sb("u_scan", [128, 2 * TC])
        xdbl_sc = sb("xdbl_sc", [96, TC])
        brep = sb("brep", [128, 2 * TC])
        crep = sb("crep", [128, 2 * TC])
        dt_sb = sb("dt_sb", [128, TC])
        sp_sb = sb("sp_sb", [128, TC])
        dtu_sb = sb("dtu_sb", [128, TC])
        dA_sb = sb("dA_sb", [128, 2 * TC])
        dBu_sb = sb("dBu_sb", [128, 2 * TC])
        H_sb = sb("H_sb", [128, 2 * TC])
        yacc_sb = sb("yacc_sb", [128, TC])
        yfin_sb = sb("yfin_sb", [128, 2 * TC])

        hmid_t = sb("hmid_t", [128, 16 * HALF], BF16)
        hmid = hmid_t[:, :]
        hm_f32 = hmid_t[:, :].bitcast(F32)
        mo_fm = hm_f32[:, 0:4096]
        zsil = hm_f32[:, 4096:5120]
        yfull = hm_f32[:, 5120:6144]
        mo_tm = hm_f32[:, 6144:7168]
        g_tm = hm_f32[:, 7168:8192]
        gacc = sb("gacc", [128, 8 * HALF])
        crep_e = sb("crep_e", [128, 2 * HALF], BF16)
        cmoe = crep_e[:, :].bitcast(F32)[0:16, 0:HALF]

        x2_tm = fview(0, 1024)
        s_tm = fview(1024, 2048)

        def ps(name, shape, dt=F32):
            return e(nc.psum_tensor(name, shape, dt))

        ps_m1 = [ps("ps_m1a", [128, 512]), ps("ps_m1b", [128, 512])]
        ps_m2 = [ps("ps_m2a", [128, 512]), ps("ps_m2b", [128, 512])]
        ps_mm = [ps("ps_mma", [128, 512]), ps("ps_mmb", [128, 512])]
        ps_x = ps("ps_x", [96, 512])
        ps_misc = ps("ps_misc", [128, 512])
        ps_t = ps_misc[:, 0:144]
        ps_tb = ps_misc[:, 144:208].bitcast(BF16)
        ps_bnd = ps_misc[:, 208:212]

        sems = {}
        for s_ in ("s_sy", "s_gp", "s_sc", "s_vq", "s_pe", "s_v", "s_s", "s_pl"):
            sems[s_] = e(nc.semaphore(s_))

        def dma(q, out_ap, in_ap, **kw):
            sem = {"sync": "s_sy", "gpsimd": "s_gp",
                   "scalar": "s_sc", "vector": "s_vq"}[q]
            tgt = P.bump(sem, 16)
            s_h = sems[sem]
            P.em(q, lambda g, o=out_ap, i=in_ap, s=s_h, kw=dict(kw):
                 g.dma_start(out=o, in_=i, **kw).then_inc(s, 16))
            return tgt

        def wait(eng, sem, tgt):
            if tgt is None or tgt <= 0:
                return
            s_h = sems[sem]
            P.em(eng, lambda g, s=s_h, t=tgt: g.wait_ge(s, t))

        def vop(fn, mark=False):
            if not mark:
                P.em("vector", fn)
                return None
            tgt = P.bump("s_v", 1)
            P.em("vector", lambda g, f=fn, s=sems["s_v"]: f(g).then_inc(s, 1))
            return tgt

        def sop(fn, mark=False):
            if not mark:
                P.em("scalar", fn)
                return None
            tgt = P.bump("s_s", 1)
            P.em("scalar", lambda g, f=fn, s=sems["s_s"]: f(g).then_inc(s, 1))
            return tgt

        def top(fn, mark=False):
            if not mark:
                P.em("tensor", fn)
                return None
            tgt = P.bump("s_pe", 1)
            P.em("tensor", lambda g, f=fn, s=sems["s_pe"]: f(g).then_inc(s, 1))
            return tgt

        def phase_guard(*engines):
            snap = {s_: P.get(s_) for s_ in ("s_pe", "s_v", "s_s")}
            own = {"tensor": "s_pe", "vector": "s_v", "scalar": "s_s"}
            for eng in engines:
                for s_, t in snap.items():
                    if own.get(eng) != s_:
                        wait(eng, s_, t)

        rt = {}

        def em_sync_prolog(t_pid):
            def f(g, t_pid=t_pid):
                rt["pollcnt"] = g.alloc_register("pollcnt")
                rt["preg"] = g.alloc_register("preg")
                rt["treg"] = g.alloc_register("treg")
                rt["freg"] = g.alloc_register("freg")
                rt["dreg"] = g.alloc_register("dreg")
                g.reg_mov(rt["pollcnt"], 0)
                g.wait_ge(sems["s_sy"], t_pid)
                g.load(rt["preg"], pid_sb[0:1, 0:1])
            P.em("sync", f)

        class _Arms:
            pass

        def em_arms(g, body):
            with g.If_eq(rt["preg"], 0):
                body(g, 0)
            with g.Else():
                body(g, 1)

        def _spin(g, flag_ap, val_idx):
            g.load(rt["treg"], non_sb[0:1, val_idx:val_idx + 1])

            def cond():
                g.reg_add(rt["pollcnt"], rt["pollcnt"], 16)
                g.dma_start(out=flagb[0:1, 0:1], in_=flag_ap).then_inc(
                    sems["s_pl"], 16)
                g.wait_ge(sems["s_pl"], rt["pollcnt"])
                g.load(rt["freg"], flagb[0:1, 0:1])
                g.reg_sub(rt["dreg"], rt["freg"], rt["treg"])
                return rt["dreg"]
            with g.While(cond):
                g.nop()

        def em_poll_simple(flag_ap, val_idx):
            P.em("sync", lambda g, f_=flag_ap, v_=val_idx: _spin(g, f_, v_))

        t_const = 0
        for out_ap, in_ap in [
            (pid_sb[:, :], parity), (non_sb[:, :], nonces),
            (mask_sb[:, :], mask1), (fc_sb[:, :], fconst),
            (idb_sb[:, :], identb),
            (idf_sb[:, :], identf), (cw_sb[:, :], convw_cols),
            (cb_sb[:, :], convb_cols), (bdt_sb[:, :], bdt_cols),
            (asb[:, :], alog_cols), (dco_sb[:, :], d_cols),
            (lng_sb[:, :], lng_cols), (lnb_sb[:, :], lnb_cols),
            (rb_sb[:, :], rb_col), (b1_sb[:, :], b1_cols),
            (b2_sb[:, :], b2_st),
            (xb_tm[0:3, :], xbnd[0:3, :]),
        ]:
            t_const = dma("sync", out_ap, in_ap)

        for kc in range(8):
            t_const = dma("sync", rw_sb[:, kc * 16:(kc + 1) * 16],
                          router_w[kc * 128:(kc + 1) * 128, :])
        em_sync_prolog(t_const)

        wait("scalar", "s_sy", t_const)
        t_s0 = sop(lambda g: g.activation(out=asb[:, :], in_=asb[:, :],
                                          func=AF.Exp), True)
        wait("vector", "s_s", t_s0)
        wait("vector", "s_sy", t_const)
        vop(lambda g: g.tensor_scalar(out=asb[:, :], in0=asb[:, :],
                                      scalar1=-1.0, scalar2=None, op0=ALU.mult))
        vop(lambda g: g.memset(carry[:, :], 0.0))
        vop(lambda g: g.tensor_copy(rwhi[:, :], rw_sb[:, :]))
        t_rw = vop(lambda g: g.tensor_tensor(out=rwlo[:, :], in0=rw_sb[:, :],
                                             in1=rwhi[:, :], op=ALU.subtract), True)

        C_SUM1, C_MU1, C_RS1, C_SS1, C_SUM2, C_SS2, C_MU2, C_RS2 = range(8)

        def col(ci, j):
            return stats[:, ci * 16 + j: ci * 16 + j + 1]

        xfm_done = [0] * NTC
        lg_guard = [0]
        t_cd = 0
        st2_done = [0] * NTC
        ctm_done = [0] * NTC
        xnd_done = [0] * NTC
        xtm_consumed = []

        for tc in range(NTC):
            for jj in range(4):
                j = tc * 4 + jj
                xt = x_tm1
                if j >= 1:
                    wait("sync", "s_v", xtm_consumed[j - 1])
                t_x = dma("sync", xt, xrows[j * 128:(j + 1) * 128, :])
                wait("vector", "s_sy", t_x)
                wait("scalar", "s_sy", t_x)
                t1 = vop(lambda g, xt=xt, j=j: g.reduce_sum(
                    out=col(C_SUM1, j), in_=xt, axis=AX), True)
                ts1 = sop(lambda g, xt=xt, j=j: g.activation(
                    out=scr2[:, :], in_=xt, func=AF.Square,
                    accum_out=col(C_SS1, j)), True)
                wait("vector", "s_v", t1)
                t2 = vop(lambda g, j=j: g.tensor_scalar(
                    out=col(C_MU1, j), in0=col(C_SUM1, j),
                    scalar1=1.0 / DM, scalar2=None, op0=ALU.mult), True)
                wait("vector", "s_s", ts1)
                wait("vector", "s_v", t2)
                vop(lambda g, j=j: g.tensor_tensor(
                    out=tcol[:, 0:1], in0=col(C_MU1, j), in1=col(C_MU1, j),
                    op=ALU.mult))
                t3 = vop(lambda g, j=j: g.tensor_scalar(
                    out=tcol[:, 1:2], in0=col(C_SS1, j),
                    scalar1=1.0 / DM, scalar2=None, op0=ALU.mult), True)
                wait("vector", "s_v", t3)
                t4 = vop(lambda g: g.tensor_tensor(
                    out=tcol[:, 1:2], in0=tcol[:, 1:2], in1=tcol[:, 0:1],
                    op=ALU.subtract), True)
                wait("scalar", "s_v", t4)
                ts2a = sop(lambda g: g.activation(
                    out=tcol[:, 2:3], in_=tcol[:, 1:2],
                    func=AF.Ln, bias=fc_sb[:, 0:1]), True)
                wait("scalar", "s_s", ts2a)
                ts2 = sop(lambda g, j=j: g.activation(
                    out=col(C_RS1, j), in_=tcol[:, 2:3], func=AF.Exp,
                    scale=fc_sb[:, 2:3]), True)
                wait("vector", "s_s", ts2)
                t_xn = vop(lambda g, xt=xt, j=j: g.tensor_scalar(
                    out=xn_tm[:, :], in0=xt, scalar1=col(C_MU1, j),
                    scalar2=col(C_RS1, j), op0=ALU.subtract, op1=ALU.mult), True)
                vop(lambda g: g.tensor_copy(xnb_tm[:, :], xn_tm[:, :]))
                t_lo = vop(lambda g: g.tensor_tensor(
                    out=xnl_tm[:, :], in0=xn_tm[:, :], in1=xnb_tm[:, :],
                    op=ALU.subtract), True)
                xtm_consumed.append(t_lo)
                r2 = vop(lambda g, j=j: g.reduce_sum(
                    out=col(C_SUM2, j), in_=xn_tm[:, :], axis=AX), True)
                wait("vector", "s_v", r2)
                m2 = vop(lambda g, j=j: g.tensor_scalar(
                    out=col(C_MU2, j), in0=col(C_SUM2, j),
                    scalar1=1.0 / DM, scalar2=None, op0=ALU.mult), True)
                wait("scalar", "s_v", t_xn)
                ss2 = sop(lambda g, j=j: g.activation(
                    out=scr2[:, :], in_=xn_tm[:, :], func=AF.Square,
                    accum_out=col(C_SS2, j)), True)
                wait("vector", "s_s", ss2)
                wait("vector", "s_v", m2)
                vop(lambda g, j=j: g.tensor_tensor(
                    out=tcol[:, 0:1], in0=col(C_MU2, j), in1=col(C_MU2, j),
                    op=ALU.mult))
                t3b = vop(lambda g, j=j: g.tensor_scalar(
                    out=tcol[:, 1:2], in0=col(C_SS2, j),
                    scalar1=1.0 / DM, scalar2=None, op0=ALU.mult), True)
                wait("vector", "s_v", t3b)
                t4b = vop(lambda g: g.tensor_tensor(
                    out=tcol[:, 1:2], in0=tcol[:, 1:2], in1=tcol[:, 0:1],
                    op=ALU.subtract), True)
                wait("scalar", "s_v", t4b)
                ts3a = sop(lambda g: g.activation(
                    out=tcol[:, 2:3], in_=tcol[:, 1:2],
                    func=AF.Ln, bias=fc_sb[:, 0:1]), True)
                wait("scalar", "s_s", ts3a)
                t_s2 = sop(lambda g, j=j: g.activation(
                    out=col(C_RS2, j), in_=tcol[:, 2:3], func=AF.Exp,
                    scale=fc_sb[:, 2:3]), True)
                wait("gpsimd", "s_s", t_s2)
                st2_done[tc] = dma(
                    "gpsimd",
                    st2_dram[0:1, j * 128:(j + 1) * 128].rearrange("a b -> b a"),
                    col(C_MU2, j))
                st2_done[tc] = dma(
                    "gpsimd",
                    st2_dram[1:2, j * 128:(j + 1) * 128].rearrange("a b -> b a"),
                    col(C_RS2, j))
                wait("tensor", "s_v", t_lo)
                t_ev = None
                for i in range(8):
                    t_pe = top(lambda g, i=i: g.transpose(
                        out=ps_tb, in_=xnb_tm[:, i * 128:(i + 1) * 128],
                        identity=idb_sb[:, :]), True)
                    wait("vector", "s_pe", t_pe)
                    t_ev = vop(lambda g, i=i, jj=jj: g.tensor_copy(
                        xn_fm_tc[:, i * TC + jj * 128: i * TC + (jj + 1) * 128],
                        ps_tb), True)
                    wait("tensor", "s_v", t_ev)
                    t_pe = top(lambda g, i=i: g.transpose(
                        out=ps_tb, in_=xnl_tm[:, i * 128:(i + 1) * 128],
                        identity=idb_sb[:, :]), True)
                    wait("vector", "s_pe", t_pe)
                    t_ev = vop(lambda g, i=i, jj=jj: g.tensor_copy(
                        xnl_fm_tc[:, i * TC + jj * 128: i * TC + (jj + 1) * 128],
                        ps_tb), True)
                    wait("tensor", "s_v", t_ev)
            xfm_done[tc] = t_ev
            wait("scalar", "s_v", t_ev)
            for i in range(8):
                xnd_done[tc] = dma("scalar",
                                   xn_dram[i * 128:(i + 1) * 128,
                                           tc * TC:(tc + 1) * TC],
                                   xn_fm_tc[:, i * TC:(i + 1) * TC])
            wait("tensor", "s_sy", t_const)
            wait("tensor", "s_v", max(xfm_done[tc], t_rw))
            nmm = 0
            t_r = None
            for term in range(3):
                for kc in range(8):
                    st_ap = (rwhi if term < 2 else rwlo)[:, kc * 16:(kc + 1) * 16]
                    mv = (xn_fm_tc if term != 1 else xnl_fm_tc)
                    mv_ap = mv[:, kc * TC:(kc + 1) * TC]
                    last = (nmm == 23)
                    t_r = top(lambda g, s_=st_ap, m_=mv_ap, f_=(nmm == 0), l_=last:
                              g.matmul(out=ps_x[0:16, :], lhsT=s_, rhs=m_,
                                       start=f_, stop=l_), last)
                    nmm += 1
            if DEBUG_TAPS and tc == 0:
                pass
            wait("scalar", "s_pe", t_r)
            if tc > 0:
                wait("scalar", "s_pe", lg_guard[0])
            t_lg = sop(lambda g: g.activation(
                out=logit_tc[:, :], in_=ps_x[0:16, :],
                func=AF.Identity, bias=rb_sb[:, :]), True)
            if DEBUG_TAPS and tc == 0:
                wait("gpsimd", "s_s", t_lg)
                dma("gpsimd", taps["tap_lg"], logit_tc[:, :])
            for jj in range(4):
                wait("tensor", "s_s", t_lg)
                t_pe = top(lambda g, jj=jj: g.transpose(
                    out=ps_t[:, 0:16],
                    in_=logit_tc[:, jj * 128:(jj + 1) * 128],
                    identity=idf_sb[0:16, 0:16]), True)
                lg_guard[0] = t_pe
                wait("vector", "s_pe", t_pe)
                tv0 = vop(lambda g: g.tensor_copy(ltm[:, :], ps_t[:, 0:16]), True)
                wait("vector", "s_v", tv0)
                tv1 = vop(lambda g: g.max(out=mx8[:, :], in_=ltm[:, :]), True)
                wait("vector", "s_v", tv1)
                vop(lambda g: g.tensor_scalar(
                    out=expw[:, :], in0=ltm[:, :], scalar1=mx8[:, 0:1],
                    scalar2=None, op0=ALU.subtract))
                t_v = vop(lambda g: g.tensor_scalar(
                    out=maskw[:, :], in0=ltm[:, :], scalar1=mx8[:, 3:4],
                    scalar2=None, op0=ALU.is_ge), True)
                wait("scalar", "s_v", t_v)
                t_s = sop(lambda g: g.activation(out=expw[:, :], in_=expw[:, :],
                                                 func=AF.Exp), True)
                wait("vector", "s_s", t_s)
                tv2 = vop(lambda g: g.tensor_tensor(
                    out=expw[:, :], in0=expw[:, :],
                    in1=maskw[:, :], op=ALU.mult), True)
                wait("vector", "s_v", tv2)
                tv3 = vop(lambda g: g.reduce_sum(out=tcol[:, 0:1], in_=expw[:, :],
                                                 axis=AX), True)
                wait("vector", "s_v", tv3)
                tv4 = vop(lambda g: g.reciprocal(out=tcol[:, 1:2],
                                                 in_=tcol[:, 0:1]), True)
                wait("vector", "s_v", tv4)
                t_v = vop(lambda g: g.tensor_scalar(
                    out=ctm[:, :], in0=expw[:, :], scalar1=tcol[:, 1:2],
                    scalar2=None, op0=ALU.mult), True)
                wait("tensor", "s_v", t_v)
                t_pe = top(lambda g: g.transpose(
                    out=ps_misc[0:16, 16:144], in_=ctm[:, :],
                    identity=idf_sb[:, :]), True)
                wait("vector", "s_pe", t_pe)
                if tc + jj > 0:
                    wait("vector", "s_sc", t_cd)
                ctm_done[tc] = vop(lambda g: g.tensor_copy(
                    ctr_st[:, :], ps_misc[0:16, 16:144]), True)
                wait("scalar", "s_v", ctm_done[tc])
                t_cd = dma("scalar",
                           c_dram[:, tc * TC + jj * 128: tc * TC + (jj + 1) * 128],
                           ctr_st[:, :])
        if DEBUG_TAPS:
            wait("gpsimd", "s_sc", t_cd)
            dma("gpsimd", taps["tap_c"], c_dram[:, :])
            wait("gpsimd", "s_sc", max(xnd_done))
            dma("gpsimd", taps["tap_xn"], xn_dram[:, :])
            dma("gpsimd", taps["tap_st2"], st2_dram[:, :])
            wait("gpsimd", "s_v", P.get("s_v"))
            dma("gpsimd", taps["tap_stats"], stats[:, :])

        def em_ln_rows(src_ap, n_):
            tv0 = vop(lambda g: g.reduce_sum(out=tcol[0:n_, 0:1], in_=src_ap,
                                             axis=AX), True)
            wait("vector", "s_v", tv0)
            t_v = vop(lambda g: g.tensor_scalar(
                out=tcol[0:n_, 0:1], in0=tcol[0:n_, 0:1], scalar1=1.0 / DM,
                scalar2=None, op0=ALU.mult), True)
            wait("scalar", "s_v", t_v)
            t_s = sop(lambda g: g.activation(
                out=scr2[0:n_, :], in_=src_ap, func=AF.Square,
                accum_out=tcol[0:n_, 1:2]), True)
            wait("vector", "s_s", t_s)
            wait("vector", "s_v", t_v)
            tv1 = vop(lambda g: g.tensor_tensor(
                out=tcol[0:n_, 2:3], in0=tcol[0:n_, 0:1],
                in1=tcol[0:n_, 0:1], op=ALU.mult), True)
            tv2 = vop(lambda g: g.tensor_scalar(
                out=tcol[0:n_, 1:2], in0=tcol[0:n_, 1:2],
                scalar1=1.0 / DM, scalar2=None, op0=ALU.mult), True)
            wait("vector", "s_v", tv2)
            t_v = vop(lambda g: g.tensor_tensor(
                out=tcol[0:n_, 1:2], in0=tcol[0:n_, 1:2], in1=tcol[0:n_, 2:3],
                op=ALU.subtract), True)
            wait("scalar", "s_v", t_v)
            ts1 = sop(lambda g: g.activation(
                out=tcol[0:n_, 2:3], in_=tcol[0:n_, 1:2],
                func=AF.Ln, bias=fc_sb[0:n_, 0:1]), True)
            wait("scalar", "s_s", ts1)
            t_s = sop(lambda g: g.activation(out=tcol[0:n_, 3:4],
                                             in_=tcol[0:n_, 2:3], func=AF.Exp,
                                             scale=fc_sb[0:n_, 2:3]), True)
            wait("vector", "s_s", t_s)
            t_v = vop(lambda g: g.tensor_scalar(
                out=src_ap, in0=src_ap, scalar1=tcol[0:n_, 0:1],
                scalar2=tcol[0:n_, 3:4], op0=ALU.subtract, op1=ALU.mult), True)
            wait("vector", "s_v", t_v)
            return t_v

        wait("vector", "s_sy", t_const)
        em_ln_rows(xb_tm[0:3, :], 3)
        em_ln_rows(xb_tm[0:3, :], 3)
        t_hb = vop(lambda g: g.tensor_copy(hb_stage[0:3, :], xb_tm[0:3, :]), True)
        wait("tensor", "s_v", t_hb)
        for i in range(8):
            t_pe = top(lambda g, i=i: g.transpose(
                out=ps_tb[:, 0:3], in_=hb_stage[0:3, i * 128:(i + 1) * 128],
                identity=idb_sb[0:3, 0:3]), True)
            wait("scalar", "s_pe", t_pe)
            t_hb2 = sop(lambda g, i=i: g.activation(
                out=hb_fm[:, i * 4:i * 4 + 3], in_=ps_tb[:, 0:3],
                func=AF.Identity, scale=lng_sb[:, i:i + 1],
                bias=lnb_sb[:, i:i + 1]), True)
            wait("tensor", "s_s", t_hb2)

        def em_h_fm(tc):
            phase_guard("vector", "scalar")
            wait("gpsimd", "s_v", P.get("s_v"))
            t_m = dma("gpsimd", mu2rep[:, :],
                      st2_dram[0:1, tc * TC:(tc + 1) * TC].partition_broadcast(128))
            t_r2 = dma("gpsimd", rstd2rep[:, :],
                       st2_dram[1:2, tc * TC:(tc + 1) * TC].partition_broadcast(128))
            wait("vector", "s_gp", t_r2)
            wait("vector", "s_sc", st2_done[tc])
            ts = None
            for i in range(8):
                if ts is not None:
                    wait("vector", "s_s", ts)
                tv0 = vop(lambda g, i=i: g.tensor_tensor(
                    out=scr[:, 0:TC], in0=xn_fm_tc[:, i * TC:(i + 1) * TC],
                    in1=mu2rep[:, :], op=ALU.subtract), True)
                wait("vector", "s_v", tv0)
                tv = vop(lambda g: g.tensor_tensor(
                    out=scr[:, 0:TC], in0=scr[:, 0:TC], in1=rstd2rep[:, :],
                    op=ALU.mult), True)
                wait("scalar", "s_v", tv)
                ts = sop(lambda g, i=i: g.activation(
                    out=h_fm_tc[:, i * TC:(i + 1) * TC], in_=scr[:, 0:TC],
                    func=AF.Identity, scale=lng_sb[:, i:i + 1],
                    bias=lnb_sb[:, i:i + 1]), True)
            return ts

        uflag_t = [0] * NTC
        p1_state = {}
        xp_mm_t = {}
        ud_t = {}
        ex_t = {}
        winu_grp_t = {}

        for tc in range(NTC):
            wait("gpsimd", "s_v", P.get("s_v"))
            wait("gpsimd", "s_pe", P.get("s_pe"))
            wait("gpsimd", "s_s", P.get("s_s"))
            wait("gpsimd", "s_sc", xnd_done[tc])
            t_xr = dma("gpsimd",
                       xn_fm_tc[:, :].rearrange("p (a n) -> p a n", a=8),
                       xn_dram[:, tc * TC:(tc + 1) * TC]
                       .rearrange("(a p) n -> p a n", p=128))
            wait("vector", "s_gp", t_xr)
            t_h = em_h_fm(tc)
            if DEBUG_TAPS and tc == 0:
                wait("gpsimd", "s_s", t_h)
                dma("gpsimd", taps["tap_h"].rearrange("(a p) n -> p a n", p=128),
                    h_fm_tc[:, :].rearrange("p (a n) -> p a n", a=8))
            wait("tensor", "s_s", t_h)
            phase_guard("tensor")
            conv_tv = {}

            def ld_win(c, tc=tc, extra_s=None):
                buf = wcA_b if c % 2 == 0 else wcB_b
                if extra_s is not None:
                    wait("gpsimd", "s_s", extra_s)
                if c >= 2:
                    wait("gpsimd", "s_pe", winu_grp_t[(tc, c * 4 - 5)])
                return dma("gpsimd",
                           buf.rearrange("p (a n) -> p a n", a=8),
                           w_in[:, c * 512:(c + 1) * 512]
                           .rearrange("(a p) n -> p a n", p=128))

            t_wc = {0: ld_win(0), 1: ld_win(1, extra_s=t_h)}
            for kt in range(16):
                slot_u = kt % 3
                if kt >= 3:
                    wait("scalar", "s_pe", xp_mm_t[(tc, kt - 3)])
                elif tc > 0:
                    wait("scalar", "s_pe", xp_mm_t[(tc - 1, kt + 13)])
                pswar = conv_tv.get(kt - 1 if tc == 0 else kt - 2)
                if pswar is None and kt == 0 and tc > 0:
                    pswar = p1_state.get("conv_last")
                if pswar is not None:
                    wait("tensor", "s_v", pswar)
                cw_ = kt // 4
                if kt == 4:
                    t_wc[2] = ld_win(2)
                if kt == 8:
                    t_wc[3] = ld_win(3)
                wbuf_p1 = wcA_b if cw_ % 2 == 0 else wcB_b
                if kt % 4 == 0:
                    wait("tensor", "s_gp", t_wc[cw_])

                def lw(k, kt=kt, wbuf_p1=wbuf_p1):
                    o = k * 512 + (kt % 4) * 128
                    return wbuf_p1[:, o:o + 128]
                for k in range(8):
                    t_pe = top(lambda g, k=k, kt=kt, lw=lw: g.matmul(
                        out=ps_mm[kt % 2][:, :],
                        lhsT=lw(k),
                        rhs=h_fm_tc[:, k * TC:(k + 1) * TC],
                        start=(k == 0), stop=(k == 7)), k == 7)
                if tc == 0:
                    for k in range(8):
                        t_pe = top(lambda g, k=k, lw=lw: g.matmul(
                            out=ps_bnd[:, 0:3],
                            lhsT=lw(k),
                            rhs=hb_fm[:, k * 4:k * 4 + 3],
                            start=(k == 0), stop=(k == 7)), k == 7)
                winu_grp_t[(tc, kt)] = t_pe
                wait("vector", "s_pe", t_pe)
                ur = uraw[:, (kt % 2) * 515:(kt % 2) * 515 + 515]
                if tc == 0:
                    t_bm = vop(lambda g, kt=kt: g.tensor_scalar(
                        out=uprev3[:, kt * 3:(kt + 1) * 3], in0=ps_bnd[:, 0:3],
                        scalar1=mask_sb[:, :], scalar2=None, op0=ALU.mult), True)
                    wait("vector", "s_v", t_bm)
                vop(lambda g, ur=ur, kt=kt: g.tensor_copy(
                    ur[:, 0:3], uprev3[:, kt * 3:(kt + 1) * 3]))
                t_ur = vop(lambda g, ur=ur, kt=kt: g.tensor_copy(
                    ur[:, 3:515], ps_mm[kt % 2][:, :]), True)
                wait("vector", "s_v", t_ur)
                t_up = vop(lambda g, ur=ur, kt=kt: g.tensor_copy(
                    uprev3[:, kt * 3:(kt + 1) * 3], ur[:, 512:515]), True)
                wait("vector", "s_v", t_up)
                ucwar = ud_t.get((tc, kt - 2)) if kt >= 2 else \
                    (ud_t.get((tc - 1, kt + 14)) if tc > 0 else None)
                if ucwar is not None:
                    wait("vector", "s_s", ucwar)
                uc = uconv[:, (kt % 2) * TC:(kt % 2 + 1) * TC]
                tv = vop(lambda g, ur=ur, uc=uc, kt=kt: g.tensor_scalar(
                    out=uc, in0=ur[:, 0:512], scalar1=cw_sb[:, kt * 4:kt * 4 + 1],
                    scalar2=None, op0=ALU.mult), True)
                for tap in range(1, 4):
                    wait("vector", "s_v", tv)
                    tv = vop(lambda g, ur=ur, uc=uc, kt=kt, tap=tap:
                             g.scalar_tensor_tensor(
                                 out=uc, in0=ur[:, tap:tap + 512],
                                 scalar=cw_sb[:, kt * 4 + tap:kt * 4 + tap + 1],
                                 in1=uc, op0=ALU.mult, op1=ALU.add),
                             True)
                conv_tv[kt] = tv
                p1_state["conv_last"] = tv
                wait("scalar", "s_v", tv)
                exwar = ex_t.get((tc, kt - 3)) if kt >= 3 else \
                    (ex_t.get((tc - 1, kt + 13)) if tc > 0 else None)
                if exwar is not None:
                    wait("scalar", "s_sy", exwar)
                ud = u_dt[:, slot_u * TC:(slot_u + 1) * TC]
                t_ud = sop(lambda g, uc=uc, ud=ud, kt=kt: g.activation(
                    out=ud, in_=uc, func=AF.Silu,
                    bias=cb_sb[:, kt:kt + 1]), True)
                ud_t[(tc, kt)] = t_ud
                if kt >= 2 or tc > 0:
                    pg = xp_mm_t[(tc, kt - 2)] if kt >= 2 else \
                        xp_mm_t[(tc - 1, kt + 14)]
                    wait("gpsimd", "s_pe", pg)
                t_wx = dma("gpsimd", wxp_sb[:, (kt % 2) * 96:(kt % 2 + 1) * 96],
                           w_xproj[kt * 128:(kt + 1) * 128, :])
                wait("tensor", "s_s", t_ud)
                wait("tensor", "s_gp", t_wx)
                xp_mm_t[(tc, kt)] = top(lambda g, kt=kt, ud=ud: g.matmul(
                    out=ps_x[:, :], lhsT=wxp_sb[:, (kt % 2) * 96:(kt % 2 + 1) * 96],
                    rhs=ud, start=(kt == 0), stop=(kt == 15)), True)
                base_sy = P.bump("s_sy", 16)
                ex_t[(tc, kt)] = base_sy

                def f_uexch(g, tc=tc, kt=kt, ud=ud, t_ud=t_ud):
                    g.wait_ge(sems["s_s"], t_ud)
                    em_arms(g, lambda g2, p, tc=tc, kt=kt, ud=ud: g2.dma_start(
                        out=sh_u[p, kt * 128:(kt + 1) * 128,
                                 tc * TC:(tc + 1) * TC],
                        in_=ud).then_inc(sems["s_sy"], 16))
                P.em("sync", f_uexch)
                if DEBUG_TAPS and tc == 0:
                    wait("scalar", "s_s", t_ud)
                    tdma = dma("scalar", taps["tap_u"][kt * 128:(kt + 1) * 128, :], ud)
                    wait("scalar", "s_sc", tdma)
            wait("vector", "s_pe", xp_mm_t[(tc, 15)])
            t_xd = vop(lambda g: g.tensor_copy(xdbl_tc[:, :], ps_x[:, :]), True)
            if DEBUG_TAPS and tc == 0:
                wait("scalar", "s_v", t_xd)
                tdma = dma("scalar", taps["tap_xdbl"], xdbl_tc[:, :])
                wait("vector", "s_sc", tdma)
            t_f1 = P.bump("s_sy", 16)
            t_f2 = P.bump("s_sy", 16)

            def f_xexch(g, tc=tc, t_xd=t_xd, t_f1=t_f1):
                g.wait_ge(sems["s_v"], t_xd)
                em_arms(g, lambda g2, p, tc=tc: g2.dma_start(
                    out=sh_xdbl[p, :, tc * TC:(tc + 1) * TC],
                    in_=xdbl_tc[:, :]).then_inc(sems["s_sy"], 16))
                g.wait_ge(sems["s_sy"], t_f1)
                em_arms(g, lambda g2, p, tc=tc: g2.dma_start(
                    out=sh_fu[p:p + 1, tc:tc + 1],
                    in_=non_sb[0:1, tc:tc + 1]).then_inc(sems["s_sy"], 16))
            P.em("sync", f_xexch)
            uflag_t[tc] = t_f2

        gacc_ready = [0, 0]
        xnh_loaded = [0, 0]
        moe_state = {"crep_guard": {}, "hmid_guard": None, "w2_last": None,
                     "w1_grp": {}, "w2_grp": {}}

        def em_xnh_load(half):
            wait("gpsimd", "s_pe", P.get("s_pe"))
            wait("gpsimd", "s_v", P.get("s_v"))
            wait("gpsimd", "s_s", P.get("s_s"))
            wait("gpsimd", "s_sy", P.get("s_sy"))
            t = dma("gpsimd",
                    xnh[:, :].rearrange("p (a n) -> p a n", a=8),
                    xn_dram[:, half * HALF:(half + 1) * HALF]
                    .rearrange("(a p) n -> p a n", p=128))
            xnh_loaded[half] = t

        def em_moe_ehalf(half, ee, first_in_half):
            phase_guard("tensor", "vector", "scalar")
            slot_c = ee % 2
            g_ = moe_state["crep_guard"].get(slot_c)
            if g_ is not None:
                wait("gpsimd", "s_v", g_)
            wait("gpsimd", "s_pe", P.get("s_pe") if first_in_half else 0)
            if first_in_half:
                wait("gpsimd", "s_v", P.get("s_v"))
                wait("gpsimd", "s_s", P.get("s_s"))
            wait("gpsimd", "s_sc", t_cd)
            t_c = dma("gpsimd", crep_e[:, slot_c * HALF:(slot_c + 1) * HALF],
                      c_dram[ee:ee + 1, half * HALF:(half + 1) * HALF]
                      .partition_broadcast(128))
            if first_in_half:
                wait("vector", "s_pe", P.get("s_pe"))
                vop(lambda g: g.memset(gacc[:, :], 0.0))
            if moe_state["w2_grp"].get(7) is not None:
                wait("scalar", "s_pe", moe_state["w2_grp"][7])

            def ld_w1(cc, ee=ee, gk=None):
                buf = pA_b if cc == 0 else pB_b
                if gk is not None:
                    wait("gpsimd", "s_pe", gk)
                return dma("gpsimd",
                           buf.rearrange("p (a n) -> p a n", a=8),
                           w1[ee, :, cc * 1024:(cc + 1) * 1024]
                           .rearrange("(a p) n -> p a n", p=128))

            def ld_w2(cc, ee=ee, gk=None):
                buf = pA_b if cc == 0 else pB_b
                if gk is not None:
                    wait("gpsimd", "s_pe", gk)
                t = 0
                for h2 in range(2):
                    t = dma("gpsimd",
                            buf[:, h2 * 4096:(h2 + 1) * 4096]
                            .rearrange("p (a n) -> p a n", a=8),
                            w2[ee, h2 * 1024:(h2 + 1) * 1024,
                               cc * 512:(cc + 1) * 512]
                            .rearrange("(a p) n -> p a n", p=128))
                return t

            t_w1a = ld_w1(0, gk=moe_state["w2_grp"].get(3))
            t_w1b = ld_w1(1, gk=moe_state["w2_grp"].get(7))
            t_w2a = None
            t_s = None
            t_sc = None
            for it in range(16):
                w1war = t_s if t_s is not None else moe_state.get("w1_s_last")
                if w1war is not None:
                    wait("tensor", "s_s", w1war)
                if it == 8:
                    t_w2a = ld_w2(0, gk=moe_state["w1_grp"][7])
                wbuf_1 = pA_b if it < 8 else pB_b
                if it == 0:
                    wait("tensor", "s_gp", max(t_w1a, xnh_loaded[half]))
                if it == 8:
                    wait("tensor", "s_gp", t_w1b)
                for nch in range(2):
                    for k in range(8):
                        t_pe = top(lambda g, k=k, nch=nch, it=it, wbuf_1=wbuf_1:
                                   g.matmul(
                                       out=ps_m1[nch][:, :],
                                       lhsT=wbuf_1[:, k * 1024 + (it % 8) * 128:
                                                   k * 1024 + (it % 8) * 128 + 128],
                                       rhs=xnh[:, k * HALF + nch * 512:
                                               k * HALF + (nch + 1) * 512],
                                       start=(k == 0), stop=(k == 7)),
                                   (k == 7 and nch == 1))
                moe_state["w1_grp"][it] = t_pe
                wait("scalar", "s_pe", t_pe)
                for nch in range(2):
                    t_s = sop(lambda g, it=it, nch=nch, ee=ee: g.activation(
                        out=hmid[:, it * HALF + nch * 512:
                                 it * HALF + (nch + 1) * 512],
                        in_=ps_m1[nch][:, :], func=AF.Gelu,
                        bias=b1_sb[:, ee * 16 + it: ee * 16 + it + 1]),
                        nch == 1)
                wait("vector", "s_s", t_s)
                wait("vector", "s_gp", t_c)
                t_sc = vop(lambda g, it=it, slot_c=slot_c: g.tensor_tensor(
                    out=hmid[:, it * HALF:(it + 1) * HALF],
                    in0=hmid[:, it * HALF:(it + 1) * HALF],
                    in1=crep_e[:, slot_c * HALF:(slot_c + 1) * HALF],
                    op=ALU.mult), True)
            moe_state["crep_guard"][slot_c] = t_sc
            moe_state["w1_s_last"] = t_s
            if DEBUG_TAPS and half == 0 and ee == 0:
                wait("gpsimd", "s_v", t_sc)
                dma("gpsimd",
                    taps["tap_hmid"].rearrange("(a p) n -> p a n", p=128),
                    hmid[:, :].rearrange("p (a n) -> p a n", a=16))
                tdma = dma("gpsimd", taps["tap_crep"],
                           crep_e[:, slot_c * HALF:(slot_c + 1) * HALF])
                wait("scalar", "s_gp", tdma)
                wait("tensor", "s_gp", tdma)
            wait("tensor", "s_v", t_sc)
            t_w2b = ld_w2(1, gk=moe_state["w1_grp"][15])
            t_g = None
            for f in range(8):
                w2war = t_g if t_g is not None else moe_state.get("w2_v_last")
                if w2war is not None:
                    wait("tensor", "s_v", w2war)
                wbuf_2 = pA_b if f < 4 else pB_b
                if f == 0:
                    wait("tensor", "s_gp", t_w2a)
                if f == 4:
                    wait("tensor", "s_gp", t_w2b)
                for nch in range(2):
                    for k in range(16):
                        t_pe = top(lambda g, k=k, nch=nch, f=f, wbuf_2=wbuf_2:
                                   g.matmul(
                                       out=ps_m2[nch][:, :],
                                       lhsT=wbuf_2[:, k * 512 + (f % 4) * 128:
                                                   k * 512 + (f % 4) * 128 + 128],
                                       rhs=hmid[:, k * HALF + nch * 512:
                                                k * HALF + (nch + 1) * 512],
                                       start=(k == 0), stop=(k == 15)),
                                   (k == 15 and nch == 1))
                moe_state["w2_grp"][f] = t_pe
                wait("vector", "s_pe", t_pe)
                for nch in range(2):
                    t_g = vop(lambda g, f=f, nch=nch: g.tensor_tensor(
                        out=gacc[:, f * HALF + nch * 512:
                                 f * HALF + (nch + 1) * 512],
                        in0=gacc[:, f * HALF + nch * 512:
                                 f * HALF + (nch + 1) * 512],
                        in1=ps_m2[nch][:, :], op=ALU.add), nch == 1)
            moe_state["w2_last"] = t_g
            moe_state["w2_v_last"] = t_g
            return t_g

        def em_moe_b2(half):
            wait("gpsimd", "s_v", P.get("s_v"))
            t_cm = dma("gpsimd", cmoe[:, :],
                       c_dram[:, half * HALF:(half + 1) * HALF])
            wait("tensor", "s_gp", t_cm)
            wait("tensor", "s_v", moe_state["w2_last"])
            t_g = None
            for f in range(8):
                if t_g is not None:
                    wait("tensor", "s_v", t_g)
                t_pe = top(lambda g, f=f: g.matmul(
                    out=ps_m2[0][:, :],
                    lhsT=b2_sb[:, f * 128:(f + 1) * 128],
                    rhs=cmoe[:, 0:512],
                    start=True, stop=True))
                t_pe = top(lambda g, f=f: g.matmul(
                    out=ps_m2[1][:, :],
                    lhsT=b2_sb[:, f * 128:(f + 1) * 128],
                    rhs=cmoe[:, 512:1024],
                    start=True, stop=True), True)
                wait("vector", "s_pe", t_pe)
                for nch in range(2):
                    t_g = vop(lambda g, f=f, nch=nch: g.tensor_tensor(
                        out=gacc[:, f * HALF + nch * 512:
                                 f * HALF + (nch + 1) * 512],
                        in0=gacc[:, f * HALF + nch * 512:
                                 f * HALF + (nch + 1) * 512],
                        in1=ps_m2[nch][:, :], op=ALU.add), nch == 1)
            if DEBUG_TAPS and half == 0:
                wait("scalar", "s_v", t_g)
                dma("scalar",
                    taps["tap_g"].rearrange("(a p) n -> p a n", p=128),
                    gacc[:, :].rearrange("p (a n) -> p a n", a=8))
            moe_state["w2_last"] = t_g
            return t_g

        yfin_last = [0] * NSTC

        dtmm_t = {}
        uslot_guard = [None, None]

        def em_scan(stc):
            phase_guard("tensor", "vector", "scalar")
            slot = stc // 4
            tcc = stc % 4
            em_poll_simple(sh_fu[slot:slot + 1, tcc:tcc + 1], tcc)
            t_ok = P.bump("s_sy", 16)
            P.em("sync", lambda g, slot=slot, tcc=tcc: g.dma_start(
                out=flagb[0:1, 1:2], in_=sh_fu[slot:slot + 1, tcc:tcc + 1]
            ).then_inc(sems["s_sy"], 16))
            pg_x = dtmm_t.get(7)
            if pg_x is not None:
                wait("sync", "s_pe", pg_x)
            t_xs = dma("sync", xdbl_sc[:, :],
                       sh_xdbl[slot, :, tcc * TC:(tcc + 1) * TC])
            u_read_t = {}

            def em_uread(d, guard):
                t = P.bump("s_sy", 16)

                def f_uread(g, d=d, slot=slot, tcc=tcc, guard=guard):
                    if guard is not None:
                        g.wait_ge(sems["s_v"], guard)
                    em_arms(g, lambda g2, p, d=d, slot=slot, tcc=tcc:
                            g2.dma_start(
                                out=u_scan[:, (d % 2) * TC:(d % 2 + 1) * TC],
                                in_=sh_u[slot, p * 1024 + d * 128:
                                         p * 1024 + (d + 1) * 128,
                                         tcc * TC:(tcc + 1) * TC]
                            ).then_inc(sems["s_sy"], 16))
                P.em("sync", f_uread)
                u_read_t[d] = t

            em_uread(0, uslot_guard[0])
            em_uread(1, uslot_guard[1])
            t_dtu_prev = None
            y_ms = {}
            t_yf = None
            for d in range(8):
                pg = dtmm_t.get(d - 2 if d >= 2 else d + 6)
                if pg is not None:
                    wait("gpsimd", "s_pe", pg)
                t_wd = dma("gpsimd", wdt_sb[:, (d % 2) * 128:(d % 2 + 1) * 128],
                           w_dt_my[:, d * 128:(d + 1) * 128])
                wait("tensor", "s_sy", max(t_xs, u_read_t[d]))
                wait("tensor", "s_gp", t_wd)
                if d >= 2:
                    wait("tensor", "s_s", spx_t[d - 2])
                t_pe = top(lambda g, d=d: g.matmul(
                    out=ps_mm[d % 2][:, :],
                    lhsT=wdt_sb[:, (d % 2) * 128:(d % 2 + 1) * 128],
                    rhs=xdbl_sc[0:64, :], start=True, stop=True), True)
                dtmm_t[d] = t_pe
                wait("scalar", "s_pe", t_pe)
                if t_dtu_prev is not None:
                    wait("scalar", "s_v", t_dtu_prev)
                if d == 0:
                    spx_t = {}
                spx_t[d] = sop(lambda g, d=d: g.activation(
                    out=sp_sb[:, :], in_=ps_mm[d % 2][:, :], func=AF.Exp,
                    bias=bdt_sb[:, d:d + 1]), True)
                wait("scalar", "s_s", spx_t[d])
                t_dt = sop(lambda g: g.activation(
                    out=dt_sb[:, :], in_=sp_sb[:, :], func=AF.Ln, bias=fc_sb[:, 1:2]), True)
                if DEBUG_TAPS and stc == 0 and d == 0:
                    wait("scalar", "s_s", t_dt)
                    tdma = dma("scalar", taps["tap_dt"], dt_sb[:, :])
                    wait("scalar", "s_sc", tdma)
                    wait("vector", "s_sc", tdma)
                wait("vector", "s_s", t_dt)
                t_dtu_prev = vop(lambda g, d=d: g.tensor_tensor(
                    out=dtu_sb[:, :], in0=dt_sb[:, :],
                    in1=u_scan[:, (d % 2) * TC:(d % 2 + 1) * TC],
                    op=ALU.mult), True)
                wait("vector", "s_v", t_dtu_prev)
                for n in range(DS):
                    rs = n % 2
                    ci = d * 16 + n
                    g_war = None
                    if n >= 2:
                        g_war = y_ms[(d, n - 2)]
                    elif d > 0:
                        g_war = y_ms[(d - 1, DS - 2 + n)]
                    if g_war is not None:
                        wait("scalar", "s_v", g_war)
                        wait("gpsimd", "s_v", g_war)
                    t_dA = sop(lambda g, rs=rs, ci=ci: g.activation(
                        out=dA_sb[:, rs * TC:(rs + 1) * TC], in_=dt_sb[:, :],
                        func=AF.Exp, scale=asb[:, ci:ci + 1]), True)
                    wait("gpsimd", "s_sy", t_ok)
                    tb = dma("gpsimd", brep[:, rs * TC:(rs + 1) * TC],
                             sh_xdbl[slot, 64 + n:65 + n, tcc * TC:(tcc + 1) * TC]
                             .partition_broadcast(128))
                    tcr = dma("gpsimd", crep[:, rs * TC:(rs + 1) * TC],
                              sh_xdbl[slot, 80 + n:81 + n, tcc * TC:(tcc + 1) * TC]
                              .partition_broadcast(128))
                    wait("vector", "s_gp", tcr)
                    wait("vector", "s_s", t_dA)
                    t_db = vop(lambda g, rs=rs: g.tensor_tensor(
                        out=dBu_sb[:, rs * TC:(rs + 1) * TC], in0=dtu_sb[:, :],
                        in1=brep[:, rs * TC:(rs + 1) * TC], op=ALU.mult), True)
                    wait("vector", "s_v", t_db)
                    t_sc2 = vop(lambda g, rs=rs, ci=ci: g.tensor_tensor_scan(
                        out=H_sb[:, rs * TC:(rs + 1) * TC],
                        data0=dA_sb[:, rs * TC:(rs + 1) * TC],
                        data1=dBu_sb[:, rs * TC:(rs + 1) * TC],
                        initial=carry[:, ci:ci + 1],
                        op0=ALU.mult, op1=ALU.add), True)
                    wait("vector", "s_v", t_sc2)
                    if n == 0:
                        t_y0 = vop(lambda g, rs=rs: g.tensor_tensor(
                            out=yacc_sb[:, :],
                            in0=H_sb[:, rs * TC:(rs + 1) * TC],
                            in1=crep[:, rs * TC:(rs + 1) * TC],
                            op=ALU.mult), True)
                    else:
                        t_y0 = vop(lambda g, rs=rs: g.tensor_tensor(
                            out=dBu_sb[:, rs * TC:(rs + 1) * TC],
                            in0=H_sb[:, rs * TC:(rs + 1) * TC],
                            in1=crep[:, rs * TC:(rs + 1) * TC], op=ALU.mult),
                            True)
                    wait("vector", "s_v", t_y0)
                    vop(lambda g, rs=rs, ci=ci: g.tensor_copy(
                        carry[:, ci:ci + 1],
                        H_sb[:, rs * TC + TC - 1: rs * TC + TC]))
                    if n == 0:
                        t_y = t_y0
                    else:
                        t_y = vop(lambda g, rs=rs: g.tensor_tensor(
                            out=yacc_sb[:, :], in0=yacc_sb[:, :],
                            in1=dBu_sb[:, rs * TC:(rs + 1) * TC],
                            op=ALU.add), True)
                        wait("vector", "s_v", t_y)
                    y_ms[(d, n)] = t_y
                if d >= 2:
                    wait("vector", "s_sy", ywr_t[d - 2])
                if d == 0:
                    ywr_t = {}
                wait("vector", "s_v", y_ms[(d, DS - 1)])
                t_yf = vop(lambda g, d=d: g.scalar_tensor_tensor(
                    out=yfin_sb[:, (d % 2) * TC:(d % 2 + 1) * TC],
                    in0=u_scan[:, (d % 2) * TC:(d % 2 + 1) * TC],
                    scalar=dco_sb[:, d:d + 1], in1=yacc_sb[:, :],
                    op0=ALU.mult, op1=ALU.add), True)
                if DEBUG_TAPS and stc == 0 and d == 0:
                    wait("scalar", "s_v", t_yf)
                    tdma = dma("scalar", taps["tap_y"], yfin_sb[:, 0:TC])
                    wait("vector", "s_sc", tdma)
                ywr_t[d] = P.bump("s_sy", 16)

                def f_ywrite(g, d=d, stc=stc, t_yf=t_yf):
                    g.wait_ge(sems["s_v"], t_yf)
                    em_arms(g, lambda g2, p, d=d, stc=stc: g2.dma_start(
                        out=sh_y[p, d * 128:(d + 1) * 128,
                                 stc * TC:(stc + 1) * TC],
                        in_=yfin_sb[:, (d % 2) * TC:(d % 2 + 1) * TC]
                    ).then_inc(sems["s_sy"], 16))
                P.em("sync", f_ywrite)
                if d + 2 < 8:
                    em_uread(d + 2, t_yf)
                else:
                    uslot_guard[d % 2] = t_yf
            yfin_last[stc] = t_yf
            t_ally = P.get("s_sy")
            P.bump("s_sy", 16)

            def f_yflag(g, stc=stc, t_ally=t_ally):
                g.wait_ge(sems["s_sy"], t_ally)
                em_arms(g, lambda g2, p, stc=stc: g2.dma_start(
                    out=sh_fy[p:p + 1, stc:stc + 1],
                    in_=non_sb[0:1, stc + 4:stc + 5]).then_inc(sems["s_sy"], 16))
            P.em("sync", f_yflag)

        p3_state = {}

        def em_p3(tc):
            phase_guard("tensor", "vector", "scalar")
            wait("sync", "s_pe", P.get("s_pe"))
            wait("sync", "s_s", P.get("s_s"))
            wait("sync", "s_v", P.get("s_v"))

            def f_poll(g, tc=tc):
                def body(g2, p, tc=tc):
                    stc_own = 4 * p + tc
                    for cs in range(2):
                        _spin(g2, sh_fy[cs:cs + 1, stc_own:stc_own + 1],
                              stc_own + 4)
                em_arms(g, body)
            P.em("sync", f_poll)
            t_okp = P.bump("s_sy", 16)
            P.em("sync", lambda g: g.dma_start(
                out=flagb[0:1, 1:2], in_=sh_fy[0:1, 0:1]
            ).then_inc(sems["s_sy"], 16))

            wait("gpsimd", "s_v", P.get("s_v"))
            wait("gpsimd", "s_pe", P.get("s_pe"))
            wait("gpsimd", "s_s", P.get("s_s"))
            t_xn = dma("gpsimd",
                       xn_fm_tc[:, :].rearrange("p (a n) -> p a n", a=8),
                       xn_dram[:, tc * TC:(tc + 1) * TC]
                       .rearrange("(a p) n -> p a n", p=128))
            wait("vector", "s_gp", t_xn)
            t_h = em_h_fm(tc)
            wait("tensor", "s_s", t_h)

            t_gt = {}
            t_z = None
            zkt_grp = {}

            def ld_z(c, extra_s=None, gk=None):
                buf = wcA_b if c % 2 == 0 else wcB_b
                if extra_s is not None:
                    wait("gpsimd", "s_s", extra_s)
                if gk is not None:
                    wait("gpsimd", "s_pe", gk)
                return dma("gpsimd",
                           buf.rearrange("p (a n) -> p a n", a=8),
                           w_in[:, DI + c * 512:DI + (c + 1) * 512]
                           .rearrange("(a p) n -> p a n", p=128))

            t_zc = {0: ld_z(0), 1: ld_z(1, extra_s=t_h)}
            for kt in range(16):
                base_y = P.bump("s_sy", 16)

                def f_yread(g, tc=tc, kt=kt,
                            guard=(t_gt.get(kt - 2) if kt >= 2 else None)):
                    if guard is not None:
                        g.wait_ge(sems["s_v"], guard)
                    em_arms(g, lambda g2, p, tc=tc, kt=kt: g2.dma_start(
                        out=yfull[:, (kt % 2) * TC:(kt % 2 + 1) * TC],
                        in_=sh_y[kt // 8, (kt % 8) * 128:(kt % 8 + 1) * 128,
                                 p * TOK + tc * TC: p * TOK + (tc + 1) * TC]
                    ).then_inc(sems["s_sy"], 16))
                P.em("sync", f_yread)
                cz = kt // 4
                if kt == 4:
                    t_zc[2] = ld_z(2, gk=zkt_grp[3])
                if kt == 8:
                    t_zc[3] = ld_z(3, gk=zkt_grp[7])
                zbuf = wcA_b if cz % 2 == 0 else wcB_b
                if kt % 4 == 0:
                    wait("tensor", "s_gp", t_zc[cz])
                if t_z is not None:
                    wait("tensor", "s_s", t_z)
                t_pe = None
                for k in range(8):
                    t_pe = top(lambda g, k=k, kt=kt, zbuf=zbuf: g.matmul(
                        out=ps_mm[kt % 2][:, :],
                        lhsT=zbuf[:, k * 512 + (kt % 4) * 128:
                                  k * 512 + (kt % 4) * 128 + 128],
                        rhs=h_fm_tc[:, k * TC:(k + 1) * TC],
                        start=(k == 0), stop=(k == 7)), k == 7)
                zkt_grp[kt] = t_pe
                if kt % 2 == 0:
                    p3_state["z_last"] = t_pe
                else:
                    p3_state["z_last2"] = t_pe
                wait("scalar", "s_pe", t_pe)
                if kt >= 2:
                    wait("scalar", "s_v", t_gt[kt - 2])
                t_z = sop(lambda g, kt=kt: g.activation(
                    out=zsil[:, (kt % 2) * TC:(kt % 2 + 1) * TC],
                    in_=ps_mm[kt % 2][:, :], func=AF.Silu), True)
                if DEBUG_TAPS and tc == 0 and kt == 0:
                    wait("scalar", "s_s", t_z)
                    dma("scalar", taps["tap_z"], zsil[:, 0:TC])
                    wait("scalar", "s_sy", base_y)
                    tdma = dma("scalar", taps["tap_yfull"], yfull[:, 0:TC])
                    wait("scalar", "s_sc", tdma)
                    wait("sync", "s_sc", tdma)
                wait("vector", "s_s", t_z)
                wait("vector", "s_sy", base_y)
                t_gt[kt] = vop(lambda g, kt=kt: g.tensor_tensor(
                    out=ygate[:, kt * TC:(kt + 1) * TC],
                    in0=yfull[:, (kt % 2) * TC:(kt % 2 + 1) * TC],
                    in1=zsil[:, (kt % 2) * TC:(kt % 2 + 1) * TC],
                    op=ALU.mult), True)
                if DEBUG_TAPS and tc == 0 and kt == 0:
                    wait("gpsimd", "s_v", t_gt[0])
                    dma("gpsimd", taps["tap_yg"], ygate[:, 0:TC])
            wait("tensor", "s_v", t_gt[15])
            mo_ev = {}
            t_mo = None
            wo_grp = {}

            def ld_wo(c, gk=None):
                buf = wcA_b if c % 2 == 0 else wcB_b
                if gk is not None:
                    wait("gpsimd", "s_pe", gk)
                t = 0
                for h2 in range(2):
                    t = dma("gpsimd",
                            buf[:, h2 * 2048:(h2 + 1) * 2048]
                            .rearrange("p (a n) -> p a n", a=8),
                            w_out[h2 * 1024:(h2 + 1) * 1024,
                                  c * 256:(c + 1) * 256]
                            .rearrange("(a p) n -> p a n", p=128))
                return t

            t_wo = {0: ld_wo(0, gk=zkt_grp[11]), 1: ld_wo(1, gk=zkt_grp[15])}
            for f in range(8):
                if f >= 2:
                    wait("tensor", "s_v", mo_ev[f - 2])
                cwo = f // 2
                if f == 2:
                    t_wo[2] = ld_wo(2, gk=wo_grp[1])
                if f == 4:
                    t_wo[3] = ld_wo(3, gk=wo_grp[3])
                wobuf = wcA_b if cwo % 2 == 0 else wcB_b
                if f % 2 == 0:
                    wait("tensor", "s_gp", t_wo[cwo])
                for k in range(16):
                    t_pe = top(lambda g, f=f, k=k, wobuf=wobuf: g.matmul(
                        out=ps_mm[f % 2][:, :],
                        lhsT=wobuf[:, k * 256 + (f % 2) * 128:
                                   k * 256 + (f % 2) * 128 + 128],
                        rhs=ygate[:, k * TC:(k + 1) * TC],
                        start=(k == 0), stop=(k == 15)), k == 15)
                wo_grp[f] = t_pe
                if f % 2 == 0:
                    p3_state["wo_last"] = t_pe
                else:
                    p3_state["wo_last2"] = t_pe
                wait("vector", "s_pe", t_pe)
                t_mo = vop(lambda g, f=f: g.tensor_copy(
                    mo_fm[:, f * TC:(f + 1) * TC], ps_mm[f % 2][:, :]), True)
                mo_ev[f] = t_mo
            if DEBUG_TAPS and tc == 0:
                wait("scalar", "s_v", t_mo)
                dma("scalar",
                    taps["tap_mo"].rearrange("(a p) n -> p a n", p=128),
                    mo_fm[:, :].rearrange("p (a n) -> p a n", a=8))
            return t_mo

        out_last = [0]
        fin_state = {}

        def em_final(tc, t_mo, half):
            phase_guard("tensor", "vector", "scalar")
            for jj in range(4):
                j = tc * 4 + jj
                if fin_state.get("t_o") is not None:
                    wait("sync", "s_v", fin_state["t_o"])
                t_x2 = dma("sync", x2_tm[:, :], xrows[j * 128:(j + 1) * 128, :])
                wait("tensor", "s_v", max(t_mo, gacc_ready[half]))
                t_ev = None
                for i in range(8):
                    t_pe = top(lambda g, i=i, jj=jj: g.transpose(
                        out=ps_t[:, 0:128],
                        in_=mo_fm[:, i * TC + jj * 128: i * TC + (jj + 1) * 128],
                        identity=idf_sb[:, :]), True)
                    wait("vector", "s_pe", t_pe)
                    t_ev = vop(lambda g, i=i: g.tensor_copy(
                        mo_tm[:, i * 128:(i + 1) * 128], ps_t[:, 0:128]), True)
                    wait("tensor", "s_v", t_ev)
                    gcol = (tc % 2) * TC + jj * 128
                    t_pe = top(lambda g, i=i, gcol=gcol: g.transpose(
                        out=ps_t[:, 0:128],
                        in_=gacc[:, i * HALF + gcol: i * HALF + gcol + 128],
                        identity=idf_sb[:, :]), True)
                    wait("vector", "s_pe", t_pe)
                    t_ev = vop(lambda g, i=i: g.tensor_copy(
                        g_tm[:, i * 128:(i + 1) * 128], ps_t[:, 0:128]), True)
                    wait("tensor", "s_v", t_ev)
                wait("scalar", "s_v", t_ev)
                t_gs = sop(lambda g: g.activation(out=g_tm, in_=g_tm,
                                                  func=AF.Silu), True)
                wait("vector", "s_s", t_gs)
                wait("vector", "s_sy", t_x2)
                if out_last[0]:
                    wait("vector", "s_sy", out_last[0])
                tf0 = vop(lambda g, j=j: g.tensor_scalar(
                    out=s_tm[:, :], in0=x2_tm[:, :],
                    scalar1=col(C_MU1, j), scalar2=col(C_RS1, j),
                    op0=ALU.subtract, op1=ALU.mult), True)
                wait("vector", "s_v", tf0)
                tf1 = vop(lambda g: g.tensor_tensor(
                    out=s_tm[:, :], in0=s_tm[:, :], in1=mo_tm, op=ALU.add), True)
                wait("vector", "s_v", tf1)
                tf2 = vop(lambda g: g.tensor_tensor(
                    out=s_tm[:, :], in0=s_tm[:, :], in1=g_tm, op=ALU.mult), True)
                wait("vector", "s_v", tf2)
                t_o = vop(lambda g: g.tensor_tensor(
                    out=s_tm[:, :], in0=s_tm[:, :], in1=x2_tm[:, :],
                    op=ALU.add), True)
                fin_state["t_o"] = t_o
                wait("sync", "s_v", t_o)
                out_last[0] = dma("sync", yout[j * 128:(j + 1) * 128, :],
                                  s_tm[:, :])

        em_xnh_load(0)
        for k in range(4):
            em_scan(k)
            for ee in range(4 * k, 4 * k + 4):
                em_moe_ehalf(0, ee, first_in_half=(ee == 0))
            if k == 3:
                gacc_ready[0] = em_moe_b2(0)
        em_scan(4)
        t_mo0 = em_p3(0)
        em_final(0, t_mo0, 0)
        em_scan(5)
        t_mo1 = em_p3(1)
        em_final(1, t_mo1, 0)
        em_xnh_load(1)
        for ee in range(0, 5):
            em_moe_ehalf(1, ee, first_in_half=(ee == 0))
        em_scan(6)
        for ee in range(5, 11):
            em_moe_ehalf(1, ee, first_in_half=False)
        em_scan(7)
        for ee in range(11, 16):
            em_moe_ehalf(1, ee, first_in_half=False)
        gacc_ready[1] = em_moe_b2(1)
        t_mo2 = em_p3(2)
        em_final(2, t_mo2, 1)
        t_mo3 = em_p3(3)
        em_final(3, t_mo3, 1)

        fins = {s_: P.get(s_) for s_ in ("s_sy", "s_gp", "s_sc", "s_vq")}
        P.em("sync", lambda g: g.wait_ge(sems["s_sy"], fins["s_sy"]))
        P.em("gpsimd", lambda g: g.wait_ge(sems["s_gp"], fins["s_gp"]))
        P.em("scalar", lambda g: g.wait_ge(sems["s_sc"], fins["s_sc"]))
        P.em("vector", lambda g: g.wait_ge(sems["s_vq"], fins["s_vq"]))

        with nc.Block() as block:
            @block.sync
            def _(g):
                for fn in P.p["sync"]:
                    fn(g)

            @block.gpsimd
            def _(g):
                for fn in P.p["gpsimd"]:
                    fn(g)

            @block.scalar
            def _(g):
                for fn in P.p["scalar"]:
                    fn(g)

            @block.vector
            def _(g):
                for fn in P.p["vector"]:
                    fn(g)

            @block.tensor
            def _(g):
                for fn in P.p["tensor"]:
                    fn(g)

    return nc


def _cols(vec, ntiles):
    return np.ascontiguousarray(vec.reshape(ntiles, 128).T).astype(np.float32)


def kernel(**inputs):
    x = np.asarray(inputs["L0_attn_x"], np.float32)
    ln_g = np.asarray(inputs["L0_attn_ln_g"], np.float32)
    ln_b = np.asarray(inputs["L0_attn_ln_b"], np.float32)
    W_in = np.asarray(inputs["L0_attn_W_in"], np.float32)
    conv_w = np.asarray(inputs["L0_attn_conv_w"], np.float32)
    conv_b = np.asarray(inputs["L0_attn_conv_b"], np.float32)
    W_xproj = np.asarray(inputs["L0_attn_W_xproj"], np.float32)
    W_dt = np.asarray(inputs["L0_attn_W_dt"], np.float32)
    b_dt = np.asarray(inputs["L0_attn_b_dt"], np.float32)
    A_log = np.asarray(inputs["L0_attn_A_log"], np.float32)
    D = np.asarray(inputs["L0_attn_D"], np.float32)
    W_out = np.asarray(inputs["L0_attn_W_out"], np.float32)
    rw = np.asarray(inputs["L0_ffn_router_w"], np.float32)
    rb = np.asarray(inputs["L0_ffn_router_b"], np.float32)
    w1 = np.asarray(inputs["L0_ffn_w1"], np.float32)
    b1 = np.asarray(inputs["L0_ffn_b1"], np.float32)
    w2 = np.asarray(inputs["L0_ffn_w2"], np.float32)
    b2 = np.asarray(inputs["L0_ffn_b2"], np.float32)

    nonce = int(time.time_ns()) & 0x7FFF0000
    nonces = ((np.arange(1, 17, dtype=np.int64) + nonce) &
              0xFFFFFFFF).astype(np.uint32).reshape(1, 16)

    identb = np.eye(128, dtype=ml_dtypes.bfloat16)
    identf = np.eye(128, dtype=np.float32)
    cw = conv_w.reshape(DI, 4)
    convw_cols = np.zeros((128, 64), np.float32)
    for kt in range(16):
        convw_cols[:, kt * 4:(kt + 1) * 4] = cw[kt * 128:(kt + 1) * 128, :]
    convb_cols = _cols(conv_b, 16)
    lng_cols = _cols(ln_g, 8)
    lnb_cols = _cols(ln_b, 8)
    b1_cols = np.zeros((128, NE * 16), np.float32)
    for ee in range(NE):
        b1_cols[:, ee * 16:(ee + 1) * 16] = _cols(b1[ee], 16)

    in_maps = []
    for c in range(8):
        b, h = c >> 1, c & 1
        lo, hi = h * 1024, (h + 1) * 1024
        xr = np.ascontiguousarray(x[b, h * TOK:(h + 1) * TOK, :])
        xb = np.zeros((4, DM), np.float32)
        if h == 1:
            xb[0:3] = x[b, h * TOK - 3:h * TOK, :]
            m1 = np.ones((128, 1), np.float32)
        else:
            m1 = np.zeros((128, 1), np.float32)
        im = dict(
            xrows=xr, xbnd=xb, mask1=m1,
            parity=np.array([[h]], np.uint32), nonces=nonces,
            w_in=W_in, convw_cols=convw_cols, convb_cols=convb_cols,
            w_xproj=W_xproj,
            w_dt_my=np.ascontiguousarray(W_dt[:, lo:hi]),
            bdt_cols=_cols(b_dt[lo:hi], 8),
            alog_cols=np.ascontiguousarray(
                A_log[lo:hi].reshape(8, 128, 16).transpose(1, 0, 2)
                .reshape(128, 128)),
            d_cols=_cols(D[lo:hi], 8),
            w_out=W_out, lng_cols=lng_cols, lnb_cols=lnb_cols,
            router_w=rw, rb_col=rb.reshape(NE, 1),
            w1=w1, b1_cols=b1_cols, w2=w2, b2_st=b2,
            identb=identb, identf=identf,
            fconst=np.broadcast_to(
                np.array([EPS, 1.0, -0.5, 0.0], np.float32), (128, 4)).copy(),
        )
        in_maps.append(im)

    nc = build_program()
    res = run_bass_kernel_spmd(nc, in_maps, core_ids=list(range(8)))
    out = np.empty((B, L, DM), np.float32)
    for c in range(8):
        b, h = c >> 1, c & 1
        out[b, h * TOK:(h + 1) * TOK, :] = res.results[c]["yout"]
    return out


# version 06
# speedup vs baseline: 1.0005x; 1.0005x over previous
import time
from contextlib import ExitStack

import numpy as np
import ml_dtypes

import concourse.bass as bass
import concourse.mybir as mybir
from concourse.bass_utils import run_bass_kernel_spmd

F32 = mybir.dt.float32
BF16 = mybir.dt.bfloat16
U32 = mybir.dt.uint32
AF = mybir.ActivationFunctionType
ALU = mybir.AluOpType
AX = mybir.AxisListType.X

B, L, DM = 4, 4096, 1024
DI, DS, DTR = 2048, 16, 64
TOK = 2048
TC = 512
NTC = 4
NSTC = 8
NE = 16
HALF = 1024
EPS = 1e-5

DEBUG_TAPS = False


class Plan:
    def __init__(self):
        self.p = {k: [] for k in ("sync", "scalar", "vector", "tensor", "gpsimd")}
        self.cnt = {}

    def em(self, eng, fn):
        self.p[eng].append(fn)

    def bump(self, sem_name, v):
        self.cnt[sem_name] = self.cnt.get(sem_name, 0) + v
        return self.cnt[sem_name]

    def get(self, sem_name):
        return self.cnt.get(sem_name, 0)


def build_program():
    nc = bass.Bass("TRN2", target_bir_lowering=False)
    P = Plan()

    def inp(name, shape, dt=F32):
        return nc.dram_tensor(name, shape, dt, kind="ExternalInput").ap()

    xrows = inp("xrows", [TOK, DM])
    xbnd = inp("xbnd", [4, DM])
    mask1 = inp("mask1", [128, 1])
    parity = inp("parity", [1, 1], U32)
    nonces = inp("nonces", [1, 16], U32)
    w_in = inp("w_in", [DM, 2 * DI])
    convw_cols = inp("convw_cols", [128, 64])
    convb_cols = inp("convb_cols", [128, 16])
    w_xproj = inp("w_xproj", [DI, 96])
    w_dt_my = inp("w_dt_my", [DTR, 1024])
    bdt_cols = inp("bdt_cols", [128, 8])
    alog_cols = inp("alog_cols", [128, 128])
    d_cols = inp("d_cols", [128, 8])
    w_out = inp("w_out", [DI, DM])
    lng_cols = inp("lng_cols", [128, 8])
    lnb_cols = inp("lnb_cols", [128, 8])
    router_w = inp("router_w", [DM, NE])
    rb_col = inp("rb_col", [NE, 1])
    w1 = inp("w1", [NE, DM, DI])
    b1_cols = inp("b1_cols", [128, NE * 16])
    w2 = inp("w2", [NE, DI, DM])
    b2_st = inp("b2_st", [NE, DM])
    identb = inp("identb", [128, 128], BF16)
    fconst = inp("fconst", [128, 4])
    identf = inp("identf", [128, 128])

    yout = nc.dram_tensor("yout", [TOK, DM], F32, kind="ExternalOutput").ap()
    taps = {}
    if DEBUG_TAPS:
        for nm, shp in [("tap_u", [DI, TC]), ("tap_xdbl", [96, TC]),
                        ("tap_dt", [128, TC]),
                        ("tap_c", [NE, TOK]), ("tap_g", [DM, HALF]),
                        ("tap_xn", [DM, TOK]), ("tap_st2", [2, TOK]),
                        ("tap_hmid", [DI, HALF]), ("tap_crep", [128, HALF]),
                        ("tap_z", [128, TC]), ("tap_yg", [128, TC]),
                        ("tap_yfull", [128, TC]),
                        ("tap_h", [DM, TC]), ("tap_lg", [NE, TC]),
                        ("tap_stats", [128, 128]),
                        ("tap_mo", [DM, TC]), ("tap_y", [128, TC])]:
            taps[nm] = nc.dram_tensor(nm, shp, F32, kind="ExternalOutput").ap()

    xn_dram = nc.dram_tensor("xn_dram", [DM, TOK], BF16).ap()
    c_dram = nc.dram_tensor("c_dram", [NE, TOK], F32).ap()
    st2_dram = nc.dram_tensor("st2_dram", [2, TOK], F32).ap()
    sh_u = nc.dram_tensor("sh_u", [2, DI, TOK], F32, addr_space="Shared").ap()
    sh_xdbl = nc.dram_tensor("sh_xdbl", [2, 96, TOK], F32, addr_space="Shared").ap()
    sh_y = nc.dram_tensor("sh_y", [2, 1024, L], F32, addr_space="Shared").ap()
    sh_fu = nc.dram_tensor("sh_fu", [2, 4], U32, addr_space="Shared").ap()
    sh_fy = nc.dram_tensor("sh_fy", [2, 8], U32, addr_space="Shared").ap()

    with ExitStack() as ctx:
        e = ctx.enter_context

        def sb(name, shape, dt=F32):
            return e(nc.sbuf_tensor(name, shape, dt))

        pid_sb = sb("pid_sb", [1, 1], U32)
        non_sb = sb("non_sb", [1, 16], U32)
        mask_sb = sb("mask_sb", [128, 1])
        fc_sb = sb("fc_sb", [128, 4])
        idb_sb = sb("idb_sb", [128, 128], BF16)
        idf_sb = sb("idf_sb", [128, 128])
        wxp_sb = sb("wxp_sb", [128, 2 * 96])
        wdt_sb = sb("wdt_sb", [DTR, 2 * 128])
        cw_sb = sb("cw_sb", [128, 64])
        cb_sb = sb("cb_sb", [128, 16])
        bdt_sb = sb("bdt_sb", [128, 8])
        asb = sb("asb", [128, 128])
        dco_sb = sb("dco_sb", [128, 8])
        lng_sb = sb("lng_sb", [128, 8])
        lnb_sb = sb("lnb_sb", [128, 8])
        rw_sb = sb("rw_sb", [128, 8 * 16])
        rwhi = sb("rwhi", [128, 8 * 16], BF16)
        rwlo = sb("rwlo", [128, 8 * 16], BF16)
        rb_sb = sb("rb_sb", [NE, 1])
        b1_sb = sb("b1_sb", [128, NE * 16])
        b2_sb = sb("b2_sb", [NE, DM])
        stats = sb("stats", [128, 16 * 8])
        carry = sb("carry", [128, 128])
        uprev3 = sb("uprev3", [128, 48])
        flagb = sb("flagb", [1, 2], U32)

        arenaA = sb("arenaA", [128, 11392])
        arA = arenaA[:, :]

        def fview(a, b):
            return arA[:, a:b]

        def bview(a, b):
            return arA[:, a:b].bitcast(BF16)

        x_tm1 = fview(0, 1024)
        xn_tm = fview(1024, 2048)
        xnb_tm = bview(2048, 2560)
        xnl_tm = bview(2560, 3072)
        xnl_fm_tc = bview(3072, 5120)
        uraw = fview(5120, 6150)
        uconv = fview(6150, 7174)
        u_dt = fview(7174, 8710)
        xdbl_tc = fview(8710, 9222)[0:96, :]
        logit_tc = fview(9222, 9734)[0:16, :]
        xb_tm = fview(9734, 10758)[0:4, :]
        hb_stage = bview(10758, 11270)[0:4, :]
        xnh = bview(0, 4096)
        ygate = bview(4096, 8192)

        scr = sb("scr", [128, TC])
        bigA = sb("bigA", [128, 8192])
        xn_fm_tc = bigA[:, 0:2048].bitcast(BF16)
        h_fm_tc = bigA[:, 2048:4096].bitcast(BF16)
        wcA_b = bigA[:, 4096:6144].bitcast(BF16)
        wcB_b = bigA[:, 6144:8192].bitcast(BF16)
        scr2 = bigA[:, 4096:5120]
        mu2rep = bigA[:, 6144:6656]
        rstd2rep = bigA[:, 6656:7168]
        pA_b = bigA[:, 0:4096].bitcast(BF16)
        pB_b = bigA[:, 4096:8192].bitcast(BF16)
        hb_fm = sb("hb_fm", [128, 8 * 4], BF16)
        ctr_st = sb("ctr_st", [NE, 128])
        ltm = sb("ltm", [128, NE])
        mx8 = sb("mx8", [128, 8])
        expw = sb("expw", [128, NE])
        maskw = sb("maskw", [128, NE])
        ctm = sb("ctm", [128, NE])
        tcol = sb("tcol", [128, 4])

        u_scan = sb("u_scan", [128, 2 * TC])
        xdbl_sc = sb("xdbl_sc", [96, TC])
        brep = sb("brep", [128, 4 * TC])
        crep = sb("crep", [128, 4 * TC])
        dt_sb = sb("dt_sb", [128, TC])
        sp_sb = sb("sp_sb", [128, TC])
        dtu_sb = sb("dtu_sb", [128, TC])
        dA_sb = sb("dA_sb", [128, 2 * TC])
        dBu_sb = sb("dBu_sb", [128, 2 * TC])
        H_sb = sb("H_sb", [128, 2 * TC])
        yacc_sb = sb("yacc_sb", [128, TC])
        yfin_sb = sb("yfin_sb", [128, 2 * TC])

        hmid_t = sb("hmid_t", [128, 16 * HALF], BF16)
        hmid = hmid_t[:, :]
        hm_f32 = hmid_t[:, :].bitcast(F32)
        mo_fm = hm_f32[:, 0:4096]
        zsil = hm_f32[:, 4096:5120]
        yfull = hm_f32[:, 5120:6144]
        mo_tm = hm_f32[:, 6144:7168]
        g_tm = hm_f32[:, 7168:8192]
        gacc = sb("gacc", [128, 8 * HALF])
        crep_e = sb("crep_e", [128, 2 * HALF], BF16)
        cmoe = crep_e[:, :].bitcast(F32)[0:16, 0:HALF]

        x2_tm = fview(0, 1024)
        s_tm = fview(1024, 2048)

        def ps(name, shape, dt=F32):
            return e(nc.psum_tensor(name, shape, dt))

        ps_m1 = [ps("ps_m1a", [128, 512]), ps("ps_m1b", [128, 512])]
        ps_m2 = [ps("ps_m2a", [128, 512]), ps("ps_m2b", [128, 512])]
        ps_mm = [ps("ps_mma", [128, 512]), ps("ps_mmb", [128, 512])]
        ps_x = ps("ps_x", [96, 512])
        ps_misc = ps("ps_misc", [128, 512])
        ps_t = ps_misc[:, 0:144]
        ps_tb = ps_misc[:, 144:208].bitcast(BF16)
        ps_bnd = ps_misc[:, 208:212]

        sems = {}
        for s_ in ("s_sy", "s_gp", "s_sc", "s_vq", "s_pe", "s_v", "s_s", "s_pl"):
            sems[s_] = e(nc.semaphore(s_))

        def dma(q, out_ap, in_ap, **kw):
            sem = {"sync": "s_sy", "gpsimd": "s_gp",
                   "scalar": "s_sc", "vector": "s_vq"}[q]
            tgt = P.bump(sem, 16)
            s_h = sems[sem]
            P.em(q, lambda g, o=out_ap, i=in_ap, s=s_h, kw=dict(kw):
                 g.dma_start(out=o, in_=i, **kw).then_inc(s, 16))
            return tgt

        def wait(eng, sem, tgt):
            if tgt is None or tgt <= 0:
                return
            s_h = sems[sem]
            P.em(eng, lambda g, s=s_h, t=tgt: g.wait_ge(s, t))

        def vop(fn, mark=False):
            if not mark:
                P.em("vector", fn)
                return None
            tgt = P.bump("s_v", 1)
            P.em("vector", lambda g, f=fn, s=sems["s_v"]: f(g).then_inc(s, 1))
            return tgt

        def sop(fn, mark=False):
            if not mark:
                P.em("scalar", fn)
                return None
            tgt = P.bump("s_s", 1)
            P.em("scalar", lambda g, f=fn, s=sems["s_s"]: f(g).then_inc(s, 1))
            return tgt

        def top(fn, mark=False):
            if not mark:
                P.em("tensor", fn)
                return None
            tgt = P.bump("s_pe", 1)
            P.em("tensor", lambda g, f=fn, s=sems["s_pe"]: f(g).then_inc(s, 1))
            return tgt

        def phase_guard(*engines):
            snap = {s_: P.get(s_) for s_ in ("s_pe", "s_v", "s_s")}
            own = {"tensor": "s_pe", "vector": "s_v", "scalar": "s_s"}
            for eng in engines:
                for s_, t in snap.items():
                    if own.get(eng) != s_:
                        wait(eng, s_, t)

        rt = {}

        def em_sync_prolog(t_pid):
            def f(g, t_pid=t_pid):
                rt["pollcnt"] = g.alloc_register("pollcnt")
                rt["preg"] = g.alloc_register("preg")
                rt["treg"] = g.alloc_register("treg")
                rt["freg"] = g.alloc_register("freg")
                rt["dreg"] = g.alloc_register("dreg")
                g.reg_mov(rt["pollcnt"], 0)
                g.wait_ge(sems["s_sy"], t_pid)
                g.load(rt["preg"], pid_sb[0:1, 0:1])
            P.em("sync", f)

        class _Arms:
            pass

        def em_arms(g, body):
            with g.If_eq(rt["preg"], 0):
                body(g, 0)
            with g.Else():
                body(g, 1)

        def _spin(g, flag_ap, val_idx):
            g.load(rt["treg"], non_sb[0:1, val_idx:val_idx + 1])

            def cond():
                g.reg_add(rt["pollcnt"], rt["pollcnt"], 16)
                g.dma_start(out=flagb[0:1, 0:1], in_=flag_ap).then_inc(
                    sems["s_pl"], 16)
                g.wait_ge(sems["s_pl"], rt["pollcnt"])
                g.load(rt["freg"], flagb[0:1, 0:1])
                g.reg_sub(rt["dreg"], rt["freg"], rt["treg"])
                return rt["dreg"]
            with g.While(cond):
                g.nop()

        def em_poll_simple(flag_ap, val_idx):
            P.em("sync", lambda g, f_=flag_ap, v_=val_idx: _spin(g, f_, v_))

        t_const = 0
        for out_ap, in_ap in [
            (pid_sb[:, :], parity), (non_sb[:, :], nonces),
            (mask_sb[:, :], mask1), (fc_sb[:, :], fconst),
            (idb_sb[:, :], identb),
            (idf_sb[:, :], identf), (cw_sb[:, :], convw_cols),
            (cb_sb[:, :], convb_cols), (bdt_sb[:, :], bdt_cols),
            (asb[:, :], alog_cols), (dco_sb[:, :], d_cols),
            (lng_sb[:, :], lng_cols), (lnb_sb[:, :], lnb_cols),
            (rb_sb[:, :], rb_col), (b1_sb[:, :], b1_cols),
            (b2_sb[:, :], b2_st),
            (xb_tm[0:3, :], xbnd[0:3, :]),
        ]:
            t_const = dma("sync", out_ap, in_ap)

        for kc in range(8):
            t_const = dma("sync", rw_sb[:, kc * 16:(kc + 1) * 16],
                          router_w[kc * 128:(kc + 1) * 128, :])
        em_sync_prolog(t_const)

        wait("scalar", "s_sy", t_const)
        t_s0 = sop(lambda g: g.activation(out=asb[:, :], in_=asb[:, :],
                                          func=AF.Exp), True)
        wait("vector", "s_s", t_s0)
        wait("vector", "s_sy", t_const)
        vop(lambda g: g.tensor_scalar(out=asb[:, :], in0=asb[:, :],
                                      scalar1=-1.0, scalar2=None, op0=ALU.mult))
        vop(lambda g: g.memset(carry[:, :], 0.0))
        vop(lambda g: g.tensor_copy(rwhi[:, :], rw_sb[:, :]))
        t_rw = vop(lambda g: g.tensor_tensor(out=rwlo[:, :], in0=rw_sb[:, :],
                                             in1=rwhi[:, :], op=ALU.subtract), True)

        C_SUM1, C_MU1, C_RS1, C_SS1, C_SUM2, C_SS2, C_MU2, C_RS2 = range(8)

        def col(ci, j):
            return stats[:, ci * 16 + j: ci * 16 + j + 1]

        xfm_done = [0] * NTC
        lg_guard = [0]
        t_cd = 0
        st2_done = [0] * NTC
        ctm_done = [0] * NTC
        xnd_done = [0] * NTC
        xtm_consumed = []

        for tc in range(NTC):
            for jj in range(4):
                j = tc * 4 + jj
                xt = x_tm1
                if j >= 1:
                    wait("sync", "s_v", xtm_consumed[j - 1])
                t_x = dma("sync", xt, xrows[j * 128:(j + 1) * 128, :])
                wait("vector", "s_sy", t_x)
                wait("scalar", "s_sy", t_x)
                t1 = vop(lambda g, xt=xt, j=j: g.reduce_sum(
                    out=col(C_SUM1, j), in_=xt, axis=AX), True)
                ts1 = sop(lambda g, xt=xt, j=j: g.activation(
                    out=scr2[:, :], in_=xt, func=AF.Square,
                    accum_out=col(C_SS1, j)), True)
                wait("vector", "s_v", t1)
                t2 = vop(lambda g, j=j: g.tensor_scalar(
                    out=col(C_MU1, j), in0=col(C_SUM1, j),
                    scalar1=1.0 / DM, scalar2=None, op0=ALU.mult), True)
                wait("vector", "s_s", ts1)
                wait("vector", "s_v", t2)
                vop(lambda g, j=j: g.tensor_tensor(
                    out=tcol[:, 0:1], in0=col(C_MU1, j), in1=col(C_MU1, j),
                    op=ALU.mult))
                t3 = vop(lambda g, j=j: g.tensor_scalar(
                    out=tcol[:, 1:2], in0=col(C_SS1, j),
                    scalar1=1.0 / DM, scalar2=None, op0=ALU.mult), True)
                wait("vector", "s_v", t3)
                t4 = vop(lambda g: g.tensor_tensor(
                    out=tcol[:, 1:2], in0=tcol[:, 1:2], in1=tcol[:, 0:1],
                    op=ALU.subtract), True)
                wait("scalar", "s_v", t4)
                ts2a = sop(lambda g: g.activation(
                    out=tcol[:, 2:3], in_=tcol[:, 1:2],
                    func=AF.Ln, bias=fc_sb[:, 0:1]), True)
                wait("scalar", "s_s", ts2a)
                ts2 = sop(lambda g, j=j: g.activation(
                    out=col(C_RS1, j), in_=tcol[:, 2:3], func=AF.Exp,
                    scale=fc_sb[:, 2:3]), True)
                wait("vector", "s_s", ts2)
                t_xn = vop(lambda g, xt=xt, j=j: g.tensor_scalar(
                    out=xn_tm[:, :], in0=xt, scalar1=col(C_MU1, j),
                    scalar2=col(C_RS1, j), op0=ALU.subtract, op1=ALU.mult), True)
                vop(lambda g: g.tensor_copy(xnb_tm[:, :], xn_tm[:, :]))
                t_lo = vop(lambda g: g.tensor_tensor(
                    out=xnl_tm[:, :], in0=xn_tm[:, :], in1=xnb_tm[:, :],
                    op=ALU.subtract), True)
                xtm_consumed.append(t_lo)
                r2 = vop(lambda g, j=j: g.reduce_sum(
                    out=col(C_SUM2, j), in_=xn_tm[:, :], axis=AX), True)
                wait("vector", "s_v", r2)
                m2 = vop(lambda g, j=j: g.tensor_scalar(
                    out=col(C_MU2, j), in0=col(C_SUM2, j),
                    scalar1=1.0 / DM, scalar2=None, op0=ALU.mult), True)
                wait("scalar", "s_v", t_xn)
                ss2 = sop(lambda g, j=j: g.activation(
                    out=scr2[:, :], in_=xn_tm[:, :], func=AF.Square,
                    accum_out=col(C_SS2, j)), True)
                wait("vector", "s_s", ss2)
                wait("vector", "s_v", m2)
                vop(lambda g, j=j: g.tensor_tensor(
                    out=tcol[:, 0:1], in0=col(C_MU2, j), in1=col(C_MU2, j),
                    op=ALU.mult))
                t3b = vop(lambda g, j=j: g.tensor_scalar(
                    out=tcol[:, 1:2], in0=col(C_SS2, j),
                    scalar1=1.0 / DM, scalar2=None, op0=ALU.mult), True)
                wait("vector", "s_v", t3b)
                t4b = vop(lambda g: g.tensor_tensor(
                    out=tcol[:, 1:2], in0=tcol[:, 1:2], in1=tcol[:, 0:1],
                    op=ALU.subtract), True)
                wait("scalar", "s_v", t4b)
                ts3a = sop(lambda g: g.activation(
                    out=tcol[:, 2:3], in_=tcol[:, 1:2],
                    func=AF.Ln, bias=fc_sb[:, 0:1]), True)
                wait("scalar", "s_s", ts3a)
                t_s2 = sop(lambda g, j=j: g.activation(
                    out=col(C_RS2, j), in_=tcol[:, 2:3], func=AF.Exp,
                    scale=fc_sb[:, 2:3]), True)
                wait("gpsimd", "s_s", t_s2)
                st2_done[tc] = dma(
                    "gpsimd",
                    st2_dram[0:1, j * 128:(j + 1) * 128].rearrange("a b -> b a"),
                    col(C_MU2, j))
                st2_done[tc] = dma(
                    "gpsimd",
                    st2_dram[1:2, j * 128:(j + 1) * 128].rearrange("a b -> b a"),
                    col(C_RS2, j))
                wait("tensor", "s_v", t_lo)
                t_ev = None
                for i in range(8):
                    t_pe = top(lambda g, i=i: g.transpose(
                        out=ps_tb, in_=xnb_tm[:, i * 128:(i + 1) * 128],
                        identity=idb_sb[:, :]), True)
                    wait("vector", "s_pe", t_pe)
                    t_ev = vop(lambda g, i=i, jj=jj: g.tensor_copy(
                        xn_fm_tc[:, i * TC + jj * 128: i * TC + (jj + 1) * 128],
                        ps_tb), True)
                    wait("tensor", "s_v", t_ev)
                    t_pe = top(lambda g, i=i: g.transpose(
                        out=ps_tb, in_=xnl_tm[:, i * 128:(i + 1) * 128],
                        identity=idb_sb[:, :]), True)
                    wait("vector", "s_pe", t_pe)
                    t_ev = vop(lambda g, i=i, jj=jj: g.tensor_copy(
                        xnl_fm_tc[:, i * TC + jj * 128: i * TC + (jj + 1) * 128],
                        ps_tb), True)
                    wait("tensor", "s_v", t_ev)
            xfm_done[tc] = t_ev
            wait("scalar", "s_v", t_ev)
            for i in range(8):
                xnd_done[tc] = dma("scalar",
                                   xn_dram[i * 128:(i + 1) * 128,
                                           tc * TC:(tc + 1) * TC],
                                   xn_fm_tc[:, i * TC:(i + 1) * TC])
            wait("tensor", "s_sy", t_const)
            wait("tensor", "s_v", max(xfm_done[tc], t_rw))
            nmm = 0
            t_r = None
            for term in range(3):
                for kc in range(8):
                    st_ap = (rwhi if term < 2 else rwlo)[:, kc * 16:(kc + 1) * 16]
                    mv = (xn_fm_tc if term != 1 else xnl_fm_tc)
                    mv_ap = mv[:, kc * TC:(kc + 1) * TC]
                    last = (nmm == 23)
                    t_r = top(lambda g, s_=st_ap, m_=mv_ap, f_=(nmm == 0), l_=last:
                              g.matmul(out=ps_x[0:16, :], lhsT=s_, rhs=m_,
                                       start=f_, stop=l_), last)
                    nmm += 1
            if DEBUG_TAPS and tc == 0:
                pass
            wait("scalar", "s_pe", t_r)
            if tc > 0:
                wait("scalar", "s_pe", lg_guard[0])
            t_lg = sop(lambda g: g.activation(
                out=logit_tc[:, :], in_=ps_x[0:16, :],
                func=AF.Identity, bias=rb_sb[:, :]), True)
            if DEBUG_TAPS and tc == 0:
                wait("gpsimd", "s_s", t_lg)
                dma("gpsimd", taps["tap_lg"], logit_tc[:, :])
            for jj in range(4):
                wait("tensor", "s_s", t_lg)
                t_pe = top(lambda g, jj=jj: g.transpose(
                    out=ps_t[:, 0:16],
                    in_=logit_tc[:, jj * 128:(jj + 1) * 128],
                    identity=idf_sb[0:16, 0:16]), True)
                lg_guard[0] = t_pe
                wait("vector", "s_pe", t_pe)
                tv0 = vop(lambda g: g.tensor_copy(ltm[:, :], ps_t[:, 0:16]), True)
                wait("vector", "s_v", tv0)
                tv1 = vop(lambda g: g.max(out=mx8[:, :], in_=ltm[:, :]), True)
                wait("vector", "s_v", tv1)
                vop(lambda g: g.tensor_scalar(
                    out=expw[:, :], in0=ltm[:, :], scalar1=mx8[:, 0:1],
                    scalar2=None, op0=ALU.subtract))
                t_v = vop(lambda g: g.tensor_scalar(
                    out=maskw[:, :], in0=ltm[:, :], scalar1=mx8[:, 3:4],
                    scalar2=None, op0=ALU.is_ge), True)
                wait("scalar", "s_v", t_v)
                t_s = sop(lambda g: g.activation(out=expw[:, :], in_=expw[:, :],
                                                 func=AF.Exp), True)
                wait("vector", "s_s", t_s)
                tv2 = vop(lambda g: g.tensor_tensor(
                    out=expw[:, :], in0=expw[:, :],
                    in1=maskw[:, :], op=ALU.mult), True)
                wait("vector", "s_v", tv2)
                tv3 = vop(lambda g: g.reduce_sum(out=tcol[:, 0:1], in_=expw[:, :],
                                                 axis=AX), True)
                wait("vector", "s_v", tv3)
                tv4 = vop(lambda g: g.reciprocal(out=tcol[:, 1:2],
                                                 in_=tcol[:, 0:1]), True)
                wait("vector", "s_v", tv4)
                t_v = vop(lambda g: g.tensor_scalar(
                    out=ctm[:, :], in0=expw[:, :], scalar1=tcol[:, 1:2],
                    scalar2=None, op0=ALU.mult), True)
                wait("tensor", "s_v", t_v)
                t_pe = top(lambda g: g.transpose(
                    out=ps_misc[0:16, 16:144], in_=ctm[:, :],
                    identity=idf_sb[:, :]), True)
                wait("vector", "s_pe", t_pe)
                if tc + jj > 0:
                    wait("vector", "s_sc", t_cd)
                ctm_done[tc] = vop(lambda g: g.tensor_copy(
                    ctr_st[:, :], ps_misc[0:16, 16:144]), True)
                wait("scalar", "s_v", ctm_done[tc])
                t_cd = dma("scalar",
                           c_dram[:, tc * TC + jj * 128: tc * TC + (jj + 1) * 128],
                           ctr_st[:, :])
        if DEBUG_TAPS:
            wait("gpsimd", "s_sc", t_cd)
            dma("gpsimd", taps["tap_c"], c_dram[:, :])
            wait("gpsimd", "s_sc", max(xnd_done))
            dma("gpsimd", taps["tap_xn"], xn_dram[:, :])
            dma("gpsimd", taps["tap_st2"], st2_dram[:, :])
            wait("gpsimd", "s_v", P.get("s_v"))
            dma("gpsimd", taps["tap_stats"], stats[:, :])

        def em_ln_rows(src_ap, n_):
            tv0 = vop(lambda g: g.reduce_sum(out=tcol[0:n_, 0:1], in_=src_ap,
                                             axis=AX), True)
            wait("vector", "s_v", tv0)
            t_v = vop(lambda g: g.tensor_scalar(
                out=tcol[0:n_, 0:1], in0=tcol[0:n_, 0:1], scalar1=1.0 / DM,
                scalar2=None, op0=ALU.mult), True)
            wait("scalar", "s_v", t_v)
            t_s = sop(lambda g: g.activation(
                out=scr2[0:n_, :], in_=src_ap, func=AF.Square,
                accum_out=tcol[0:n_, 1:2]), True)
            wait("vector", "s_s", t_s)
            wait("vector", "s_v", t_v)
            tv1 = vop(lambda g: g.tensor_tensor(
                out=tcol[0:n_, 2:3], in0=tcol[0:n_, 0:1],
                in1=tcol[0:n_, 0:1], op=ALU.mult), True)
            tv2 = vop(lambda g: g.tensor_scalar(
                out=tcol[0:n_, 1:2], in0=tcol[0:n_, 1:2],
                scalar1=1.0 / DM, scalar2=None, op0=ALU.mult), True)
            wait("vector", "s_v", tv2)
            t_v = vop(lambda g: g.tensor_tensor(
                out=tcol[0:n_, 1:2], in0=tcol[0:n_, 1:2], in1=tcol[0:n_, 2:3],
                op=ALU.subtract), True)
            wait("scalar", "s_v", t_v)
            ts1 = sop(lambda g: g.activation(
                out=tcol[0:n_, 2:3], in_=tcol[0:n_, 1:2],
                func=AF.Ln, bias=fc_sb[0:n_, 0:1]), True)
            wait("scalar", "s_s", ts1)
            t_s = sop(lambda g: g.activation(out=tcol[0:n_, 3:4],
                                             in_=tcol[0:n_, 2:3], func=AF.Exp,
                                             scale=fc_sb[0:n_, 2:3]), True)
            wait("vector", "s_s", t_s)
            t_v = vop(lambda g: g.tensor_scalar(
                out=src_ap, in0=src_ap, scalar1=tcol[0:n_, 0:1],
                scalar2=tcol[0:n_, 3:4], op0=ALU.subtract, op1=ALU.mult), True)
            wait("vector", "s_v", t_v)
            return t_v

        wait("vector", "s_sy", t_const)
        em_ln_rows(xb_tm[0:3, :], 3)
        em_ln_rows(xb_tm[0:3, :], 3)
        t_hb = vop(lambda g: g.tensor_copy(hb_stage[0:3, :], xb_tm[0:3, :]), True)
        wait("tensor", "s_v", t_hb)
        for i in range(8):
            t_pe = top(lambda g, i=i: g.transpose(
                out=ps_tb[:, 0:3], in_=hb_stage[0:3, i * 128:(i + 1) * 128],
                identity=idb_sb[0:3, 0:3]), True)
            wait("scalar", "s_pe", t_pe)
            t_hb2 = sop(lambda g, i=i: g.activation(
                out=hb_fm[:, i * 4:i * 4 + 3], in_=ps_tb[:, 0:3],
                func=AF.Identity, scale=lng_sb[:, i:i + 1],
                bias=lnb_sb[:, i:i + 1]), True)
            wait("tensor", "s_s", t_hb2)

        def em_h_fm(tc):
            phase_guard("vector", "scalar")
            wait("gpsimd", "s_v", P.get("s_v"))
            t_m = dma("gpsimd", mu2rep[:, :],
                      st2_dram[0:1, tc * TC:(tc + 1) * TC].partition_broadcast(128))
            t_r2 = dma("gpsimd", rstd2rep[:, :],
                       st2_dram[1:2, tc * TC:(tc + 1) * TC].partition_broadcast(128))
            wait("vector", "s_gp", t_r2)
            wait("vector", "s_sc", st2_done[tc])
            ts = None
            for i in range(8):
                if ts is not None:
                    wait("vector", "s_s", ts)
                tv0 = vop(lambda g, i=i: g.tensor_tensor(
                    out=scr[:, 0:TC], in0=xn_fm_tc[:, i * TC:(i + 1) * TC],
                    in1=mu2rep[:, :], op=ALU.subtract), True)
                wait("vector", "s_v", tv0)
                tv = vop(lambda g: g.tensor_tensor(
                    out=scr[:, 0:TC], in0=scr[:, 0:TC], in1=rstd2rep[:, :],
                    op=ALU.mult), True)
                wait("scalar", "s_v", tv)
                ts = sop(lambda g, i=i: g.activation(
                    out=h_fm_tc[:, i * TC:(i + 1) * TC], in_=scr[:, 0:TC],
                    func=AF.Identity, scale=lng_sb[:, i:i + 1],
                    bias=lnb_sb[:, i:i + 1]), True)
            return ts

        uflag_t = [0] * NTC
        p1_state = {}
        xp_mm_t = {}
        ud_t = {}
        ex_t = {}
        winu_grp_t = {}

        for tc in range(NTC):
            wait("gpsimd", "s_v", P.get("s_v"))
            wait("gpsimd", "s_pe", P.get("s_pe"))
            wait("gpsimd", "s_s", P.get("s_s"))
            wait("gpsimd", "s_sc", xnd_done[tc])
            t_xr = dma("gpsimd",
                       xn_fm_tc[:, :].rearrange("p (a n) -> p a n", a=8),
                       xn_dram[:, tc * TC:(tc + 1) * TC]
                       .rearrange("(a p) n -> p a n", p=128))
            wait("vector", "s_gp", t_xr)
            t_h = em_h_fm(tc)
            if DEBUG_TAPS and tc == 0:
                wait("gpsimd", "s_s", t_h)
                dma("gpsimd", taps["tap_h"].rearrange("(a p) n -> p a n", p=128),
                    h_fm_tc[:, :].rearrange("p (a n) -> p a n", a=8))
            wait("tensor", "s_s", t_h)
            phase_guard("tensor")
            conv_tv = {}

            def ld_win(c, tc=tc, extra_s=None):
                buf = wcA_b if c % 2 == 0 else wcB_b
                if extra_s is not None:
                    wait("gpsimd", "s_s", extra_s)
                if c >= 2:
                    wait("gpsimd", "s_pe", winu_grp_t[(tc, c * 4 - 5)])
                return dma("gpsimd",
                           buf.rearrange("p (a n) -> p a n", a=8),
                           w_in[:, c * 512:(c + 1) * 512]
                           .rearrange("(a p) n -> p a n", p=128))

            t_wc = {0: ld_win(0), 1: ld_win(1, extra_s=t_h)}
            for kt in range(16):
                slot_u = kt % 3
                if kt >= 3:
                    wait("scalar", "s_pe", xp_mm_t[(tc, kt - 3)])
                elif tc > 0:
                    wait("scalar", "s_pe", xp_mm_t[(tc - 1, kt + 13)])
                pswar = conv_tv.get(kt - 1 if tc == 0 else kt - 2)
                if pswar is None and kt == 0 and tc > 0:
                    pswar = p1_state.get("conv_last")
                if pswar is not None:
                    wait("tensor", "s_v", pswar)
                cw_ = kt // 4
                if kt == 4:
                    t_wc[2] = ld_win(2)
                if kt == 8:
                    t_wc[3] = ld_win(3)
                wbuf_p1 = wcA_b if cw_ % 2 == 0 else wcB_b
                if kt % 4 == 0:
                    wait("tensor", "s_gp", t_wc[cw_])

                def lw(k, kt=kt, wbuf_p1=wbuf_p1):
                    o = k * 512 + (kt % 4) * 128
                    return wbuf_p1[:, o:o + 128]
                for k in range(8):
                    t_pe = top(lambda g, k=k, kt=kt, lw=lw: g.matmul(
                        out=ps_mm[kt % 2][:, :],
                        lhsT=lw(k),
                        rhs=h_fm_tc[:, k * TC:(k + 1) * TC],
                        start=(k == 0), stop=(k == 7)), k == 7)
                if tc == 0:
                    for k in range(8):
                        t_pe = top(lambda g, k=k, lw=lw: g.matmul(
                            out=ps_bnd[:, 0:3],
                            lhsT=lw(k),
                            rhs=hb_fm[:, k * 4:k * 4 + 3],
                            start=(k == 0), stop=(k == 7)), k == 7)
                winu_grp_t[(tc, kt)] = t_pe
                wait("vector", "s_pe", t_pe)
                ur = uraw[:, (kt % 2) * 515:(kt % 2) * 515 + 515]
                if tc == 0:
                    t_bm = vop(lambda g, kt=kt: g.tensor_scalar(
                        out=uprev3[:, kt * 3:(kt + 1) * 3], in0=ps_bnd[:, 0:3],
                        scalar1=mask_sb[:, :], scalar2=None, op0=ALU.mult), True)
                    wait("vector", "s_v", t_bm)
                vop(lambda g, ur=ur, kt=kt: g.tensor_copy(
                    ur[:, 0:3], uprev3[:, kt * 3:(kt + 1) * 3]))
                t_ur = vop(lambda g, ur=ur, kt=kt: g.tensor_copy(
                    ur[:, 3:515], ps_mm[kt % 2][:, :]), True)
                wait("vector", "s_v", t_ur)
                t_up = vop(lambda g, ur=ur, kt=kt: g.tensor_copy(
                    uprev3[:, kt * 3:(kt + 1) * 3], ur[:, 512:515]), True)
                wait("vector", "s_v", t_up)
                ucwar = ud_t.get((tc, kt - 2)) if kt >= 2 else \
                    (ud_t.get((tc - 1, kt + 14)) if tc > 0 else None)
                if ucwar is not None:
                    wait("vector", "s_s", ucwar)
                uc = uconv[:, (kt % 2) * TC:(kt % 2 + 1) * TC]
                tv = vop(lambda g, ur=ur, uc=uc, kt=kt: g.tensor_scalar(
                    out=uc, in0=ur[:, 0:512], scalar1=cw_sb[:, kt * 4:kt * 4 + 1],
                    scalar2=None, op0=ALU.mult), True)
                for tap in range(1, 4):
                    wait("vector", "s_v", tv)
                    tv = vop(lambda g, ur=ur, uc=uc, kt=kt, tap=tap:
                             g.scalar_tensor_tensor(
                                 out=uc, in0=ur[:, tap:tap + 512],
                                 scalar=cw_sb[:, kt * 4 + tap:kt * 4 + tap + 1],
                                 in1=uc, op0=ALU.mult, op1=ALU.add),
                             True)
                conv_tv[kt] = tv
                p1_state["conv_last"] = tv
                wait("scalar", "s_v", tv)
                exwar = ex_t.get((tc, kt - 3)) if kt >= 3 else \
                    (ex_t.get((tc - 1, kt + 13)) if tc > 0 else None)
                if exwar is not None:
                    wait("scalar", "s_sy", exwar)
                ud = u_dt[:, slot_u * TC:(slot_u + 1) * TC]
                t_ud = sop(lambda g, uc=uc, ud=ud, kt=kt: g.activation(
                    out=ud, in_=uc, func=AF.Silu,
                    bias=cb_sb[:, kt:kt + 1]), True)
                ud_t[(tc, kt)] = t_ud
                if kt >= 2 or tc > 0:
                    pg = xp_mm_t[(tc, kt - 2)] if kt >= 2 else \
                        xp_mm_t[(tc - 1, kt + 14)]
                    wait("gpsimd", "s_pe", pg)
                t_wx = dma("gpsimd", wxp_sb[:, (kt % 2) * 96:(kt % 2 + 1) * 96],
                           w_xproj[kt * 128:(kt + 1) * 128, :])
                wait("tensor", "s_s", t_ud)
                wait("tensor", "s_gp", t_wx)
                xp_mm_t[(tc, kt)] = top(lambda g, kt=kt, ud=ud: g.matmul(
                    out=ps_x[:, :], lhsT=wxp_sb[:, (kt % 2) * 96:(kt % 2 + 1) * 96],
                    rhs=ud, start=(kt == 0), stop=(kt == 15)), True)
                base_sy = P.bump("s_sy", 16)
                ex_t[(tc, kt)] = base_sy

                def f_uexch(g, tc=tc, kt=kt, ud=ud, t_ud=t_ud):
                    g.wait_ge(sems["s_s"], t_ud)
                    em_arms(g, lambda g2, p, tc=tc, kt=kt, ud=ud: g2.dma_start(
                        out=sh_u[p, kt * 128:(kt + 1) * 128,
                                 tc * TC:(tc + 1) * TC],
                        in_=ud).then_inc(sems["s_sy"], 16))
                P.em("sync", f_uexch)
                if DEBUG_TAPS and tc == 0:
                    wait("scalar", "s_s", t_ud)
                    tdma = dma("scalar", taps["tap_u"][kt * 128:(kt + 1) * 128, :], ud)
                    wait("scalar", "s_sc", tdma)
            wait("vector", "s_pe", xp_mm_t[(tc, 15)])
            t_xd = vop(lambda g: g.tensor_copy(xdbl_tc[:, :], ps_x[:, :]), True)
            if DEBUG_TAPS and tc == 0:
                wait("scalar", "s_v", t_xd)
                tdma = dma("scalar", taps["tap_xdbl"], xdbl_tc[:, :])
                wait("vector", "s_sc", tdma)
            t_f1 = P.bump("s_sy", 16)
            t_f2 = P.bump("s_sy", 16)

            def f_xexch(g, tc=tc, t_xd=t_xd, t_f1=t_f1):
                g.wait_ge(sems["s_v"], t_xd)
                em_arms(g, lambda g2, p, tc=tc: g2.dma_start(
                    out=sh_xdbl[p, :, tc * TC:(tc + 1) * TC],
                    in_=xdbl_tc[:, :]).then_inc(sems["s_sy"], 16))
                g.wait_ge(sems["s_sy"], t_f1)
                em_arms(g, lambda g2, p, tc=tc: g2.dma_start(
                    out=sh_fu[p:p + 1, tc:tc + 1],
                    in_=non_sb[0:1, tc:tc + 1]).then_inc(sems["s_sy"], 16))
            P.em("sync", f_xexch)
            uflag_t[tc] = t_f2

        gacc_ready = [0, 0]
        xnh_loaded = [0, 0]
        moe_state = {"crep_guard": {}, "hmid_guard": None, "w2_last": None,
                     "w1_grp": {}, "w2_grp": {}}

        def em_xnh_load(half):
            wait("gpsimd", "s_pe", P.get("s_pe"))
            wait("gpsimd", "s_v", P.get("s_v"))
            wait("gpsimd", "s_s", P.get("s_s"))
            wait("gpsimd", "s_sy", P.get("s_sy"))
            t = dma("gpsimd",
                    xnh[:, :].rearrange("p (a n) -> p a n", a=8),
                    xn_dram[:, half * HALF:(half + 1) * HALF]
                    .rearrange("(a p) n -> p a n", p=128))
            xnh_loaded[half] = t

        def em_moe_ehalf(half, ee, first_in_half):
            phase_guard("tensor", "vector", "scalar")
            slot_c = ee % 2
            g_ = moe_state["crep_guard"].get(slot_c)
            if g_ is not None:
                wait("gpsimd", "s_v", g_)
            wait("gpsimd", "s_pe", P.get("s_pe") if first_in_half else 0)
            if first_in_half:
                wait("gpsimd", "s_v", P.get("s_v"))
                wait("gpsimd", "s_s", P.get("s_s"))
            wait("gpsimd", "s_sc", t_cd)
            t_c = dma("gpsimd", crep_e[:, slot_c * HALF:(slot_c + 1) * HALF],
                      c_dram[ee:ee + 1, half * HALF:(half + 1) * HALF]
                      .partition_broadcast(128))
            if first_in_half:
                wait("vector", "s_pe", P.get("s_pe"))
                vop(lambda g: g.memset(gacc[:, :], 0.0))
            if moe_state["w2_grp"].get(7) is not None:
                wait("scalar", "s_pe", moe_state["w2_grp"][7])

            def ld_w1(cc, ee=ee, gk=None):
                buf = pA_b if cc == 0 else pB_b
                if gk is not None:
                    wait("gpsimd", "s_pe", gk)
                return dma("gpsimd",
                           buf.rearrange("p (a n) -> p a n", a=8),
                           w1[ee, :, cc * 1024:(cc + 1) * 1024]
                           .rearrange("(a p) n -> p a n", p=128))

            def ld_w2(cc, ee=ee, gk=None):
                buf = pA_b if cc == 0 else pB_b
                if gk is not None:
                    wait("gpsimd", "s_pe", gk)
                t = 0
                for h2 in range(2):
                    t = dma("gpsimd",
                            buf[:, h2 * 4096:(h2 + 1) * 4096]
                            .rearrange("p (a n) -> p a n", a=8),
                            w2[ee, h2 * 1024:(h2 + 1) * 1024,
                               cc * 512:(cc + 1) * 512]
                            .rearrange("(a p) n -> p a n", p=128))
                return t

            t_w1a = ld_w1(0, gk=moe_state["w2_grp"].get(3))
            t_w1b = ld_w1(1, gk=moe_state["w2_grp"].get(7))
            t_w2a = None
            t_s = None
            t_sc = None
            for it in range(16):
                w1war = t_s if t_s is not None else moe_state.get("w1_s_last")
                if w1war is not None:
                    wait("tensor", "s_s", w1war)
                if it == 8:
                    t_w2a = ld_w2(0, gk=moe_state["w1_grp"][7])
                wbuf_1 = pA_b if it < 8 else pB_b
                if it == 0:
                    wait("tensor", "s_gp", max(t_w1a, xnh_loaded[half]))
                if it == 8:
                    wait("tensor", "s_gp", t_w1b)
                for nch in range(2):
                    for k in range(8):
                        t_pe = top(lambda g, k=k, nch=nch, it=it, wbuf_1=wbuf_1:
                                   g.matmul(
                                       out=ps_m1[nch][:, :],
                                       lhsT=wbuf_1[:, k * 1024 + (it % 8) * 128:
                                                   k * 1024 + (it % 8) * 128 + 128],
                                       rhs=xnh[:, k * HALF + nch * 512:
                                               k * HALF + (nch + 1) * 512],
                                       start=(k == 0), stop=(k == 7)),
                                   (k == 7 and nch == 1))
                moe_state["w1_grp"][it] = t_pe
                wait("scalar", "s_pe", t_pe)
                for nch in range(2):
                    t_s = sop(lambda g, it=it, nch=nch, ee=ee: g.activation(
                        out=hmid[:, it * HALF + nch * 512:
                                 it * HALF + (nch + 1) * 512],
                        in_=ps_m1[nch][:, :], func=AF.Gelu,
                        bias=b1_sb[:, ee * 16 + it: ee * 16 + it + 1]),
                        nch == 1)
                wait("vector", "s_s", t_s)
                wait("vector", "s_gp", t_c)
                t_sc = vop(lambda g, it=it, slot_c=slot_c: g.tensor_tensor(
                    out=hmid[:, it * HALF:(it + 1) * HALF],
                    in0=hmid[:, it * HALF:(it + 1) * HALF],
                    in1=crep_e[:, slot_c * HALF:(slot_c + 1) * HALF],
                    op=ALU.mult), True)
            moe_state["crep_guard"][slot_c] = t_sc
            moe_state["w1_s_last"] = t_s
            if DEBUG_TAPS and half == 0 and ee == 0:
                wait("gpsimd", "s_v", t_sc)
                dma("gpsimd",
                    taps["tap_hmid"].rearrange("(a p) n -> p a n", p=128),
                    hmid[:, :].rearrange("p (a n) -> p a n", a=16))
                tdma = dma("gpsimd", taps["tap_crep"],
                           crep_e[:, slot_c * HALF:(slot_c + 1) * HALF])
                wait("scalar", "s_gp", tdma)
                wait("tensor", "s_gp", tdma)
            wait("tensor", "s_v", t_sc)
            t_w2b = ld_w2(1, gk=moe_state["w1_grp"][15])
            t_g = None
            for f in range(8):
                w2war = t_g if t_g is not None else moe_state.get("w2_v_last")
                if w2war is not None:
                    wait("tensor", "s_v", w2war)
                wbuf_2 = pA_b if f < 4 else pB_b
                if f == 0:
                    wait("tensor", "s_gp", t_w2a)
                if f == 4:
                    wait("tensor", "s_gp", t_w2b)
                for nch in range(2):
                    for k in range(16):
                        t_pe = top(lambda g, k=k, nch=nch, f=f, wbuf_2=wbuf_2:
                                   g.matmul(
                                       out=ps_m2[nch][:, :],
                                       lhsT=wbuf_2[:, k * 512 + (f % 4) * 128:
                                                   k * 512 + (f % 4) * 128 + 128],
                                       rhs=hmid[:, k * HALF + nch * 512:
                                                k * HALF + (nch + 1) * 512],
                                       start=(k == 0), stop=(k == 15)),
                                   (k == 15 and nch == 1))
                moe_state["w2_grp"][f] = t_pe
                wait("vector", "s_pe", t_pe)
                for nch in range(2):
                    t_g = vop(lambda g, f=f, nch=nch: g.tensor_tensor(
                        out=gacc[:, f * HALF + nch * 512:
                                 f * HALF + (nch + 1) * 512],
                        in0=gacc[:, f * HALF + nch * 512:
                                 f * HALF + (nch + 1) * 512],
                        in1=ps_m2[nch][:, :], op=ALU.add), nch == 1)
            moe_state["w2_last"] = t_g
            moe_state["w2_v_last"] = t_g
            return t_g

        def em_moe_b2(half):
            wait("gpsimd", "s_v", P.get("s_v"))
            t_cm = dma("gpsimd", cmoe[:, :],
                       c_dram[:, half * HALF:(half + 1) * HALF])
            wait("tensor", "s_gp", t_cm)
            wait("tensor", "s_v", moe_state["w2_last"])
            t_g = None
            for f in range(8):
                if t_g is not None:
                    wait("tensor", "s_v", t_g)
                t_pe = top(lambda g, f=f: g.matmul(
                    out=ps_m2[0][:, :],
                    lhsT=b2_sb[:, f * 128:(f + 1) * 128],
                    rhs=cmoe[:, 0:512],
                    start=True, stop=True))
                t_pe = top(lambda g, f=f: g.matmul(
                    out=ps_m2[1][:, :],
                    lhsT=b2_sb[:, f * 128:(f + 1) * 128],
                    rhs=cmoe[:, 512:1024],
                    start=True, stop=True), True)
                wait("vector", "s_pe", t_pe)
                for nch in range(2):
                    t_g = vop(lambda g, f=f, nch=nch: g.tensor_tensor(
                        out=gacc[:, f * HALF + nch * 512:
                                 f * HALF + (nch + 1) * 512],
                        in0=gacc[:, f * HALF + nch * 512:
                                 f * HALF + (nch + 1) * 512],
                        in1=ps_m2[nch][:, :], op=ALU.add), nch == 1)
            if DEBUG_TAPS and half == 0:
                wait("scalar", "s_v", t_g)
                dma("scalar",
                    taps["tap_g"].rearrange("(a p) n -> p a n", p=128),
                    gacc[:, :].rearrange("p (a n) -> p a n", a=8))
            moe_state["w2_last"] = t_g
            return t_g

        yfin_last = [0] * NSTC

        dtmm_t = {}
        uslot_guard = [None, None]

        def em_scan(stc):
            phase_guard("tensor", "vector", "scalar")
            slot = stc // 4
            tcc = stc % 4
            em_poll_simple(sh_fu[slot:slot + 1, tcc:tcc + 1], tcc)
            t_ok = P.bump("s_sy", 16)
            P.em("sync", lambda g, slot=slot, tcc=tcc: g.dma_start(
                out=flagb[0:1, 1:2], in_=sh_fu[slot:slot + 1, tcc:tcc + 1]
            ).then_inc(sems["s_sy"], 16))
            pg_x = dtmm_t.get(7)
            if pg_x is not None:
                wait("sync", "s_pe", pg_x)
            t_xs = dma("sync", xdbl_sc[:, :],
                       sh_xdbl[slot, :, tcc * TC:(tcc + 1) * TC])
            u_read_t = {}

            def em_uread(d, guard):
                t = P.bump("s_sy", 16)

                def f_uread(g, d=d, slot=slot, tcc=tcc, guard=guard):
                    if guard is not None:
                        g.wait_ge(sems["s_v"], guard)
                    em_arms(g, lambda g2, p, d=d, slot=slot, tcc=tcc:
                            g2.dma_start(
                                out=u_scan[:, (d % 2) * TC:(d % 2 + 1) * TC],
                                in_=sh_u[slot, p * 1024 + d * 128:
                                         p * 1024 + (d + 1) * 128,
                                         tcc * TC:(tcc + 1) * TC]
                            ).then_inc(sems["s_sy"], 16))
                P.em("sync", f_uread)
                u_read_t[d] = t

            em_uread(0, uslot_guard[0])
            em_uread(1, uslot_guard[1])
            t_dtu_prev = None
            y_ms = {}
            t_yf = None
            for d in range(8):
                pg = dtmm_t.get(d - 2 if d >= 2 else d + 6)
                if pg is not None:
                    wait("gpsimd", "s_pe", pg)
                t_wd = dma("gpsimd", wdt_sb[:, (d % 2) * 128:(d % 2 + 1) * 128],
                           w_dt_my[:, d * 128:(d + 1) * 128])
                wait("tensor", "s_sy", max(t_xs, u_read_t[d]))
                wait("tensor", "s_gp", t_wd)
                if d >= 2:
                    wait("tensor", "s_s", spx_t[d - 2])
                t_pe = top(lambda g, d=d: g.matmul(
                    out=ps_mm[d % 2][:, :],
                    lhsT=wdt_sb[:, (d % 2) * 128:(d % 2 + 1) * 128],
                    rhs=xdbl_sc[0:64, :], start=True, stop=True), True)
                dtmm_t[d] = t_pe
                wait("scalar", "s_pe", t_pe)
                if t_dtu_prev is not None:
                    wait("scalar", "s_v", t_dtu_prev)
                if d == 0:
                    spx_t = {}
                spx_t[d] = sop(lambda g, d=d: g.activation(
                    out=sp_sb[:, :], in_=ps_mm[d % 2][:, :], func=AF.Exp,
                    bias=bdt_sb[:, d:d + 1]), True)
                wait("scalar", "s_s", spx_t[d])
                t_dt = sop(lambda g: g.activation(
                    out=dt_sb[:, :], in_=sp_sb[:, :], func=AF.Ln, bias=fc_sb[:, 1:2]), True)
                if DEBUG_TAPS and stc == 0 and d == 0:
                    wait("scalar", "s_s", t_dt)
                    tdma = dma("scalar", taps["tap_dt"], dt_sb[:, :])
                    wait("scalar", "s_sc", tdma)
                    wait("vector", "s_sc", tdma)
                wait("vector", "s_s", t_dt)
                t_dtu_prev = vop(lambda g, d=d: g.tensor_tensor(
                    out=dtu_sb[:, :], in0=dt_sb[:, :],
                    in1=u_scan[:, (d % 2) * TC:(d % 2 + 1) * TC],
                    op=ALU.mult), True)
                wait("vector", "s_v", t_dtu_prev)
                for n in range(DS):
                    rs = n % 2
                    rb = n % 4
                    ci = d * 16 + n
                    g_war = None
                    if n >= 2:
                        g_war = y_ms[(d, n - 2)]
                    elif d > 0:
                        g_war = y_ms[(d - 1, DS - 2 + n)]
                    if g_war is not None:
                        wait("scalar", "s_v", g_war)
                    g_war4 = None
                    if n >= 4:
                        g_war4 = y_ms[(d, n - 4)]
                    elif d > 0:
                        g_war4 = y_ms[(d - 1, DS - 4 + n)]
                    if g_war4 is not None:
                        wait("gpsimd", "s_v", g_war4)
                    t_dA = sop(lambda g, rs=rs, ci=ci: g.activation(
                        out=dA_sb[:, rs * TC:(rs + 1) * TC], in_=dt_sb[:, :],
                        func=AF.Exp, scale=asb[:, ci:ci + 1]), True)
                    wait("gpsimd", "s_sy", t_ok)
                    tb = dma("gpsimd", brep[:, rb * TC:(rb + 1) * TC],
                             sh_xdbl[slot, 64 + n:65 + n, tcc * TC:(tcc + 1) * TC]
                             .partition_broadcast(128))
                    tcr = dma("gpsimd", crep[:, rb * TC:(rb + 1) * TC],
                              sh_xdbl[slot, 80 + n:81 + n, tcc * TC:(tcc + 1) * TC]
                              .partition_broadcast(128))
                    wait("vector", "s_gp", tcr)
                    wait("vector", "s_s", t_dA)
                    t_db = vop(lambda g, rs=rs, rb=rb: g.tensor_tensor(
                        out=dBu_sb[:, rs * TC:(rs + 1) * TC], in0=dtu_sb[:, :],
                        in1=brep[:, rb * TC:(rb + 1) * TC], op=ALU.mult), True)
                    wait("vector", "s_v", t_db)
                    t_sc2 = vop(lambda g, rs=rs, ci=ci: g.tensor_tensor_scan(
                        out=H_sb[:, rs * TC:(rs + 1) * TC],
                        data0=dA_sb[:, rs * TC:(rs + 1) * TC],
                        data1=dBu_sb[:, rs * TC:(rs + 1) * TC],
                        initial=carry[:, ci:ci + 1],
                        op0=ALU.mult, op1=ALU.add), True)
                    wait("vector", "s_v", t_sc2)
                    if n == 0:
                        t_y0 = vop(lambda g, rs=rs, rb=rb: g.tensor_tensor(
                            out=yacc_sb[:, :],
                            in0=H_sb[:, rs * TC:(rs + 1) * TC],
                            in1=crep[:, rb * TC:(rb + 1) * TC],
                            op=ALU.mult), True)
                    else:
                        t_y0 = vop(lambda g, rs=rs, rb=rb: g.tensor_tensor(
                            out=dBu_sb[:, rs * TC:(rs + 1) * TC],
                            in0=H_sb[:, rs * TC:(rs + 1) * TC],
                            in1=crep[:, rb * TC:(rb + 1) * TC], op=ALU.mult),
                            True)
                    wait("vector", "s_v", t_y0)
                    vop(lambda g, rs=rs, ci=ci: g.tensor_copy(
                        carry[:, ci:ci + 1],
                        H_sb[:, rs * TC + TC - 1: rs * TC + TC]))
                    if n == 0:
                        t_y = t_y0
                    else:
                        t_y = vop(lambda g, rs=rs: g.tensor_tensor(
                            out=yacc_sb[:, :], in0=yacc_sb[:, :],
                            in1=dBu_sb[:, rs * TC:(rs + 1) * TC],
                            op=ALU.add), True)
                        wait("vector", "s_v", t_y)
                    y_ms[(d, n)] = t_y
                if d >= 2:
                    wait("vector", "s_sy", ywr_t[d - 2])
                if d == 0:
                    ywr_t = {}
                wait("vector", "s_v", y_ms[(d, DS - 1)])
                t_yf = vop(lambda g, d=d: g.scalar_tensor_tensor(
                    out=yfin_sb[:, (d % 2) * TC:(d % 2 + 1) * TC],
                    in0=u_scan[:, (d % 2) * TC:(d % 2 + 1) * TC],
                    scalar=dco_sb[:, d:d + 1], in1=yacc_sb[:, :],
                    op0=ALU.mult, op1=ALU.add), True)
                if DEBUG_TAPS and stc == 0 and d == 0:
                    wait("scalar", "s_v", t_yf)
                    tdma = dma("scalar", taps["tap_y"], yfin_sb[:, 0:TC])
                    wait("vector", "s_sc", tdma)
                ywr_t[d] = P.bump("s_sy", 16)

                def f_ywrite(g, d=d, stc=stc, t_yf=t_yf):
                    g.wait_ge(sems["s_v"], t_yf)
                    em_arms(g, lambda g2, p, d=d, stc=stc: g2.dma_start(
                        out=sh_y[p, d * 128:(d + 1) * 128,
                                 stc * TC:(stc + 1) * TC],
                        in_=yfin_sb[:, (d % 2) * TC:(d % 2 + 1) * TC]
                    ).then_inc(sems["s_sy"], 16))
                P.em("sync", f_ywrite)
                if d + 2 < 8:
                    em_uread(d + 2, t_yf)
                else:
                    uslot_guard[d % 2] = t_yf
            yfin_last[stc] = t_yf
            t_ally = P.get("s_sy")
            P.bump("s_sy", 16)

            def f_yflag(g, stc=stc, t_ally=t_ally):
                g.wait_ge(sems["s_sy"], t_ally)
                em_arms(g, lambda g2, p, stc=stc: g2.dma_start(
                    out=sh_fy[p:p + 1, stc:stc + 1],
                    in_=non_sb[0:1, stc + 4:stc + 5]).then_inc(sems["s_sy"], 16))
            P.em("sync", f_yflag)

        p3_state = {}

        def em_p3(tc):
            phase_guard("tensor", "vector", "scalar")
            wait("sync", "s_pe", P.get("s_pe"))
            wait("sync", "s_s", P.get("s_s"))
            wait("sync", "s_v", P.get("s_v"))

            def f_poll(g, tc=tc):
                def body(g2, p, tc=tc):
                    stc_own = 4 * p + tc
                    for cs in range(2):
                        _spin(g2, sh_fy[cs:cs + 1, stc_own:stc_own + 1],
                              stc_own + 4)
                em_arms(g, body)
            P.em("sync", f_poll)
            t_okp = P.bump("s_sy", 16)
            P.em("sync", lambda g: g.dma_start(
                out=flagb[0:1, 1:2], in_=sh_fy[0:1, 0:1]
            ).then_inc(sems["s_sy"], 16))

            wait("gpsimd", "s_v", P.get("s_v"))
            wait("gpsimd", "s_pe", P.get("s_pe"))
            wait("gpsimd", "s_s", P.get("s_s"))
            t_xn = dma("gpsimd",
                       xn_fm_tc[:, :].rearrange("p (a n) -> p a n", a=8),
                       xn_dram[:, tc * TC:(tc + 1) * TC]
                       .rearrange("(a p) n -> p a n", p=128))
            wait("vector", "s_gp", t_xn)
            t_h = em_h_fm(tc)
            wait("tensor", "s_s", t_h)

            t_gt = {}
            t_z = None
            zkt_grp = {}

            def ld_z(c, extra_s=None, gk=None):
                buf = wcA_b if c % 2 == 0 else wcB_b
                if extra_s is not None:
                    wait("gpsimd", "s_s", extra_s)
                if gk is not None:
                    wait("gpsimd", "s_pe", gk)
                return dma("gpsimd",
                           buf.rearrange("p (a n) -> p a n", a=8),
                           w_in[:, DI + c * 512:DI + (c + 1) * 512]
                           .rearrange("(a p) n -> p a n", p=128))

            t_zc = {0: ld_z(0), 1: ld_z(1, extra_s=t_h)}
            for kt in range(16):
                base_y = P.bump("s_sy", 16)

                def f_yread(g, tc=tc, kt=kt,
                            guard=(t_gt.get(kt - 2) if kt >= 2 else None)):
                    if guard is not None:
                        g.wait_ge(sems["s_v"], guard)
                    em_arms(g, lambda g2, p, tc=tc, kt=kt: g2.dma_start(
                        out=yfull[:, (kt % 2) * TC:(kt % 2 + 1) * TC],
                        in_=sh_y[kt // 8, (kt % 8) * 128:(kt % 8 + 1) * 128,
                                 p * TOK + tc * TC: p * TOK + (tc + 1) * TC]
                    ).then_inc(sems["s_sy"], 16))
                P.em("sync", f_yread)
                cz = kt // 4
                if kt == 4:
                    t_zc[2] = ld_z(2, gk=zkt_grp[3])
                if kt == 8:
                    t_zc[3] = ld_z(3, gk=zkt_grp[7])
                zbuf = wcA_b if cz % 2 == 0 else wcB_b
                if kt % 4 == 0:
                    wait("tensor", "s_gp", t_zc[cz])
                if t_z is not None:
                    wait("tensor", "s_s", t_z)
                t_pe = None
                for k in range(8):
                    t_pe = top(lambda g, k=k, kt=kt, zbuf=zbuf: g.matmul(
                        out=ps_mm[kt % 2][:, :],
                        lhsT=zbuf[:, k * 512 + (kt % 4) * 128:
                                  k * 512 + (kt % 4) * 128 + 128],
                        rhs=h_fm_tc[:, k * TC:(k + 1) * TC],
                        start=(k == 0), stop=(k == 7)), k == 7)
                zkt_grp[kt] = t_pe
                if kt % 2 == 0:
                    p3_state["z_last"] = t_pe
                else:
                    p3_state["z_last2"] = t_pe
                wait("scalar", "s_pe", t_pe)
                if kt >= 2:
                    wait("scalar", "s_v", t_gt[kt - 2])
                t_z = sop(lambda g, kt=kt: g.activation(
                    out=zsil[:, (kt % 2) * TC:(kt % 2 + 1) * TC],
                    in_=ps_mm[kt % 2][:, :], func=AF.Silu), True)
                if DEBUG_TAPS and tc == 0 and kt == 0:
                    wait("scalar", "s_s", t_z)
                    dma("scalar", taps["tap_z"], zsil[:, 0:TC])
                    wait("scalar", "s_sy", base_y)
                    tdma = dma("scalar", taps["tap_yfull"], yfull[:, 0:TC])
                    wait("scalar", "s_sc", tdma)
                    wait("sync", "s_sc", tdma)
                wait("vector", "s_s", t_z)
                wait("vector", "s_sy", base_y)
                t_gt[kt] = vop(lambda g, kt=kt: g.tensor_tensor(
                    out=ygate[:, kt * TC:(kt + 1) * TC],
                    in0=yfull[:, (kt % 2) * TC:(kt % 2 + 1) * TC],
                    in1=zsil[:, (kt % 2) * TC:(kt % 2 + 1) * TC],
                    op=ALU.mult), True)
                if DEBUG_TAPS and tc == 0 and kt == 0:
                    wait("gpsimd", "s_v", t_gt[0])
                    dma("gpsimd", taps["tap_yg"], ygate[:, 0:TC])
            wait("tensor", "s_v", t_gt[15])
            mo_ev = {}
            t_mo = None
            wo_grp = {}

            def ld_wo(c, gk=None):
                buf = wcA_b if c % 2 == 0 else wcB_b
                if gk is not None:
                    wait("gpsimd", "s_pe", gk)
                t = 0
                for h2 in range(2):
                    t = dma("gpsimd",
                            buf[:, h2 * 2048:(h2 + 1) * 2048]
                            .rearrange("p (a n) -> p a n", a=8),
                            w_out[h2 * 1024:(h2 + 1) * 1024,
                                  c * 256:(c + 1) * 256]
                            .rearrange("(a p) n -> p a n", p=128))
                return t

            t_wo = {0: ld_wo(0, gk=zkt_grp[11]), 1: ld_wo(1, gk=zkt_grp[15])}
            for f in range(8):
                if f >= 2:
                    wait("tensor", "s_v", mo_ev[f - 2])
                cwo = f // 2
                if f == 2:
                    t_wo[2] = ld_wo(2, gk=wo_grp[1])
                if f == 4:
                    t_wo[3] = ld_wo(3, gk=wo_grp[3])
                wobuf = wcA_b if cwo % 2 == 0 else wcB_b
                if f % 2 == 0:
                    wait("tensor", "s_gp", t_wo[cwo])
                for k in range(16):
                    t_pe = top(lambda g, f=f, k=k, wobuf=wobuf: g.matmul(
                        out=ps_mm[f % 2][:, :],
                        lhsT=wobuf[:, k * 256 + (f % 2) * 128:
                                   k * 256 + (f % 2) * 128 + 128],
                        rhs=ygate[:, k * TC:(k + 1) * TC],
                        start=(k == 0), stop=(k == 15)), k == 15)
                wo_grp[f] = t_pe
                if f % 2 == 0:
                    p3_state["wo_last"] = t_pe
                else:
                    p3_state["wo_last2"] = t_pe
                wait("vector", "s_pe", t_pe)
                t_mo = vop(lambda g, f=f: g.tensor_copy(
                    mo_fm[:, f * TC:(f + 1) * TC], ps_mm[f % 2][:, :]), True)
                mo_ev[f] = t_mo
            if DEBUG_TAPS and tc == 0:
                wait("scalar", "s_v", t_mo)
                dma("scalar",
                    taps["tap_mo"].rearrange("(a p) n -> p a n", p=128),
                    mo_fm[:, :].rearrange("p (a n) -> p a n", a=8))
            return t_mo

        out_last = [0]
        fin_state = {}

        def em_final(tc, t_mo, half):
            phase_guard("tensor", "vector", "scalar")
            for jj in range(4):
                j = tc * 4 + jj
                if fin_state.get("t_o") is not None:
                    wait("sync", "s_v", fin_state["t_o"])
                t_x2 = dma("sync", x2_tm[:, :], xrows[j * 128:(j + 1) * 128, :])
                wait("tensor", "s_v", max(t_mo, gacc_ready[half]))
                t_ev = None
                for i in range(8):
                    t_pe = top(lambda g, i=i, jj=jj: g.transpose(
                        out=ps_t[:, 0:128],
                        in_=mo_fm[:, i * TC + jj * 128: i * TC + (jj + 1) * 128],
                        identity=idf_sb[:, :]), True)
                    wait("vector", "s_pe", t_pe)
                    t_ev = vop(lambda g, i=i: g.tensor_copy(
                        mo_tm[:, i * 128:(i + 1) * 128], ps_t[:, 0:128]), True)
                    wait("tensor", "s_v", t_ev)
                    gcol = (tc % 2) * TC + jj * 128
                    t_pe = top(lambda g, i=i, gcol=gcol: g.transpose(
                        out=ps_t[:, 0:128],
                        in_=gacc[:, i * HALF + gcol: i * HALF + gcol + 128],
                        identity=idf_sb[:, :]), True)
                    wait("vector", "s_pe", t_pe)
                    t_ev = vop(lambda g, i=i: g.tensor_copy(
                        g_tm[:, i * 128:(i + 1) * 128], ps_t[:, 0:128]), True)
                    wait("tensor", "s_v", t_ev)
                wait("scalar", "s_v", t_ev)
                t_gs = sop(lambda g: g.activation(out=g_tm, in_=g_tm,
                                                  func=AF.Silu), True)
                wait("vector", "s_s", t_gs)
                wait("vector", "s_sy", t_x2)
                if out_last[0]:
                    wait("vector", "s_sy", out_last[0])
                tf0 = vop(lambda g, j=j: g.tensor_scalar(
                    out=s_tm[:, :], in0=x2_tm[:, :],
                    scalar1=col(C_MU1, j), scalar2=col(C_RS1, j),
                    op0=ALU.subtract, op1=ALU.mult), True)
                wait("vector", "s_v", tf0)
                tf1 = vop(lambda g: g.tensor_tensor(
                    out=s_tm[:, :], in0=s_tm[:, :], in1=mo_tm, op=ALU.add), True)
                wait("vector", "s_v", tf1)
                tf2 = vop(lambda g: g.tensor_tensor(
                    out=s_tm[:, :], in0=s_tm[:, :], in1=g_tm, op=ALU.mult), True)
                wait("vector", "s_v", tf2)
                t_o = vop(lambda g: g.tensor_tensor(
                    out=s_tm[:, :], in0=s_tm[:, :], in1=x2_tm[:, :],
                    op=ALU.add), True)
                fin_state["t_o"] = t_o
                wait("sync", "s_v", t_o)
                out_last[0] = dma("sync", yout[j * 128:(j + 1) * 128, :],
                                  s_tm[:, :])

        em_xnh_load(0)
        for k in range(4):
            em_scan(k)
            for ee in range(4 * k, 4 * k + 4):
                em_moe_ehalf(0, ee, first_in_half=(ee == 0))
            if k == 3:
                gacc_ready[0] = em_moe_b2(0)
        em_scan(4)
        t_mo0 = em_p3(0)
        em_final(0, t_mo0, 0)
        em_scan(5)
        t_mo1 = em_p3(1)
        em_final(1, t_mo1, 0)
        em_xnh_load(1)
        for ee in range(0, 5):
            em_moe_ehalf(1, ee, first_in_half=(ee == 0))
        em_scan(6)
        for ee in range(5, 11):
            em_moe_ehalf(1, ee, first_in_half=False)
        em_scan(7)
        for ee in range(11, 16):
            em_moe_ehalf(1, ee, first_in_half=False)
        gacc_ready[1] = em_moe_b2(1)
        t_mo2 = em_p3(2)
        em_final(2, t_mo2, 1)
        t_mo3 = em_p3(3)
        em_final(3, t_mo3, 1)

        fins = {s_: P.get(s_) for s_ in ("s_sy", "s_gp", "s_sc", "s_vq")}
        P.em("sync", lambda g: g.wait_ge(sems["s_sy"], fins["s_sy"]))
        P.em("gpsimd", lambda g: g.wait_ge(sems["s_gp"], fins["s_gp"]))
        P.em("scalar", lambda g: g.wait_ge(sems["s_sc"], fins["s_sc"]))
        P.em("vector", lambda g: g.wait_ge(sems["s_vq"], fins["s_vq"]))

        with nc.Block() as block:
            @block.sync
            def _(g):
                for fn in P.p["sync"]:
                    fn(g)

            @block.gpsimd
            def _(g):
                for fn in P.p["gpsimd"]:
                    fn(g)

            @block.scalar
            def _(g):
                for fn in P.p["scalar"]:
                    fn(g)

            @block.vector
            def _(g):
                for fn in P.p["vector"]:
                    fn(g)

            @block.tensor
            def _(g):
                for fn in P.p["tensor"]:
                    fn(g)

    return nc


def _cols(vec, ntiles):
    return np.ascontiguousarray(vec.reshape(ntiles, 128).T).astype(np.float32)


def kernel(**inputs):
    x = np.asarray(inputs["L0_attn_x"], np.float32)
    ln_g = np.asarray(inputs["L0_attn_ln_g"], np.float32)
    ln_b = np.asarray(inputs["L0_attn_ln_b"], np.float32)
    W_in = np.asarray(inputs["L0_attn_W_in"], np.float32)
    conv_w = np.asarray(inputs["L0_attn_conv_w"], np.float32)
    conv_b = np.asarray(inputs["L0_attn_conv_b"], np.float32)
    W_xproj = np.asarray(inputs["L0_attn_W_xproj"], np.float32)
    W_dt = np.asarray(inputs["L0_attn_W_dt"], np.float32)
    b_dt = np.asarray(inputs["L0_attn_b_dt"], np.float32)
    A_log = np.asarray(inputs["L0_attn_A_log"], np.float32)
    D = np.asarray(inputs["L0_attn_D"], np.float32)
    W_out = np.asarray(inputs["L0_attn_W_out"], np.float32)
    rw = np.asarray(inputs["L0_ffn_router_w"], np.float32)
    rb = np.asarray(inputs["L0_ffn_router_b"], np.float32)
    w1 = np.asarray(inputs["L0_ffn_w1"], np.float32)
    b1 = np.asarray(inputs["L0_ffn_b1"], np.float32)
    w2 = np.asarray(inputs["L0_ffn_w2"], np.float32)
    b2 = np.asarray(inputs["L0_ffn_b2"], np.float32)

    nonce = int(time.time_ns()) & 0x7FFF0000
    nonces = ((np.arange(1, 17, dtype=np.int64) + nonce) &
              0xFFFFFFFF).astype(np.uint32).reshape(1, 16)

    identb = np.eye(128, dtype=ml_dtypes.bfloat16)
    identf = np.eye(128, dtype=np.float32)
    cw = conv_w.reshape(DI, 4)
    convw_cols = np.zeros((128, 64), np.float32)
    for kt in range(16):
        convw_cols[:, kt * 4:(kt + 1) * 4] = cw[kt * 128:(kt + 1) * 128, :]
    convb_cols = _cols(conv_b, 16)
    lng_cols = _cols(ln_g, 8)
    lnb_cols = _cols(ln_b, 8)
    b1_cols = np.zeros((128, NE * 16), np.float32)
    for ee in range(NE):
        b1_cols[:, ee * 16:(ee + 1) * 16] = _cols(b1[ee], 16)

    in_maps = []
    for c in range(8):
        b, h = c >> 1, c & 1
        lo, hi = h * 1024, (h + 1) * 1024
        xr = np.ascontiguousarray(x[b, h * TOK:(h + 1) * TOK, :])
        xb = np.zeros((4, DM), np.float32)
        if h == 1:
            xb[0:3] = x[b, h * TOK - 3:h * TOK, :]
            m1 = np.ones((128, 1), np.float32)
        else:
            m1 = np.zeros((128, 1), np.float32)
        im = dict(
            xrows=xr, xbnd=xb, mask1=m1,
            parity=np.array([[h]], np.uint32), nonces=nonces,
            w_in=W_in, convw_cols=convw_cols, convb_cols=convb_cols,
            w_xproj=W_xproj,
            w_dt_my=np.ascontiguousarray(W_dt[:, lo:hi]),
            bdt_cols=_cols(b_dt[lo:hi], 8),
            alog_cols=np.ascontiguousarray(
                A_log[lo:hi].reshape(8, 128, 16).transpose(1, 0, 2)
                .reshape(128, 128)),
            d_cols=_cols(D[lo:hi], 8),
            w_out=W_out, lng_cols=lng_cols, lnb_cols=lnb_cols,
            router_w=rw, rb_col=rb.reshape(NE, 1),
            w1=w1, b1_cols=b1_cols, w2=w2, b2_st=b2,
            identb=identb, identf=identf,
            fconst=np.broadcast_to(
                np.array([EPS, 1.0, -0.5, 0.0], np.float32), (128, 4)).copy(),
        )
        in_maps.append(im)

    nc = build_program()
    res = run_bass_kernel_spmd(nc, in_maps, core_ids=list(range(8)))
    out = np.empty((B, L, DM), np.float32)
    for c in range(8):
        b, h = c >> 1, c & 1
        out[b, h * TOK:(h + 1) * TOK, :] = res.results[c]["yout"]
    return out


# version 07
# speedup vs baseline: 1.1991x; 1.1986x over previous
import time
from contextlib import ExitStack

import numpy as np
import ml_dtypes

import concourse.bass as bass
import concourse.mybir as mybir
from concourse.bass_utils import run_bass_kernel_spmd

F32 = mybir.dt.float32
BF16 = mybir.dt.bfloat16
U32 = mybir.dt.uint32
AF = mybir.ActivationFunctionType
ALU = mybir.AluOpType
AX = mybir.AxisListType.X

B, L, DM = 4, 4096, 1024
DI, DS, DTR = 2048, 16, 64
TOK = 2048
TC = 512
NTC = 4
NSTC = 8
NE = 16
HALF = 1024
EPS = 1e-5

DEBUG_TAPS = False


class Plan:
    def __init__(self):
        self.p = {k: [] for k in ("sync", "scalar", "vector", "tensor", "gpsimd")}
        self.cnt = {}

    def em(self, eng, fn):
        self.p[eng].append(fn)

    def bump(self, sem_name, v):
        self.cnt[sem_name] = self.cnt.get(sem_name, 0) + v
        return self.cnt[sem_name]

    def get(self, sem_name):
        return self.cnt.get(sem_name, 0)


def build_program():
    nc = bass.Bass("TRN2", target_bir_lowering=False)
    P = Plan()

    def inp(name, shape, dt=F32):
        return nc.dram_tensor(name, shape, dt, kind="ExternalInput").ap()

    xrows = inp("xrows", [TOK, DM])
    xbnd = inp("xbnd", [4, DM])
    mask1 = inp("mask1", [128, 1])
    parity = inp("parity", [1, 1], U32)
    nonces = inp("nonces", [1, 16], U32)
    w_in = inp("w_in", [DM, 2 * DI])
    convw_cols = inp("convw_cols", [128, 64])
    convb_cols = inp("convb_cols", [128, 16])
    w_xproj = inp("w_xproj", [DI, 96])
    w_dt_my = inp("w_dt_my", [DTR, 1024])
    bdt_cols = inp("bdt_cols", [128, 8])
    alog_cols = inp("alog_cols", [128, 128])
    d_cols = inp("d_cols", [128, 8])
    w_out = inp("w_out", [DI, DM])
    lng_cols = inp("lng_cols", [128, 8])
    lnb_cols = inp("lnb_cols", [128, 8])
    router_w = inp("router_w", [DM, NE])
    rb_col = inp("rb_col", [NE, 1])
    w1 = inp("w1", [NE, DM, DI])
    b1_cols = inp("b1_cols", [128, NE * 16])
    w2 = inp("w2", [NE, DI, DM])
    b2_st = inp("b2_st", [NE, DM])
    identb = inp("identb", [128, 128], BF16)
    fconst = inp("fconst", [128, 4])
    identf = inp("identf", [128, 128])

    yout = nc.dram_tensor("yout", [TOK, DM], F32, kind="ExternalOutput").ap()
    taps = {}
    if DEBUG_TAPS:
        for nm, shp in [("tap_u", [DI, TC]), ("tap_xdbl", [96, TC]),
                        ("tap_dt", [128, TC]),
                        ("tap_c", [NE, TOK]), ("tap_g", [DM, HALF]),
                        ("tap_xn", [DM, TOK]), ("tap_st2", [2, TOK]),
                        ("tap_hmid", [DI, HALF]), ("tap_crep", [128, HALF]),
                        ("tap_z", [128, TC]), ("tap_yg", [128, TC]),
                        ("tap_yfull", [128, TC]),
                        ("tap_h", [DM, TC]), ("tap_lg", [NE, TC]),
                        ("tap_stats", [128, 128]),
                        ("tap_mo", [DM, TC]), ("tap_y", [128, TC])]:
            taps[nm] = nc.dram_tensor(nm, shp, F32, kind="ExternalOutput").ap()

    xn_dram = nc.dram_tensor("xn_dram", [DM, TOK], BF16).ap()
    c_dram = nc.dram_tensor("c_dram", [NE, TOK], F32).ap()
    st2_dram = nc.dram_tensor("st2_dram", [2, TOK], F32).ap()
    sh_u = nc.dram_tensor("sh_u", [2, DI, TOK], F32, addr_space="Shared").ap()
    sh_xdbl = nc.dram_tensor("sh_xdbl", [2, 96, TOK], F32, addr_space="Shared").ap()
    sh_y = nc.dram_tensor("sh_y", [2, 1024, L], F32, addr_space="Shared").ap()
    sh_fu = nc.dram_tensor("sh_fu", [2, 4], U32, addr_space="Shared").ap()
    sh_fy = nc.dram_tensor("sh_fy", [2, 8], U32, addr_space="Shared").ap()

    with ExitStack() as ctx:
        e = ctx.enter_context

        def sb(name, shape, dt=F32):
            return e(nc.sbuf_tensor(name, shape, dt))

        pid_sb = sb("pid_sb", [1, 1], U32)
        non_sb = sb("non_sb", [1, 16], U32)
        mask_sb = sb("mask_sb", [128, 1])
        fc_sb = sb("fc_sb", [128, 4])
        idb_sb = sb("idb_sb", [128, 128], BF16)
        idf_sb = sb("idf_sb", [128, 128])
        wxp_sb = sb("wxp_sb", [128, 2 * 96])
        wdt_sb = sb("wdt_sb", [DTR, 2 * 128])
        cw_sb = sb("cw_sb", [128, 64])
        cb_sb = sb("cb_sb", [128, 16])
        bdt_sb = sb("bdt_sb", [128, 8])
        asb = sb("asb", [128, 128])
        dco_sb = sb("dco_sb", [128, 8])
        lng_sb = sb("lng_sb", [128, 8])
        lnb_sb = sb("lnb_sb", [128, 8])
        rw_sb = sb("rw_sb", [128, 8 * 16])
        rwhi = sb("rwhi", [128, 8 * 16], BF16)
        rwlo = sb("rwlo", [128, 8 * 16], BF16)
        rb_sb = sb("rb_sb", [NE, 1])
        b1_sb = sb("b1_sb", [128, NE * 16])
        b2_sb = sb("b2_sb", [NE, DM])
        stats = sb("stats", [128, 16 * 8])
        carry = sb("carry", [128, 128])
        uprev3 = sb("uprev3", [128, 48])
        flagb = sb("flagb", [1, 2], U32)

        arenaA = sb("arenaA", [128, 11392])
        arA = arenaA[:, :]

        def fview(a, b):
            return arA[:, a:b]

        def bview(a, b):
            return arA[:, a:b].bitcast(BF16)

        x_tm1 = fview(0, 1024)
        xn_tm = fview(1024, 2048)
        xnb_tm = bview(2048, 2560)
        xnl_tm = bview(2560, 3072)
        xnl_fm_tc = bview(3072, 5120)
        uraw = fview(5120, 6150)
        uconv = fview(6150, 7174)
        u_dt = fview(7174, 8710)
        xdbl_tc = fview(8710, 9222)[0:96, :]
        logit_tc = fview(9222, 9734)[0:16, :]
        xb_tm = fview(9734, 10758)[0:4, :]
        hb_stage = bview(10758, 11270)[0:4, :]
        xnh = bview(0, 4096)
        ygate = bview(4096, 8192)

        scr = sb("scr", [128, TC])
        bigA = sb("bigA", [128, 8192])
        xn_fm_tc = bigA[:, 0:2048].bitcast(BF16)
        h_fm_tc = bigA[:, 2048:4096].bitcast(BF16)
        wcA_b = bigA[:, 4096:6144].bitcast(BF16)
        wcB_b = bigA[:, 6144:8192].bitcast(BF16)
        scr2 = bigA[:, 4096:5120]
        mu2rep = bigA[:, 6144:6656]
        rstd2rep = bigA[:, 6656:7168]
        pA_b = bigA[:, 0:4096].bitcast(BF16)
        pB_b = bigA[:, 4096:8192].bitcast(BF16)
        hb_fm = sb("hb_fm", [128, 8 * 4], BF16)
        ctr_st = sb("ctr_st", [NE, 128])
        ltm = sb("ltm", [128, NE])
        mx8 = sb("mx8", [128, 8])
        expw = sb("expw", [128, NE])
        maskw = sb("maskw", [128, NE])
        ctm = sb("ctm", [128, NE])
        tcol = sb("tcol", [128, 4])

        u_scan = sb("u_scan", [128, 2 * TC])
        xdbl_sc = sb("xdbl_sc", [96, TC])
        brep = sb("brep", [128, 3 * TC])
        crep = sb("crep", [128, 3 * TC])
        dt_sb = sb("dt_sb", [128, TC])
        sp_sb = sb("sp_sb", [128, TC])
        dtu_sb = sb("dtu_sb", [128, TC])
        dA_sb = sb("dA_sb", [128, 4 * TC])
        dBu_sb = sb("dBu_sb", [128, 2 * TC])
        H_sb = sb("H_sb", [128, 2 * TC])
        yacc_sb = sb("yacc_sb", [128, TC])
        yfin_sb = sb("yfin_sb", [128, 2 * TC])

        hmid_t = sb("hmid_t", [128, 16 * HALF], BF16)
        hmid = hmid_t[:, :]
        hm_f32 = hmid_t[:, :].bitcast(F32)
        mo_fm = hm_f32[:, 0:4096]
        zsil = hm_f32[:, 4096:5120]
        yfull = hm_f32[:, 5120:6144]
        mo_tm = hm_f32[:, 6144:7168]
        g_tm = hm_f32[:, 7168:8192]
        gacc = sb("gacc", [128, 8 * HALF])
        crep_e = sb("crep_e", [128, 2 * HALF], BF16)
        cmoe = crep_e[:, :].bitcast(F32)[0:16, 0:HALF]

        x2_tm = fview(0, 1024)
        s_tm = fview(1024, 2048)

        def ps(name, shape, dt=F32):
            return e(nc.psum_tensor(name, shape, dt))

        ps_m1 = [ps("ps_m1a", [128, 512]), ps("ps_m1b", [128, 512])]
        ps_m2 = [ps("ps_m2a", [128, 512]), ps("ps_m2b", [128, 512])]
        ps_mm = [ps("ps_mma", [128, 512]), ps("ps_mmb", [128, 512])]
        ps_x = ps("ps_x", [96, 512])
        ps_misc = ps("ps_misc", [128, 512])
        ps_t = ps_misc[:, 0:144]
        ps_tb = ps_misc[:, 144:208].bitcast(BF16)
        ps_bnd = ps_misc[:, 208:212]

        sems = {}
        for s_ in ("s_sy", "s_gp", "s_sc", "s_vq", "s_pe", "s_v", "s_s", "s_pl"):
            sems[s_] = e(nc.semaphore(s_))

        def dma(q, out_ap, in_ap, **kw):
            sem = {"sync": "s_sy", "gpsimd": "s_gp",
                   "scalar": "s_sc", "vector": "s_vq"}[q]
            tgt = P.bump(sem, 16)
            s_h = sems[sem]
            P.em(q, lambda g, o=out_ap, i=in_ap, s=s_h, kw=dict(kw):
                 g.dma_start(out=o, in_=i, **kw).then_inc(s, 16))
            return tgt

        def wait(eng, sem, tgt):
            if tgt is None or tgt <= 0:
                return
            s_h = sems[sem]
            P.em(eng, lambda g, s=s_h, t=tgt: g.wait_ge(s, t))

        def vop(fn, mark=False):
            if not mark:
                P.em("vector", fn)
                return None
            tgt = P.bump("s_v", 1)
            P.em("vector", lambda g, f=fn, s=sems["s_v"]: f(g).then_inc(s, 1))
            return tgt

        def sop(fn, mark=False):
            if not mark:
                P.em("scalar", fn)
                return None
            tgt = P.bump("s_s", 1)
            P.em("scalar", lambda g, f=fn, s=sems["s_s"]: f(g).then_inc(s, 1))
            return tgt

        def top(fn, mark=False):
            if not mark:
                P.em("tensor", fn)
                return None
            tgt = P.bump("s_pe", 1)
            P.em("tensor", lambda g, f=fn, s=sems["s_pe"]: f(g).then_inc(s, 1))
            return tgt

        def phase_guard(*engines):
            snap = {s_: P.get(s_) for s_ in ("s_pe", "s_v", "s_s")}
            own = {"tensor": "s_pe", "vector": "s_v", "scalar": "s_s"}
            for eng in engines:
                for s_, t in snap.items():
                    if own.get(eng) != s_:
                        wait(eng, s_, t)

        rt = {}

        def em_sync_prolog(t_pid):
            def f(g, t_pid=t_pid):
                rt["pollcnt"] = g.alloc_register("pollcnt")
                rt["preg"] = g.alloc_register("preg")
                rt["treg"] = g.alloc_register("treg")
                rt["freg"] = g.alloc_register("freg")
                rt["dreg"] = g.alloc_register("dreg")
                g.reg_mov(rt["pollcnt"], 0)
                g.wait_ge(sems["s_sy"], t_pid)
                g.load(rt["preg"], pid_sb[0:1, 0:1])
            P.em("sync", f)

        class _Arms:
            pass

        def em_arms(g, body):
            with g.If_eq(rt["preg"], 0):
                body(g, 0)
            with g.Else():
                body(g, 1)

        def _spin(g, flag_ap, val_idx):
            g.load(rt["treg"], non_sb[0:1, val_idx:val_idx + 1])

            def cond():
                g.reg_add(rt["pollcnt"], rt["pollcnt"], 16)
                g.dma_start(out=flagb[0:1, 0:1], in_=flag_ap).then_inc(
                    sems["s_pl"], 16)
                g.wait_ge(sems["s_pl"], rt["pollcnt"])
                g.load(rt["freg"], flagb[0:1, 0:1])
                g.reg_sub(rt["dreg"], rt["freg"], rt["treg"])
                return rt["dreg"]
            with g.While(cond):
                g.nop()

        def em_poll_simple(flag_ap, val_idx):
            P.em("sync", lambda g, f_=flag_ap, v_=val_idx: _spin(g, f_, v_))

        t_const = 0
        for out_ap, in_ap in [
            (pid_sb[:, :], parity), (non_sb[:, :], nonces),
            (mask_sb[:, :], mask1), (fc_sb[:, :], fconst),
            (idb_sb[:, :], identb),
            (idf_sb[:, :], identf), (cw_sb[:, :], convw_cols),
            (cb_sb[:, :], convb_cols), (bdt_sb[:, :], bdt_cols),
            (asb[:, :], alog_cols), (dco_sb[:, :], d_cols),
            (lng_sb[:, :], lng_cols), (lnb_sb[:, :], lnb_cols),
            (rb_sb[:, :], rb_col), (b1_sb[:, :], b1_cols),
            (b2_sb[:, :], b2_st),
            (xb_tm[0:3, :], xbnd[0:3, :]),
        ]:
            t_const = dma("sync", out_ap, in_ap)

        for kc in range(8):
            t_const = dma("sync", rw_sb[:, kc * 16:(kc + 1) * 16],
                          router_w[kc * 128:(kc + 1) * 128, :])
        em_sync_prolog(t_const)

        wait("scalar", "s_sy", t_const)
        t_s0 = sop(lambda g: g.activation(out=asb[:, :], in_=asb[:, :],
                                          func=AF.Exp), True)
        wait("vector", "s_s", t_s0)
        wait("vector", "s_sy", t_const)
        vop(lambda g: g.tensor_scalar(out=asb[:, :], in0=asb[:, :],
                                      scalar1=-1.0, scalar2=None, op0=ALU.mult))
        vop(lambda g: g.memset(carry[:, :], 0.0))
        vop(lambda g: g.tensor_copy(rwhi[:, :], rw_sb[:, :]))
        t_rw = vop(lambda g: g.tensor_tensor(out=rwlo[:, :], in0=rw_sb[:, :],
                                             in1=rwhi[:, :], op=ALU.subtract), True)

        C_SUM1, C_MU1, C_RS1, C_SS1, C_SUM2, C_SS2, C_MU2, C_RS2 = range(8)

        def col(ci, j):
            return stats[:, ci * 16 + j: ci * 16 + j + 1]

        xfm_done = [0] * NTC
        lg_guard = [0]
        t_cd = 0
        st2_done = [0] * NTC
        ctm_done = [0] * NTC
        xnd_done = [0] * NTC
        xtm_consumed = []

        for tc in range(NTC):
            for jj in range(4):
                j = tc * 4 + jj
                xt = x_tm1
                if j >= 1:
                    wait("sync", "s_v", xtm_consumed[j - 1])
                t_x = dma("sync", xt, xrows[j * 128:(j + 1) * 128, :])
                wait("vector", "s_sy", t_x)
                wait("scalar", "s_sy", t_x)
                t1 = vop(lambda g, xt=xt, j=j: g.reduce_sum(
                    out=col(C_SUM1, j), in_=xt, axis=AX), True)
                ts1 = sop(lambda g, xt=xt, j=j: g.activation(
                    out=scr2[:, :], in_=xt, func=AF.Square,
                    accum_out=col(C_SS1, j)), True)
                wait("vector", "s_v", t1)
                t2 = vop(lambda g, j=j: g.tensor_scalar(
                    out=col(C_MU1, j), in0=col(C_SUM1, j),
                    scalar1=1.0 / DM, scalar2=None, op0=ALU.mult), True)
                wait("vector", "s_s", ts1)
                wait("vector", "s_v", t2)
                vop(lambda g, j=j: g.tensor_tensor(
                    out=tcol[:, 0:1], in0=col(C_MU1, j), in1=col(C_MU1, j),
                    op=ALU.mult))
                t3 = vop(lambda g, j=j: g.tensor_scalar(
                    out=tcol[:, 1:2], in0=col(C_SS1, j),
                    scalar1=1.0 / DM, scalar2=None, op0=ALU.mult), True)
                wait("vector", "s_v", t3)
                t4 = vop(lambda g: g.tensor_tensor(
                    out=tcol[:, 1:2], in0=tcol[:, 1:2], in1=tcol[:, 0:1],
                    op=ALU.subtract), True)
                wait("scalar", "s_v", t4)
                ts2a = sop(lambda g: g.activation(
                    out=tcol[:, 2:3], in_=tcol[:, 1:2],
                    func=AF.Ln, bias=fc_sb[:, 0:1]), True)
                wait("scalar", "s_s", ts2a)
                ts2 = sop(lambda g, j=j: g.activation(
                    out=col(C_RS1, j), in_=tcol[:, 2:3], func=AF.Exp,
                    scale=fc_sb[:, 2:3]), True)
                wait("vector", "s_s", ts2)
                t_xn = vop(lambda g, xt=xt, j=j: g.tensor_scalar(
                    out=xn_tm[:, :], in0=xt, scalar1=col(C_MU1, j),
                    scalar2=col(C_RS1, j), op0=ALU.subtract, op1=ALU.mult), True)
                vop(lambda g: g.tensor_copy(xnb_tm[:, :], xn_tm[:, :]))
                t_lo = vop(lambda g: g.tensor_tensor(
                    out=xnl_tm[:, :], in0=xn_tm[:, :], in1=xnb_tm[:, :],
                    op=ALU.subtract), True)
                xtm_consumed.append(t_lo)
                r2 = vop(lambda g, j=j: g.reduce_sum(
                    out=col(C_SUM2, j), in_=xn_tm[:, :], axis=AX), True)
                wait("vector", "s_v", r2)
                m2 = vop(lambda g, j=j: g.tensor_scalar(
                    out=col(C_MU2, j), in0=col(C_SUM2, j),
                    scalar1=1.0 / DM, scalar2=None, op0=ALU.mult), True)
                wait("scalar", "s_v", t_xn)
                ss2 = sop(lambda g, j=j: g.activation(
                    out=scr2[:, :], in_=xn_tm[:, :], func=AF.Square,
                    accum_out=col(C_SS2, j)), True)
                wait("vector", "s_s", ss2)
                wait("vector", "s_v", m2)
                vop(lambda g, j=j: g.tensor_tensor(
                    out=tcol[:, 0:1], in0=col(C_MU2, j), in1=col(C_MU2, j),
                    op=ALU.mult))
                t3b = vop(lambda g, j=j: g.tensor_scalar(
                    out=tcol[:, 1:2], in0=col(C_SS2, j),
                    scalar1=1.0 / DM, scalar2=None, op0=ALU.mult), True)
                wait("vector", "s_v", t3b)
                t4b = vop(lambda g: g.tensor_tensor(
                    out=tcol[:, 1:2], in0=tcol[:, 1:2], in1=tcol[:, 0:1],
                    op=ALU.subtract), True)
                wait("scalar", "s_v", t4b)
                ts3a = sop(lambda g: g.activation(
                    out=tcol[:, 2:3], in_=tcol[:, 1:2],
                    func=AF.Ln, bias=fc_sb[:, 0:1]), True)
                wait("scalar", "s_s", ts3a)
                t_s2 = sop(lambda g, j=j: g.activation(
                    out=col(C_RS2, j), in_=tcol[:, 2:3], func=AF.Exp,
                    scale=fc_sb[:, 2:3]), True)
                wait("gpsimd", "s_s", t_s2)
                st2_done[tc] = dma(
                    "gpsimd",
                    st2_dram[0:1, j * 128:(j + 1) * 128].rearrange("a b -> b a"),
                    col(C_MU2, j))
                st2_done[tc] = dma(
                    "gpsimd",
                    st2_dram[1:2, j * 128:(j + 1) * 128].rearrange("a b -> b a"),
                    col(C_RS2, j))
                wait("tensor", "s_v", t_lo)
                t_ev = None
                for i in range(8):
                    t_pe = top(lambda g, i=i: g.transpose(
                        out=ps_tb, in_=xnb_tm[:, i * 128:(i + 1) * 128],
                        identity=idb_sb[:, :]), True)
                    wait("vector", "s_pe", t_pe)
                    t_ev = vop(lambda g, i=i, jj=jj: g.tensor_copy(
                        xn_fm_tc[:, i * TC + jj * 128: i * TC + (jj + 1) * 128],
                        ps_tb), True)
                    wait("tensor", "s_v", t_ev)
                    t_pe = top(lambda g, i=i: g.transpose(
                        out=ps_tb, in_=xnl_tm[:, i * 128:(i + 1) * 128],
                        identity=idb_sb[:, :]), True)
                    wait("vector", "s_pe", t_pe)
                    t_ev = vop(lambda g, i=i, jj=jj: g.tensor_copy(
                        xnl_fm_tc[:, i * TC + jj * 128: i * TC + (jj + 1) * 128],
                        ps_tb), True)
                    wait("tensor", "s_v", t_ev)
            xfm_done[tc] = t_ev
            wait("scalar", "s_v", t_ev)
            for i in range(8):
                xnd_done[tc] = dma("scalar",
                                   xn_dram[i * 128:(i + 1) * 128,
                                           tc * TC:(tc + 1) * TC],
                                   xn_fm_tc[:, i * TC:(i + 1) * TC])
            wait("tensor", "s_sy", t_const)
            wait("tensor", "s_v", max(xfm_done[tc], t_rw))
            nmm = 0
            t_r = None
            for term in range(3):
                for kc in range(8):
                    st_ap = (rwhi if term < 2 else rwlo)[:, kc * 16:(kc + 1) * 16]
                    mv = (xn_fm_tc if term != 1 else xnl_fm_tc)
                    mv_ap = mv[:, kc * TC:(kc + 1) * TC]
                    last = (nmm == 23)
                    t_r = top(lambda g, s_=st_ap, m_=mv_ap, f_=(nmm == 0), l_=last:
                              g.matmul(out=ps_x[0:16, :], lhsT=s_, rhs=m_,
                                       start=f_, stop=l_), last)
                    nmm += 1
            if DEBUG_TAPS and tc == 0:
                pass
            wait("scalar", "s_pe", t_r)
            if tc > 0:
                wait("scalar", "s_pe", lg_guard[0])
            t_lg = sop(lambda g: g.activation(
                out=logit_tc[:, :], in_=ps_x[0:16, :],
                func=AF.Identity, bias=rb_sb[:, :]), True)
            if DEBUG_TAPS and tc == 0:
                wait("gpsimd", "s_s", t_lg)
                dma("gpsimd", taps["tap_lg"], logit_tc[:, :])
            for jj in range(4):
                wait("tensor", "s_s", t_lg)
                t_pe = top(lambda g, jj=jj: g.transpose(
                    out=ps_t[:, 0:16],
                    in_=logit_tc[:, jj * 128:(jj + 1) * 128],
                    identity=idf_sb[0:16, 0:16]), True)
                lg_guard[0] = t_pe
                wait("vector", "s_pe", t_pe)
                tv0 = vop(lambda g: g.tensor_copy(ltm[:, :], ps_t[:, 0:16]), True)
                wait("vector", "s_v", tv0)
                tv1 = vop(lambda g: g.max(out=mx8[:, :], in_=ltm[:, :]), True)
                wait("vector", "s_v", tv1)
                vop(lambda g: g.tensor_scalar(
                    out=expw[:, :], in0=ltm[:, :], scalar1=mx8[:, 0:1],
                    scalar2=None, op0=ALU.subtract))
                t_v = vop(lambda g: g.tensor_scalar(
                    out=maskw[:, :], in0=ltm[:, :], scalar1=mx8[:, 3:4],
                    scalar2=None, op0=ALU.is_ge), True)
                wait("scalar", "s_v", t_v)
                t_s = sop(lambda g: g.activation(out=expw[:, :], in_=expw[:, :],
                                                 func=AF.Exp), True)
                wait("vector", "s_s", t_s)
                tv2 = vop(lambda g: g.tensor_tensor(
                    out=expw[:, :], in0=expw[:, :],
                    in1=maskw[:, :], op=ALU.mult), True)
                wait("vector", "s_v", tv2)
                tv3 = vop(lambda g: g.reduce_sum(out=tcol[:, 0:1], in_=expw[:, :],
                                                 axis=AX), True)
                wait("vector", "s_v", tv3)
                tv4 = vop(lambda g: g.reciprocal(out=tcol[:, 1:2],
                                                 in_=tcol[:, 0:1]), True)
                wait("vector", "s_v", tv4)
                t_v = vop(lambda g: g.tensor_scalar(
                    out=ctm[:, :], in0=expw[:, :], scalar1=tcol[:, 1:2],
                    scalar2=None, op0=ALU.mult), True)
                wait("tensor", "s_v", t_v)
                t_pe = top(lambda g: g.transpose(
                    out=ps_misc[0:16, 16:144], in_=ctm[:, :],
                    identity=idf_sb[:, :]), True)
                wait("vector", "s_pe", t_pe)
                if tc + jj > 0:
                    wait("vector", "s_sc", t_cd)
                ctm_done[tc] = vop(lambda g: g.tensor_copy(
                    ctr_st[:, :], ps_misc[0:16, 16:144]), True)
                wait("scalar", "s_v", ctm_done[tc])
                t_cd = dma("scalar",
                           c_dram[:, tc * TC + jj * 128: tc * TC + (jj + 1) * 128],
                           ctr_st[:, :])
        if DEBUG_TAPS:
            wait("gpsimd", "s_sc", t_cd)
            dma("gpsimd", taps["tap_c"], c_dram[:, :])
            wait("gpsimd", "s_sc", max(xnd_done))
            dma("gpsimd", taps["tap_xn"], xn_dram[:, :])
            dma("gpsimd", taps["tap_st2"], st2_dram[:, :])
            wait("gpsimd", "s_v", P.get("s_v"))
            dma("gpsimd", taps["tap_stats"], stats[:, :])

        def em_ln_rows(src_ap, n_):
            tv0 = vop(lambda g: g.reduce_sum(out=tcol[0:n_, 0:1], in_=src_ap,
                                             axis=AX), True)
            wait("vector", "s_v", tv0)
            t_v = vop(lambda g: g.tensor_scalar(
                out=tcol[0:n_, 0:1], in0=tcol[0:n_, 0:1], scalar1=1.0 / DM,
                scalar2=None, op0=ALU.mult), True)
            wait("scalar", "s_v", t_v)
            t_s = sop(lambda g: g.activation(
                out=scr2[0:n_, :], in_=src_ap, func=AF.Square,
                accum_out=tcol[0:n_, 1:2]), True)
            wait("vector", "s_s", t_s)
            wait("vector", "s_v", t_v)
            tv1 = vop(lambda g: g.tensor_tensor(
                out=tcol[0:n_, 2:3], in0=tcol[0:n_, 0:1],
                in1=tcol[0:n_, 0:1], op=ALU.mult), True)
            tv2 = vop(lambda g: g.tensor_scalar(
                out=tcol[0:n_, 1:2], in0=tcol[0:n_, 1:2],
                scalar1=1.0 / DM, scalar2=None, op0=ALU.mult), True)
            wait("vector", "s_v", tv2)
            t_v = vop(lambda g: g.tensor_tensor(
                out=tcol[0:n_, 1:2], in0=tcol[0:n_, 1:2], in1=tcol[0:n_, 2:3],
                op=ALU.subtract), True)
            wait("scalar", "s_v", t_v)
            ts1 = sop(lambda g: g.activation(
                out=tcol[0:n_, 2:3], in_=tcol[0:n_, 1:2],
                func=AF.Ln, bias=fc_sb[0:n_, 0:1]), True)
            wait("scalar", "s_s", ts1)
            t_s = sop(lambda g: g.activation(out=tcol[0:n_, 3:4],
                                             in_=tcol[0:n_, 2:3], func=AF.Exp,
                                             scale=fc_sb[0:n_, 2:3]), True)
            wait("vector", "s_s", t_s)
            t_v = vop(lambda g: g.tensor_scalar(
                out=src_ap, in0=src_ap, scalar1=tcol[0:n_, 0:1],
                scalar2=tcol[0:n_, 3:4], op0=ALU.subtract, op1=ALU.mult), True)
            wait("vector", "s_v", t_v)
            return t_v

        wait("vector", "s_sy", t_const)
        em_ln_rows(xb_tm[0:3, :], 3)
        em_ln_rows(xb_tm[0:3, :], 3)
        t_hb = vop(lambda g: g.tensor_copy(hb_stage[0:3, :], xb_tm[0:3, :]), True)
        wait("tensor", "s_v", t_hb)
        for i in range(8):
            t_pe = top(lambda g, i=i: g.transpose(
                out=ps_tb[:, 0:3], in_=hb_stage[0:3, i * 128:(i + 1) * 128],
                identity=idb_sb[0:3, 0:3]), True)
            wait("scalar", "s_pe", t_pe)
            t_hb2 = sop(lambda g, i=i: g.activation(
                out=hb_fm[:, i * 4:i * 4 + 3], in_=ps_tb[:, 0:3],
                func=AF.Identity, scale=lng_sb[:, i:i + 1],
                bias=lnb_sb[:, i:i + 1]), True)
            wait("tensor", "s_s", t_hb2)

        def em_h_fm(tc):
            phase_guard("vector", "scalar")
            wait("gpsimd", "s_v", P.get("s_v"))
            t_m = dma("gpsimd", mu2rep[:, :],
                      st2_dram[0:1, tc * TC:(tc + 1) * TC].partition_broadcast(128))
            t_r2 = dma("gpsimd", rstd2rep[:, :],
                       st2_dram[1:2, tc * TC:(tc + 1) * TC].partition_broadcast(128))
            wait("vector", "s_gp", t_r2)
            wait("vector", "s_sc", st2_done[tc])
            ts = None
            for i in range(8):
                if ts is not None:
                    wait("vector", "s_s", ts)
                tv0 = vop(lambda g, i=i: g.tensor_tensor(
                    out=scr[:, 0:TC], in0=xn_fm_tc[:, i * TC:(i + 1) * TC],
                    in1=mu2rep[:, :], op=ALU.subtract), True)
                wait("vector", "s_v", tv0)
                tv = vop(lambda g: g.tensor_tensor(
                    out=scr[:, 0:TC], in0=scr[:, 0:TC], in1=rstd2rep[:, :],
                    op=ALU.mult), True)
                wait("scalar", "s_v", tv)
                ts = sop(lambda g, i=i: g.activation(
                    out=h_fm_tc[:, i * TC:(i + 1) * TC], in_=scr[:, 0:TC],
                    func=AF.Identity, scale=lng_sb[:, i:i + 1],
                    bias=lnb_sb[:, i:i + 1]), True)
            return ts

        uflag_t = [0] * NTC
        p1_state = {}
        xp_mm_t = {}
        ud_t = {}
        ex_t = {}
        winu_grp_t = {}

        for tc in range(NTC):
            wait("gpsimd", "s_v", P.get("s_v"))
            wait("gpsimd", "s_pe", P.get("s_pe"))
            wait("gpsimd", "s_s", P.get("s_s"))
            wait("gpsimd", "s_sc", xnd_done[tc])
            t_xr = dma("gpsimd",
                       xn_fm_tc[:, :].rearrange("p (a n) -> p a n", a=8),
                       xn_dram[:, tc * TC:(tc + 1) * TC]
                       .rearrange("(a p) n -> p a n", p=128))
            wait("vector", "s_gp", t_xr)
            t_h = em_h_fm(tc)
            if DEBUG_TAPS and tc == 0:
                wait("gpsimd", "s_s", t_h)
                dma("gpsimd", taps["tap_h"].rearrange("(a p) n -> p a n", p=128),
                    h_fm_tc[:, :].rearrange("p (a n) -> p a n", a=8))
            wait("tensor", "s_s", t_h)
            phase_guard("tensor")
            conv_tv = {}

            def ld_win(c, tc=tc, extra_s=None):
                buf = wcA_b if c % 2 == 0 else wcB_b
                if extra_s is not None:
                    wait("gpsimd", "s_s", extra_s)
                if c >= 2:
                    wait("gpsimd", "s_pe", winu_grp_t[(tc, c * 4 - 5)])
                return dma("gpsimd",
                           buf.rearrange("p (a n) -> p a n", a=8),
                           w_in[:, c * 512:(c + 1) * 512]
                           .rearrange("(a p) n -> p a n", p=128))

            t_wc = {0: ld_win(0), 1: ld_win(1, extra_s=t_h)}
            for kt in range(16):
                slot_u = kt % 3
                if kt >= 3:
                    wait("scalar", "s_pe", xp_mm_t[(tc, kt - 3)])
                elif tc > 0:
                    wait("scalar", "s_pe", xp_mm_t[(tc - 1, kt + 13)])
                pswar = conv_tv.get(kt - 1 if tc == 0 else kt - 2)
                if pswar is None and kt == 0 and tc > 0:
                    pswar = p1_state.get("conv_last")
                if pswar is not None:
                    wait("tensor", "s_v", pswar)
                cw_ = kt // 4
                if kt == 4:
                    t_wc[2] = ld_win(2)
                if kt == 8:
                    t_wc[3] = ld_win(3)
                wbuf_p1 = wcA_b if cw_ % 2 == 0 else wcB_b
                if kt % 4 == 0:
                    wait("tensor", "s_gp", t_wc[cw_])

                def lw(k, kt=kt, wbuf_p1=wbuf_p1):
                    o = k * 512 + (kt % 4) * 128
                    return wbuf_p1[:, o:o + 128]
                for k in range(8):
                    t_pe = top(lambda g, k=k, kt=kt, lw=lw: g.matmul(
                        out=ps_mm[kt % 2][:, :],
                        lhsT=lw(k),
                        rhs=h_fm_tc[:, k * TC:(k + 1) * TC],
                        start=(k == 0), stop=(k == 7)), k == 7)
                if tc == 0:
                    for k in range(8):
                        t_pe = top(lambda g, k=k, lw=lw: g.matmul(
                            out=ps_bnd[:, 0:3],
                            lhsT=lw(k),
                            rhs=hb_fm[:, k * 4:k * 4 + 3],
                            start=(k == 0), stop=(k == 7)), k == 7)
                winu_grp_t[(tc, kt)] = t_pe
                wait("vector", "s_pe", t_pe)
                ur = uraw[:, (kt % 2) * 515:(kt % 2) * 515 + 515]
                if tc == 0:
                    t_bm = vop(lambda g, kt=kt: g.tensor_scalar(
                        out=uprev3[:, kt * 3:(kt + 1) * 3], in0=ps_bnd[:, 0:3],
                        scalar1=mask_sb[:, :], scalar2=None, op0=ALU.mult), True)
                    wait("vector", "s_v", t_bm)
                vop(lambda g, ur=ur, kt=kt: g.tensor_copy(
                    ur[:, 0:3], uprev3[:, kt * 3:(kt + 1) * 3]))
                t_ur = vop(lambda g, ur=ur, kt=kt: g.tensor_copy(
                    ur[:, 3:515], ps_mm[kt % 2][:, :]), True)
                wait("vector", "s_v", t_ur)
                t_up = vop(lambda g, ur=ur, kt=kt: g.tensor_copy(
                    uprev3[:, kt * 3:(kt + 1) * 3], ur[:, 512:515]), True)
                wait("vector", "s_v", t_up)
                ucwar = ud_t.get((tc, kt - 2)) if kt >= 2 else \
                    (ud_t.get((tc - 1, kt + 14)) if tc > 0 else None)
                if ucwar is not None:
                    wait("vector", "s_s", ucwar)
                uc = uconv[:, (kt % 2) * TC:(kt % 2 + 1) * TC]
                tv = vop(lambda g, ur=ur, uc=uc, kt=kt: g.tensor_scalar(
                    out=uc, in0=ur[:, 0:512], scalar1=cw_sb[:, kt * 4:kt * 4 + 1],
                    scalar2=None, op0=ALU.mult), True)
                for tap in range(1, 4):
                    wait("vector", "s_v", tv)
                    tv = vop(lambda g, ur=ur, uc=uc, kt=kt, tap=tap:
                             g.scalar_tensor_tensor(
                                 out=uc, in0=ur[:, tap:tap + 512],
                                 scalar=cw_sb[:, kt * 4 + tap:kt * 4 + tap + 1],
                                 in1=uc, op0=ALU.mult, op1=ALU.add),
                             True)
                conv_tv[kt] = tv
                p1_state["conv_last"] = tv
                wait("scalar", "s_v", tv)
                exwar = ex_t.get((tc, kt - 3)) if kt >= 3 else \
                    (ex_t.get((tc - 1, kt + 13)) if tc > 0 else None)
                if exwar is not None:
                    wait("scalar", "s_sy", exwar)
                ud = u_dt[:, slot_u * TC:(slot_u + 1) * TC]
                t_ud = sop(lambda g, uc=uc, ud=ud, kt=kt: g.activation(
                    out=ud, in_=uc, func=AF.Silu,
                    bias=cb_sb[:, kt:kt + 1]), True)
                ud_t[(tc, kt)] = t_ud
                if kt >= 2 or tc > 0:
                    pg = xp_mm_t[(tc, kt - 2)] if kt >= 2 else \
                        xp_mm_t[(tc - 1, kt + 14)]
                    wait("gpsimd", "s_pe", pg)
                t_wx = dma("gpsimd", wxp_sb[:, (kt % 2) * 96:(kt % 2 + 1) * 96],
                           w_xproj[kt * 128:(kt + 1) * 128, :])
                wait("tensor", "s_s", t_ud)
                wait("tensor", "s_gp", t_wx)
                xp_mm_t[(tc, kt)] = top(lambda g, kt=kt, ud=ud: g.matmul(
                    out=ps_x[:, :], lhsT=wxp_sb[:, (kt % 2) * 96:(kt % 2 + 1) * 96],
                    rhs=ud, start=(kt == 0), stop=(kt == 15)), True)
                base_sy = P.bump("s_sy", 16)
                ex_t[(tc, kt)] = base_sy

                def f_uexch(g, tc=tc, kt=kt, ud=ud, t_ud=t_ud):
                    g.wait_ge(sems["s_s"], t_ud)
                    em_arms(g, lambda g2, p, tc=tc, kt=kt, ud=ud: g2.dma_start(
                        out=sh_u[p, kt * 128:(kt + 1) * 128,
                                 tc * TC:(tc + 1) * TC],
                        in_=ud).then_inc(sems["s_sy"], 16))
                P.em("sync", f_uexch)
                if DEBUG_TAPS and tc == 0:
                    wait("scalar", "s_s", t_ud)
                    tdma = dma("scalar", taps["tap_u"][kt * 128:(kt + 1) * 128, :], ud)
                    wait("scalar", "s_sc", tdma)
            wait("vector", "s_pe", xp_mm_t[(tc, 15)])
            t_xd = vop(lambda g: g.tensor_copy(xdbl_tc[:, :], ps_x[:, :]), True)
            if DEBUG_TAPS and tc == 0:
                wait("scalar", "s_v", t_xd)
                tdma = dma("scalar", taps["tap_xdbl"], xdbl_tc[:, :])
                wait("vector", "s_sc", tdma)
            t_f1 = P.bump("s_sy", 16)
            t_f2 = P.bump("s_sy", 16)

            def f_xexch(g, tc=tc, t_xd=t_xd, t_f1=t_f1):
                g.wait_ge(sems["s_v"], t_xd)
                em_arms(g, lambda g2, p, tc=tc: g2.dma_start(
                    out=sh_xdbl[p, :, tc * TC:(tc + 1) * TC],
                    in_=xdbl_tc[:, :]).then_inc(sems["s_sy"], 16))
                g.wait_ge(sems["s_sy"], t_f1)
                em_arms(g, lambda g2, p, tc=tc: g2.dma_start(
                    out=sh_fu[p:p + 1, tc:tc + 1],
                    in_=non_sb[0:1, tc:tc + 1]).then_inc(sems["s_sy"], 16))
            P.em("sync", f_xexch)
            uflag_t[tc] = t_f2

        gacc_ready = [0, 0]
        xnh_loaded = [0, 0]
        moe_state = {"crep_guard": {}, "hmid_guard": None, "w2_last": None,
                     "w1_grp": {}, "w2_grp": {}}

        def em_xnh_load(half):
            wait("gpsimd", "s_pe", P.get("s_pe"))
            wait("gpsimd", "s_v", P.get("s_v"))
            wait("gpsimd", "s_s", P.get("s_s"))
            wait("gpsimd", "s_sy", P.get("s_sy"))
            t = dma("gpsimd",
                    xnh[:, :].rearrange("p (a n) -> p a n", a=8),
                    xn_dram[:, half * HALF:(half + 1) * HALF]
                    .rearrange("(a p) n -> p a n", p=128))
            xnh_loaded[half] = t

        def em_moe_ehalf(half, ee, first_in_half):
            phase_guard("tensor", "vector", "scalar")
            slot_c = ee % 2
            g_ = moe_state["crep_guard"].get(slot_c)
            if g_ is not None:
                wait("gpsimd", "s_v", g_)
            wait("gpsimd", "s_pe", P.get("s_pe") if first_in_half else 0)
            if first_in_half:
                wait("gpsimd", "s_v", P.get("s_v"))
                wait("gpsimd", "s_s", P.get("s_s"))
            wait("gpsimd", "s_sc", t_cd)
            t_c = dma("gpsimd", crep_e[:, slot_c * HALF:(slot_c + 1) * HALF],
                      c_dram[ee:ee + 1, half * HALF:(half + 1) * HALF]
                      .partition_broadcast(128))
            if first_in_half:
                wait("vector", "s_pe", P.get("s_pe"))
                vop(lambda g: g.memset(gacc[:, :], 0.0))
            if moe_state["w2_grp"].get(7) is not None:
                wait("scalar", "s_pe", moe_state["w2_grp"][7])

            def ld_w1(cc, ee=ee, gk=None):
                buf = pA_b if cc == 0 else pB_b
                if gk is not None:
                    wait("gpsimd", "s_pe", gk)
                return dma("gpsimd",
                           buf.rearrange("p (a n) -> p a n", a=8),
                           w1[ee, :, cc * 1024:(cc + 1) * 1024]
                           .rearrange("(a p) n -> p a n", p=128))

            def ld_w2(cc, ee=ee, gk=None):
                buf = pA_b if cc == 0 else pB_b
                if gk is not None:
                    wait("gpsimd", "s_pe", gk)
                t = 0
                for h2 in range(2):
                    t = dma("gpsimd",
                            buf[:, h2 * 4096:(h2 + 1) * 4096]
                            .rearrange("p (a n) -> p a n", a=8),
                            w2[ee, h2 * 1024:(h2 + 1) * 1024,
                               cc * 512:(cc + 1) * 512]
                            .rearrange("(a p) n -> p a n", p=128))
                return t

            t_w1a = ld_w1(0, gk=moe_state["w2_grp"].get(3))
            t_w1b = ld_w1(1, gk=moe_state["w2_grp"].get(7))
            t_w2a = None
            t_s = None
            t_sc = None
            for it in range(16):
                w1war = t_s if t_s is not None else moe_state.get("w1_s_last")
                if w1war is not None:
                    wait("tensor", "s_s", w1war)
                if it == 8:
                    t_w2a = ld_w2(0, gk=moe_state["w1_grp"][7])
                wbuf_1 = pA_b if it < 8 else pB_b
                if it == 0:
                    wait("tensor", "s_gp", max(t_w1a, xnh_loaded[half]))
                if it == 8:
                    wait("tensor", "s_gp", t_w1b)
                for nch in range(2):
                    for k in range(8):
                        t_pe = top(lambda g, k=k, nch=nch, it=it, wbuf_1=wbuf_1:
                                   g.matmul(
                                       out=ps_m1[nch][:, :],
                                       lhsT=wbuf_1[:, k * 1024 + (it % 8) * 128:
                                                   k * 1024 + (it % 8) * 128 + 128],
                                       rhs=xnh[:, k * HALF + nch * 512:
                                               k * HALF + (nch + 1) * 512],
                                       start=(k == 0), stop=(k == 7)),
                                   (k == 7 and nch == 1))
                moe_state["w1_grp"][it] = t_pe
                wait("scalar", "s_pe", t_pe)
                for nch in range(2):
                    t_s = sop(lambda g, it=it, nch=nch, ee=ee: g.activation(
                        out=hmid[:, it * HALF + nch * 512:
                                 it * HALF + (nch + 1) * 512],
                        in_=ps_m1[nch][:, :], func=AF.Gelu,
                        bias=b1_sb[:, ee * 16 + it: ee * 16 + it + 1]),
                        nch == 1)
                wait("vector", "s_s", t_s)
                wait("vector", "s_gp", t_c)
                t_sc = vop(lambda g, it=it, slot_c=slot_c: g.tensor_tensor(
                    out=hmid[:, it * HALF:(it + 1) * HALF],
                    in0=hmid[:, it * HALF:(it + 1) * HALF],
                    in1=crep_e[:, slot_c * HALF:(slot_c + 1) * HALF],
                    op=ALU.mult), True)
            moe_state["crep_guard"][slot_c] = t_sc
            moe_state["w1_s_last"] = t_s
            if DEBUG_TAPS and half == 0 and ee == 0:
                wait("gpsimd", "s_v", t_sc)
                dma("gpsimd",
                    taps["tap_hmid"].rearrange("(a p) n -> p a n", p=128),
                    hmid[:, :].rearrange("p (a n) -> p a n", a=16))
                tdma = dma("gpsimd", taps["tap_crep"],
                           crep_e[:, slot_c * HALF:(slot_c + 1) * HALF])
                wait("scalar", "s_gp", tdma)
                wait("tensor", "s_gp", tdma)
            wait("tensor", "s_v", t_sc)
            t_w2b = ld_w2(1, gk=moe_state["w1_grp"][15])
            t_g = None
            for f in range(8):
                w2war = t_g if t_g is not None else moe_state.get("w2_v_last")
                if w2war is not None:
                    wait("tensor", "s_v", w2war)
                wbuf_2 = pA_b if f < 4 else pB_b
                if f == 0:
                    wait("tensor", "s_gp", t_w2a)
                if f == 4:
                    wait("tensor", "s_gp", t_w2b)
                for nch in range(2):
                    for k in range(16):
                        t_pe = top(lambda g, k=k, nch=nch, f=f, wbuf_2=wbuf_2:
                                   g.matmul(
                                       out=ps_m2[nch][:, :],
                                       lhsT=wbuf_2[:, k * 512 + (f % 4) * 128:
                                                   k * 512 + (f % 4) * 128 + 128],
                                       rhs=hmid[:, k * HALF + nch * 512:
                                                k * HALF + (nch + 1) * 512],
                                       start=(k == 0), stop=(k == 15)),
                                   (k == 15 and nch == 1))
                moe_state["w2_grp"][f] = t_pe
                wait("vector", "s_pe", t_pe)
                for nch in range(2):
                    t_g = vop(lambda g, f=f, nch=nch: g.tensor_tensor(
                        out=gacc[:, f * HALF + nch * 512:
                                 f * HALF + (nch + 1) * 512],
                        in0=gacc[:, f * HALF + nch * 512:
                                 f * HALF + (nch + 1) * 512],
                        in1=ps_m2[nch][:, :], op=ALU.add), nch == 1)
            moe_state["w2_last"] = t_g
            moe_state["w2_v_last"] = t_g
            return t_g

        def em_moe_b2(half):
            wait("gpsimd", "s_v", P.get("s_v"))
            t_cm = dma("gpsimd", cmoe[:, :],
                       c_dram[:, half * HALF:(half + 1) * HALF])
            wait("tensor", "s_gp", t_cm)
            wait("tensor", "s_v", moe_state["w2_last"])
            t_g = None
            for f in range(8):
                if t_g is not None:
                    wait("tensor", "s_v", t_g)
                t_pe = top(lambda g, f=f: g.matmul(
                    out=ps_m2[0][:, :],
                    lhsT=b2_sb[:, f * 128:(f + 1) * 128],
                    rhs=cmoe[:, 0:512],
                    start=True, stop=True))
                t_pe = top(lambda g, f=f: g.matmul(
                    out=ps_m2[1][:, :],
                    lhsT=b2_sb[:, f * 128:(f + 1) * 128],
                    rhs=cmoe[:, 512:1024],
                    start=True, stop=True), True)
                wait("vector", "s_pe", t_pe)
                for nch in range(2):
                    t_g = vop(lambda g, f=f, nch=nch: g.tensor_tensor(
                        out=gacc[:, f * HALF + nch * 512:
                                 f * HALF + (nch + 1) * 512],
                        in0=gacc[:, f * HALF + nch * 512:
                                 f * HALF + (nch + 1) * 512],
                        in1=ps_m2[nch][:, :], op=ALU.add), nch == 1)
            if DEBUG_TAPS and half == 0:
                wait("scalar", "s_v", t_g)
                dma("scalar",
                    taps["tap_g"].rearrange("(a p) n -> p a n", p=128),
                    gacc[:, :].rearrange("p (a n) -> p a n", a=8))
            moe_state["w2_last"] = t_g
            return t_g

        yfin_last = [0] * NSTC

        dtmm_t = {}
        uslot_guard = [None, None]

        def em_scan(stc):
            phase_guard("tensor", "vector", "scalar")
            slot = stc // 4
            tcc = stc % 4
            em_poll_simple(sh_fu[slot:slot + 1, tcc:tcc + 1], tcc)
            t_ok = P.bump("s_sy", 16)
            P.em("sync", lambda g, slot=slot, tcc=tcc: g.dma_start(
                out=flagb[0:1, 1:2], in_=sh_fu[slot:slot + 1, tcc:tcc + 1]
            ).then_inc(sems["s_sy"], 16))
            pg_x = dtmm_t.get(7)
            if pg_x is not None:
                wait("sync", "s_pe", pg_x)
            t_xs = dma("sync", xdbl_sc[:, :],
                       sh_xdbl[slot, :, tcc * TC:(tcc + 1) * TC])
            u_read_t = {}

            def em_uread(d, guard):
                t = P.bump("s_sy", 16)

                def f_uread(g, d=d, slot=slot, tcc=tcc, guard=guard):
                    if guard is not None:
                        g.wait_ge(sems["s_v"], guard)
                    em_arms(g, lambda g2, p, d=d, slot=slot, tcc=tcc:
                            g2.dma_start(
                                out=u_scan[:, (d % 2) * TC:(d % 2 + 1) * TC],
                                in_=sh_u[slot, p * 1024 + d * 128:
                                         p * 1024 + (d + 1) * 128,
                                         tcc * TC:(tcc + 1) * TC]
                            ).then_inc(sems["s_sy"], 16))
                P.em("sync", f_uread)
                u_read_t[d] = t

            em_uread(0, uslot_guard[0])
            em_uread(1, uslot_guard[1])
            t_dtu_prev = None
            y_ms = {}
            t_yf = None
            for d in range(8):
                pg = dtmm_t.get(d - 2 if d >= 2 else d + 6)
                if pg is not None:
                    wait("gpsimd", "s_pe", pg)
                t_wd = dma("gpsimd", wdt_sb[:, (d % 2) * 128:(d % 2 + 1) * 128],
                           w_dt_my[:, d * 128:(d + 1) * 128])
                wait("tensor", "s_sy", max(t_xs, u_read_t[d]))
                wait("tensor", "s_gp", t_wd)
                if d >= 2:
                    wait("tensor", "s_s", spx_t[d - 2])
                t_pe = top(lambda g, d=d: g.matmul(
                    out=ps_mm[d % 2][:, :],
                    lhsT=wdt_sb[:, (d % 2) * 128:(d % 2 + 1) * 128],
                    rhs=xdbl_sc[0:64, :], start=True, stop=True), True)
                dtmm_t[d] = t_pe
                wait("scalar", "s_pe", t_pe)
                if t_dtu_prev is not None:
                    wait("scalar", "s_v", t_dtu_prev)
                if d == 0:
                    spx_t = {}
                spx_t[d] = sop(lambda g, d=d: g.activation(
                    out=sp_sb[:, :], in_=ps_mm[d % 2][:, :], func=AF.Exp,
                    bias=bdt_sb[:, d:d + 1]), True)
                wait("scalar", "s_s", spx_t[d])
                t_dt = sop(lambda g: g.activation(
                    out=dt_sb[:, :], in_=sp_sb[:, :], func=AF.Ln, bias=fc_sb[:, 1:2]), True)
                if DEBUG_TAPS and stc == 0 and d == 0:
                    wait("scalar", "s_s", t_dt)
                    tdma = dma("scalar", taps["tap_dt"], dt_sb[:, :])
                    wait("scalar", "s_sc", tdma)
                    wait("vector", "s_sc", tdma)
                wait("vector", "s_s", t_dt)
                t_dtu_prev = vop(lambda g, d=d: g.tensor_tensor(
                    out=dtu_sb[:, :], in0=dt_sb[:, :],
                    in1=u_scan[:, (d % 2) * TC:(d % 2 + 1) * TC],
                    op=ALU.mult), True)
                wait("vector", "s_v", t_dtu_prev)
                for n in range(DS):
                    rs = n % 2
                    ra = n % 4
                    rb = n % 3
                    ci = d * 16 + n
                    g_war4 = None
                    if n >= 4:
                        g_war4 = y_ms[(d, n - 4)]
                    elif d > 0:
                        g_war4 = y_ms[(d - 1, DS - 4 + n)]
                    if g_war4 is not None:
                        wait("scalar", "s_v", g_war4)
                    g_war3 = None
                    if n >= 3:
                        g_war3 = y_ms[(d, n - 3)]
                    elif d > 0:
                        g_war3 = y_ms[(d - 1, DS - 3 + n)]
                    if g_war3 is not None:
                        wait("gpsimd", "s_v", g_war3)
                    t_dA = sop(lambda g, ra=ra, ci=ci: g.activation(
                        out=dA_sb[:, ra * TC:(ra + 1) * TC], in_=dt_sb[:, :],
                        func=AF.Exp, scale=asb[:, ci:ci + 1]), True)
                    wait("gpsimd", "s_sy", t_ok)
                    tb = dma("gpsimd", brep[:, rb * TC:(rb + 1) * TC],
                             sh_xdbl[slot, 64 + n:65 + n, tcc * TC:(tcc + 1) * TC]
                             .partition_broadcast(128))
                    tcr = dma("gpsimd", crep[:, rb * TC:(rb + 1) * TC],
                              sh_xdbl[slot, 80 + n:81 + n, tcc * TC:(tcc + 1) * TC]
                              .partition_broadcast(128))
                    wait("vector", "s_gp", tcr)
                    wait("vector", "s_s", t_dA)
                    t_db = vop(lambda g, rs=rs, rb=rb: g.tensor_tensor(
                        out=dBu_sb[:, rs * TC:(rs + 1) * TC], in0=dtu_sb[:, :],
                        in1=brep[:, rb * TC:(rb + 1) * TC], op=ALU.mult), True)
                    wait("vector", "s_v", t_db)
                    t_sc2 = vop(lambda g, rs=rs, ra=ra, ci=ci: g.tensor_tensor_scan(
                        out=H_sb[:, rs * TC:(rs + 1) * TC],
                        data0=dA_sb[:, ra * TC:(ra + 1) * TC],
                        data1=dBu_sb[:, rs * TC:(rs + 1) * TC],
                        initial=carry[:, ci:ci + 1],
                        op0=ALU.mult, op1=ALU.add), True)
                    wait("vector", "s_v", t_sc2)
                    if n == 0:
                        t_y0 = vop(lambda g, rs=rs, rb=rb: g.tensor_tensor(
                            out=yacc_sb[:, :],
                            in0=H_sb[:, rs * TC:(rs + 1) * TC],
                            in1=crep[:, rb * TC:(rb + 1) * TC],
                            op=ALU.mult), True)
                    else:
                        t_y0 = vop(lambda g, rs=rs, rb=rb: g.tensor_tensor(
                            out=dBu_sb[:, rs * TC:(rs + 1) * TC],
                            in0=H_sb[:, rs * TC:(rs + 1) * TC],
                            in1=crep[:, rb * TC:(rb + 1) * TC], op=ALU.mult),
                            True)
                    wait("vector", "s_v", t_y0)
                    vop(lambda g, rs=rs, ci=ci: g.tensor_copy(
                        carry[:, ci:ci + 1],
                        H_sb[:, rs * TC + TC - 1: rs * TC + TC]))
                    if n == 0:
                        t_y = t_y0
                    else:
                        t_y = vop(lambda g, rs=rs: g.tensor_tensor(
                            out=yacc_sb[:, :], in0=yacc_sb[:, :],
                            in1=dBu_sb[:, rs * TC:(rs + 1) * TC],
                            op=ALU.add), True)
                        wait("vector", "s_v", t_y)
                    y_ms[(d, n)] = t_y
                if d >= 2:
                    wait("vector", "s_sy", ywr_t[d - 2])
                if d == 0:
                    ywr_t = {}
                wait("vector", "s_v", y_ms[(d, DS - 1)])
                t_yf = vop(lambda g, d=d: g.scalar_tensor_tensor(
                    out=yfin_sb[:, (d % 2) * TC:(d % 2 + 1) * TC],
                    in0=u_scan[:, (d % 2) * TC:(d % 2 + 1) * TC],
                    scalar=dco_sb[:, d:d + 1], in1=yacc_sb[:, :],
                    op0=ALU.mult, op1=ALU.add), True)
                if DEBUG_TAPS and stc == 0 and d == 0:
                    wait("scalar", "s_v", t_yf)
                    tdma = dma("scalar", taps["tap_y"], yfin_sb[:, 0:TC])
                    wait("vector", "s_sc", tdma)
                ywr_t[d] = P.bump("s_sy", 16)

                def f_ywrite(g, d=d, stc=stc, t_yf=t_yf):
                    g.wait_ge(sems["s_v"], t_yf)
                    em_arms(g, lambda g2, p, d=d, stc=stc: g2.dma_start(
                        out=sh_y[p, d * 128:(d + 1) * 128,
                                 stc * TC:(stc + 1) * TC],
                        in_=yfin_sb[:, (d % 2) * TC:(d % 2 + 1) * TC]
                    ).then_inc(sems["s_sy"], 16))
                P.em("sync", f_ywrite)
                if d + 2 < 8:
                    em_uread(d + 2, t_yf)
                else:
                    uslot_guard[d % 2] = t_yf
            yfin_last[stc] = t_yf
            t_ally = P.get("s_sy")
            P.bump("s_sy", 16)

            def f_yflag(g, stc=stc, t_ally=t_ally):
                g.wait_ge(sems["s_sy"], t_ally)
                em_arms(g, lambda g2, p, stc=stc: g2.dma_start(
                    out=sh_fy[p:p + 1, stc:stc + 1],
                    in_=non_sb[0:1, stc + 4:stc + 5]).then_inc(sems["s_sy"], 16))
            P.em("sync", f_yflag)

        p3_state = {}

        def em_p3(tc):
            phase_guard("tensor", "vector", "scalar")
            wait("sync", "s_pe", P.get("s_pe"))
            wait("sync", "s_s", P.get("s_s"))
            wait("sync", "s_v", P.get("s_v"))

            def f_poll(g, tc=tc):
                def body(g2, p, tc=tc):
                    stc_own = 4 * p + tc
                    for cs in range(2):
                        _spin(g2, sh_fy[cs:cs + 1, stc_own:stc_own + 1],
                              stc_own + 4)
                em_arms(g, body)
            P.em("sync", f_poll)
            t_okp = P.bump("s_sy", 16)
            P.em("sync", lambda g: g.dma_start(
                out=flagb[0:1, 1:2], in_=sh_fy[0:1, 0:1]
            ).then_inc(sems["s_sy"], 16))

            wait("gpsimd", "s_v", P.get("s_v"))
            wait("gpsimd", "s_pe", P.get("s_pe"))
            wait("gpsimd", "s_s", P.get("s_s"))
            t_xn = dma("gpsimd",
                       xn_fm_tc[:, :].rearrange("p (a n) -> p a n", a=8),
                       xn_dram[:, tc * TC:(tc + 1) * TC]
                       .rearrange("(a p) n -> p a n", p=128))
            wait("vector", "s_gp", t_xn)
            t_h = em_h_fm(tc)
            wait("tensor", "s_s", t_h)

            t_gt = {}
            t_z = None
            zkt_grp = {}

            def ld_z(c, extra_s=None, gk=None):
                buf = wcA_b if c % 2 == 0 else wcB_b
                if extra_s is not None:
                    wait("gpsimd", "s_s", extra_s)
                if gk is not None:
                    wait("gpsimd", "s_pe", gk)
                return dma("gpsimd",
                           buf.rearrange("p (a n) -> p a n", a=8),
                           w_in[:, DI + c * 512:DI + (c + 1) * 512]
                           .rearrange("(a p) n -> p a n", p=128))

            t_zc = {0: ld_z(0), 1: ld_z(1, extra_s=t_h)}
            for kt in range(16):
                base_y = P.bump("s_sy", 16)

                def f_yread(g, tc=tc, kt=kt,
                            guard=(t_gt.get(kt - 2) if kt >= 2 else None)):
                    if guard is not None:
                        g.wait_ge(sems["s_v"], guard)
                    em_arms(g, lambda g2, p, tc=tc, kt=kt: g2.dma_start(
                        out=yfull[:, (kt % 2) * TC:(kt % 2 + 1) * TC],
                        in_=sh_y[kt // 8, (kt % 8) * 128:(kt % 8 + 1) * 128,
                                 p * TOK + tc * TC: p * TOK + (tc + 1) * TC]
                    ).then_inc(sems["s_sy"], 16))
                P.em("sync", f_yread)
                cz = kt // 4
                if kt == 4:
                    t_zc[2] = ld_z(2, gk=zkt_grp[3])
                if kt == 8:
                    t_zc[3] = ld_z(3, gk=zkt_grp[7])
                zbuf = wcA_b if cz % 2 == 0 else wcB_b
                if kt % 4 == 0:
                    wait("tensor", "s_gp", t_zc[cz])
                if t_z is not None:
                    wait("tensor", "s_s", t_z)
                t_pe = None
                for k in range(8):
                    t_pe = top(lambda g, k=k, kt=kt, zbuf=zbuf: g.matmul(
                        out=ps_mm[kt % 2][:, :],
                        lhsT=zbuf[:, k * 512 + (kt % 4) * 128:
                                  k * 512 + (kt % 4) * 128 + 128],
                        rhs=h_fm_tc[:, k * TC:(k + 1) * TC],
                        start=(k == 0), stop=(k == 7)), k == 7)
                zkt_grp[kt] = t_pe
                if kt % 2 == 0:
                    p3_state["z_last"] = t_pe
                else:
                    p3_state["z_last2"] = t_pe
                wait("scalar", "s_pe", t_pe)
                if kt >= 2:
                    wait("scalar", "s_v", t_gt[kt - 2])
                t_z = sop(lambda g, kt=kt: g.activation(
                    out=zsil[:, (kt % 2) * TC:(kt % 2 + 1) * TC],
                    in_=ps_mm[kt % 2][:, :], func=AF.Silu), True)
                if DEBUG_TAPS and tc == 0 and kt == 0:
                    wait("scalar", "s_s", t_z)
                    dma("scalar", taps["tap_z"], zsil[:, 0:TC])
                    wait("scalar", "s_sy", base_y)
                    tdma = dma("scalar", taps["tap_yfull"], yfull[:, 0:TC])
                    wait("scalar", "s_sc", tdma)
                    wait("sync", "s_sc", tdma)
                wait("vector", "s_s", t_z)
                wait("vector", "s_sy", base_y)
                t_gt[kt] = vop(lambda g, kt=kt: g.tensor_tensor(
                    out=ygate[:, kt * TC:(kt + 1) * TC],
                    in0=yfull[:, (kt % 2) * TC:(kt % 2 + 1) * TC],
                    in1=zsil[:, (kt % 2) * TC:(kt % 2 + 1) * TC],
                    op=ALU.mult), True)
                if DEBUG_TAPS and tc == 0 and kt == 0:
                    wait("gpsimd", "s_v", t_gt[0])
                    dma("gpsimd", taps["tap_yg"], ygate[:, 0:TC])
            wait("tensor", "s_v", t_gt[15])
            mo_ev = {}
            t_mo = None
            wo_grp = {}

            def ld_wo(c, gk=None):
                buf = wcA_b if c % 2 == 0 else wcB_b
                if gk is not None:
                    wait("gpsimd", "s_pe", gk)
                t = 0
                for h2 in range(2):
                    t = dma("gpsimd",
                            buf[:, h2 * 2048:(h2 + 1) * 2048]
                            .rearrange("p (a n) -> p a n", a=8),
                            w_out[h2 * 1024:(h2 + 1) * 1024,
                                  c * 256:(c + 1) * 256]
                            .rearrange("(a p) n -> p a n", p=128))
                return t

            t_wo = {0: ld_wo(0, gk=zkt_grp[11]), 1: ld_wo(1, gk=zkt_grp[15])}
            for f in range(8):
                if f >= 2:
                    wait("tensor", "s_v", mo_ev[f - 2])
                cwo = f // 2
                if f == 2:
                    t_wo[2] = ld_wo(2, gk=wo_grp[1])
                if f == 4:
                    t_wo[3] = ld_wo(3, gk=wo_grp[3])
                wobuf = wcA_b if cwo % 2 == 0 else wcB_b
                if f % 2 == 0:
                    wait("tensor", "s_gp", t_wo[cwo])
                for k in range(16):
                    t_pe = top(lambda g, f=f, k=k, wobuf=wobuf: g.matmul(
                        out=ps_mm[f % 2][:, :],
                        lhsT=wobuf[:, k * 256 + (f % 2) * 128:
                                   k * 256 + (f % 2) * 128 + 128],
                        rhs=ygate[:, k * TC:(k + 1) * TC],
                        start=(k == 0), stop=(k == 15)), k == 15)
                wo_grp[f] = t_pe
                if f % 2 == 0:
                    p3_state["wo_last"] = t_pe
                else:
                    p3_state["wo_last2"] = t_pe
                wait("vector", "s_pe", t_pe)
                t_mo = vop(lambda g, f=f: g.tensor_copy(
                    mo_fm[:, f * TC:(f + 1) * TC], ps_mm[f % 2][:, :]), True)
                mo_ev[f] = t_mo
            if DEBUG_TAPS and tc == 0:
                wait("scalar", "s_v", t_mo)
                dma("scalar",
                    taps["tap_mo"].rearrange("(a p) n -> p a n", p=128),
                    mo_fm[:, :].rearrange("p (a n) -> p a n", a=8))
            return t_mo

        out_last = [0]
        fin_state = {}

        def em_final(tc, t_mo, half):
            phase_guard("tensor", "vector", "scalar")
            for jj in range(4):
                j = tc * 4 + jj
                if fin_state.get("t_o") is not None:
                    wait("sync", "s_v", fin_state["t_o"])
                t_x2 = dma("sync", x2_tm[:, :], xrows[j * 128:(j + 1) * 128, :])
                wait("tensor", "s_v", max(t_mo, gacc_ready[half]))
                t_ev = None
                for i in range(8):
                    t_pe = top(lambda g, i=i, jj=jj: g.transpose(
                        out=ps_t[:, 0:128],
                        in_=mo_fm[:, i * TC + jj * 128: i * TC + (jj + 1) * 128],
                        identity=idf_sb[:, :]), True)
                    wait("vector", "s_pe", t_pe)
                    t_ev = vop(lambda g, i=i: g.tensor_copy(
                        mo_tm[:, i * 128:(i + 1) * 128], ps_t[:, 0:128]), True)
                    wait("tensor", "s_v", t_ev)
                    gcol = (tc % 2) * TC + jj * 128
                    t_pe = top(lambda g, i=i, gcol=gcol: g.transpose(
                        out=ps_t[:, 0:128],
                        in_=gacc[:, i * HALF + gcol: i * HALF + gcol + 128],
                        identity=idf_sb[:, :]), True)
                    wait("vector", "s_pe", t_pe)
                    t_ev = vop(lambda g, i=i: g.tensor_copy(
                        g_tm[:, i * 128:(i + 1) * 128], ps_t[:, 0:128]), True)
                    wait("tensor", "s_v", t_ev)
                wait("scalar", "s_v", t_ev)
                t_gs = sop(lambda g: g.activation(out=g_tm, in_=g_tm,
                                                  func=AF.Silu), True)
                wait("vector", "s_s", t_gs)
                wait("vector", "s_sy", t_x2)
                if out_last[0]:
                    wait("vector", "s_sy", out_last[0])
                tf0 = vop(lambda g, j=j: g.tensor_scalar(
                    out=s_tm[:, :], in0=x2_tm[:, :],
                    scalar1=col(C_MU1, j), scalar2=col(C_RS1, j),
                    op0=ALU.subtract, op1=ALU.mult), True)
                wait("vector", "s_v", tf0)
                tf1 = vop(lambda g: g.tensor_tensor(
                    out=s_tm[:, :], in0=s_tm[:, :], in1=mo_tm, op=ALU.add), True)
                wait("vector", "s_v", tf1)
                tf2 = vop(lambda g: g.tensor_tensor(
                    out=s_tm[:, :], in0=s_tm[:, :], in1=g_tm, op=ALU.mult), True)
                wait("vector", "s_v", tf2)
                t_o = vop(lambda g: g.tensor_tensor(
                    out=s_tm[:, :], in0=s_tm[:, :], in1=x2_tm[:, :],
                    op=ALU.add), True)
                fin_state["t_o"] = t_o
                wait("sync", "s_v", t_o)
                out_last[0] = dma("sync", yout[j * 128:(j + 1) * 128, :],
                                  s_tm[:, :])

        em_xnh_load(0)
        for k in range(4):
            em_scan(k)
            for ee in range(4 * k, 4 * k + 4):
                em_moe_ehalf(0, ee, first_in_half=(ee == 0))
            if k == 3:
                gacc_ready[0] = em_moe_b2(0)
        em_scan(4)
        t_mo0 = em_p3(0)
        em_final(0, t_mo0, 0)
        em_scan(5)
        t_mo1 = em_p3(1)
        em_final(1, t_mo1, 0)
        em_xnh_load(1)
        for ee in range(0, 5):
            em_moe_ehalf(1, ee, first_in_half=(ee == 0))
        em_scan(6)
        for ee in range(5, 11):
            em_moe_ehalf(1, ee, first_in_half=False)
        em_scan(7)
        for ee in range(11, 16):
            em_moe_ehalf(1, ee, first_in_half=False)
        gacc_ready[1] = em_moe_b2(1)
        t_mo2 = em_p3(2)
        em_final(2, t_mo2, 1)
        t_mo3 = em_p3(3)
        em_final(3, t_mo3, 1)

        fins = {s_: P.get(s_) for s_ in ("s_sy", "s_gp", "s_sc", "s_vq")}
        P.em("sync", lambda g: g.wait_ge(sems["s_sy"], fins["s_sy"]))
        P.em("gpsimd", lambda g: g.wait_ge(sems["s_gp"], fins["s_gp"]))
        P.em("scalar", lambda g: g.wait_ge(sems["s_sc"], fins["s_sc"]))
        P.em("vector", lambda g: g.wait_ge(sems["s_vq"], fins["s_vq"]))

        with nc.Block() as block:
            @block.sync
            def _(g):
                for fn in P.p["sync"]:
                    fn(g)

            @block.gpsimd
            def _(g):
                for fn in P.p["gpsimd"]:
                    fn(g)

            @block.scalar
            def _(g):
                for fn in P.p["scalar"]:
                    fn(g)

            @block.vector
            def _(g):
                for fn in P.p["vector"]:
                    fn(g)

            @block.tensor
            def _(g):
                for fn in P.p["tensor"]:
                    fn(g)

    return nc


def _cols(vec, ntiles):
    return np.ascontiguousarray(vec.reshape(ntiles, 128).T).astype(np.float32)


def kernel(**inputs):
    x = np.asarray(inputs["L0_attn_x"], np.float32)
    ln_g = np.asarray(inputs["L0_attn_ln_g"], np.float32)
    ln_b = np.asarray(inputs["L0_attn_ln_b"], np.float32)
    W_in = np.asarray(inputs["L0_attn_W_in"], np.float32)
    conv_w = np.asarray(inputs["L0_attn_conv_w"], np.float32)
    conv_b = np.asarray(inputs["L0_attn_conv_b"], np.float32)
    W_xproj = np.asarray(inputs["L0_attn_W_xproj"], np.float32)
    W_dt = np.asarray(inputs["L0_attn_W_dt"], np.float32)
    b_dt = np.asarray(inputs["L0_attn_b_dt"], np.float32)
    A_log = np.asarray(inputs["L0_attn_A_log"], np.float32)
    D = np.asarray(inputs["L0_attn_D"], np.float32)
    W_out = np.asarray(inputs["L0_attn_W_out"], np.float32)
    rw = np.asarray(inputs["L0_ffn_router_w"], np.float32)
    rb = np.asarray(inputs["L0_ffn_router_b"], np.float32)
    w1 = np.asarray(inputs["L0_ffn_w1"], np.float32)
    b1 = np.asarray(inputs["L0_ffn_b1"], np.float32)
    w2 = np.asarray(inputs["L0_ffn_w2"], np.float32)
    b2 = np.asarray(inputs["L0_ffn_b2"], np.float32)

    nonce = int(time.time_ns()) & 0x7FFF0000
    nonces = ((np.arange(1, 17, dtype=np.int64) + nonce) &
              0xFFFFFFFF).astype(np.uint32).reshape(1, 16)

    identb = np.eye(128, dtype=ml_dtypes.bfloat16)
    identf = np.eye(128, dtype=np.float32)
    cw = conv_w.reshape(DI, 4)
    convw_cols = np.zeros((128, 64), np.float32)
    for kt in range(16):
        convw_cols[:, kt * 4:(kt + 1) * 4] = cw[kt * 128:(kt + 1) * 128, :]
    convb_cols = _cols(conv_b, 16)
    lng_cols = _cols(ln_g, 8)
    lnb_cols = _cols(ln_b, 8)
    b1_cols = np.zeros((128, NE * 16), np.float32)
    for ee in range(NE):
        b1_cols[:, ee * 16:(ee + 1) * 16] = _cols(b1[ee], 16)

    in_maps = []
    for c in range(8):
        b, h = c >> 1, c & 1
        lo, hi = h * 1024, (h + 1) * 1024
        xr = np.ascontiguousarray(x[b, h * TOK:(h + 1) * TOK, :])
        xb = np.zeros((4, DM), np.float32)
        if h == 1:
            xb[0:3] = x[b, h * TOK - 3:h * TOK, :]
            m1 = np.ones((128, 1), np.float32)
        else:
            m1 = np.zeros((128, 1), np.float32)
        im = dict(
            xrows=xr, xbnd=xb, mask1=m1,
            parity=np.array([[h]], np.uint32), nonces=nonces,
            w_in=W_in, convw_cols=convw_cols, convb_cols=convb_cols,
            w_xproj=W_xproj,
            w_dt_my=np.ascontiguousarray(W_dt[:, lo:hi]),
            bdt_cols=_cols(b_dt[lo:hi], 8),
            alog_cols=np.ascontiguousarray(
                A_log[lo:hi].reshape(8, 128, 16).transpose(1, 0, 2)
                .reshape(128, 128)),
            d_cols=_cols(D[lo:hi], 8),
            w_out=W_out, lng_cols=lng_cols, lnb_cols=lnb_cols,
            router_w=rw, rb_col=rb.reshape(NE, 1),
            w1=w1, b1_cols=b1_cols, w2=w2, b2_st=b2,
            identb=identb, identf=identf,
            fconst=np.broadcast_to(
                np.array([EPS, 1.0, -0.5, 0.0], np.float32), (128, 4)).copy(),
        )
        in_maps.append(im)

    nc = build_program()
    res = run_bass_kernel_spmd(nc, in_maps, core_ids=list(range(8)))
    out = np.empty((B, L, DM), np.float32)
    for c in range(8):
        b, h = c >> 1, c & 1
        out[b, h * TOK:(h + 1) * TOK, :] = res.results[c]["yout"]
    return out
